# version 15
# baseline (speedup 1.0000x reference)
import numpy as np
import concourse.bass as bass
import concourse.mybir as mybir
from concourse.bass_utils import run_bass_kernel_spmd

F32 = mybir.dt.float32
BF16 = mybir.dt.bfloat16
AF = mybir.ActivationFunctionType
ALU = mybir.AluOpType
AX = mybir.AxisListType

NCORES = 8
T = 4096
D = 1024
NG = D // 128
MAGIC = 1.5 * 2.0 ** 23

ENGS = ("sync", "tensor", "vector", "scalar", "gpsimd")
A_TILES = 32
B_PAIRS = 8
B_NT1B = 16
DBG_PARKS = False


class Sem:
    __slots__ = ("h", "n", "name")

    def __init__(self, h, name):
        self.h = h
        self.n = 0
        self.name = name


class _Rec:

    def __init__(self, kb, eng):
        self._kb = kb
        self._eng = eng

    def wait(self, sem, val=None):
        v = sem.n if val is None else val
        self._kb.ops[self._eng].append(("__wait__", (sem.h, v), {}, ()))

    def __getattr__(self, meth):
        kb = self._kb
        eng = self._eng

        def rec(*a, inc=None, **kw):
            incs = []
            if inc is not None:
                if isinstance(inc, tuple) and len(inc) == 2 and isinstance(inc[0], Sem):
                    inc = [inc]
                for s, k in inc:
                    s.n += k
                    incs.append((s.h, k))
            kb.ops[eng].append((meth, a, kw, tuple(incs)))

        return rec


class KB:

    def __init__(self, nc):
        self.nc = nc
        self.ops = {e: [] for e in ENGS}
        self._sems = {}
        self._guards = {}
        self._ctx = []
        self.y = _Rec(self, "sync")
        self.t = _Rec(self, "tensor")
        self.v = _Rec(self, "vector")
        self.s = _Rec(self, "scalar")
        self.g = _Rec(self, "gpsimd")
        self.done = {e: self.sem("done_" + e) for e in ENGS}
        self.PW = self.sem("PW")
        self.CONST = self.sem("CONST")

    def constcol(self, name, value, dtype=F32):
        h = self.sbuf(name, [128, 1], dtype)
        self.g.memset(h[:, :], float(value), inc=(self.CONST, 1))
        return h

    def sem(self, name):
        if "_" in name:
            name = name.split("_", 1)[1]
        if name not in self._sems:
            cm = self.nc.semaphore(name)
            h = cm.__enter__()
            self._ctx.append(cm)
            self._sems[name] = Sem(h, name)
        return self._sems[name]

    def sbuf(self, name, shape, dtype):
        g = self.nc.sbuf_tensor(name, shape, dtype)
        h = g.__enter__()
        self._guards[name] = g
        return h

    def psum(self, name, shape, dtype=F32):
        g = self.nc.psum_tensor(name, shape, dtype)
        h = g.__enter__()
        self._guards[name] = g
        return h

    def free(self, *names):
        for nm in names:
            g = self._guards.pop(nm)
            g.__exit__(None, None, None)

    def free_all_bufs(self):
        for nm in reversed(list(self._guards)):
            self.free(nm)

    def stage_end(self, write_sems=()):
        for s in write_sems:
            self.y.wait(s)
        for e in ENGS:
            r = getattr(self, {"sync": "y", "tensor": "t", "vector": "v",
                               "scalar": "s", "gpsimd": "g"}[e])
            if e != "sync":
                r.drain()
            r.nop(inc=(self.done[e], 1))
        for e in ENGS:
            r = getattr(self, {"sync": "y", "tensor": "t", "vector": "v",
                               "scalar": "s", "gpsimd": "g"}[e])
            for e2 in ENGS:
                if e2 != e:
                    r.wait(self.done[e2])

    def emit(self):
        nc = self.nc
        with nc.Block() as block:
            for eng in ENGS:
                items = self.ops[eng]

                def run(engine, items=items):
                    for meth, a, kw, incs in items:
                        if meth == "__wait__":
                            engine.wait_ge(a[0], a[1])
                            continue
                        instr = getattr(engine, meth)(*a, **kw)
                        for h, k in incs:
                            instr = instr.then_inc(h, k)

                getattr(block, eng)(run)

    def close(self):
        self.free_all_bufs()
        for cm in reversed(self._ctx):
            cm.__exit__(None, None, None)



def build_identity(kb, name, dtype):
    nc = kb.nc
    colio = kb.sbuf(name + "_ci", [128, 1], F32)
    rowio = kb.sbuf(name + "_ri", [128, 128], F32)
    ident = kb.sbuf(name, [128, 128], dtype)
    s = kb.sem(name + "_s")
    kb.g.iota(colio[:, :], [[0, 1]], channel_multiplier=1, allow_small_or_imprecise_dtypes=True, inc=(s, 1))
    kb.g.iota(rowio[:, :], [[1, 128]], channel_multiplier=0, allow_small_or_imprecise_dtypes=True, inc=(s, 1))
    kb.v.wait(s)
    kb.v.tensor_scalar(ident[:, :], rowio[:, :], colio[:, :], None, ALU.is_equal,
                       inc=(s, 1))
    return ident, s


def stage_A(kb, Tn):
    x = Tn["x"]
    x_fm = Tn["x_fm"]
    xn0 = Tn["xn0"]

    epscol = kb.constcol("A_eps", 1e-5)
    identb, sid = build_identity(kb, "A_idb", BF16)
    identf, sidf = build_identity(kb, "A_idf", F32)
    kb.t.wait(sid)
    kb.t.wait(sidf)

    xt = [kb.sbuf(f"A_xt{i}", [128, 1024], F32) for i in range(2)]
    xnt = [kb.sbuf(f"A_xnt{i}", [128, 1024], BF16) for i in range(2)]
    stf = [kb.sbuf(f"A_stf{i}", [128, 1024], F32) for i in range(2)]
    stb = [kb.sbuf(f"A_stb{i}", [128, 1024], BF16) for i in range(2)]
    st = kb.sbuf("A_stat", [128, 8], F32)
    sq = [kb.sbuf(f"A_sq{i}", [128, 1024], F32) for i in range(2)]
    ptf = [kb.psum(f"A_ptf{i}", [128, 1024], F32) for i in range(2)]
    ptb = [kb.psum(f"A_ptb{i}", [128, 1024], BF16) for i in range(2)]

    ld = [kb.sem(f"A_ld{i}") for i in range(2)]
    pp = [kb.sem(f"A_pp{i}") for i in range(2)]
    tp = kb.sem("A_tp")
    sv = kb.sem("A_sv")
    rv = kb.sem("A_rv")
    cpv = kb.sem("A_cpv")
    ss = kb.sem("A_ss")

    for t in range(A_TILES):
        b = t % 2
        if t >= 2:
            kb.y.wait(tp, t - 1)
        kb.y.dma_start(out=xt[b][:, :], in_=x[t * 128:(t + 1) * 128, :],
                       inc=(ld[b], 16))
        kb.s.wait(ld[b], ld[b].n)
        if t >= 2:
            kb.s.wait(rv, t - 1)
        kb.s.activation(sq[b][:, :], xt[b][:, :], AF.Square, inc=(ss, 1))
        kb.v.wait(ld[b], ld[b].n)
        kb.v.reduce_sum(st[:, 0:1], xt[b][:, :], axis=AX.X)
        kb.v.wait(ss, ss.n)
        kb.v.reduce_sum(st[:, 1:2], sq[b][:, :], axis=AX.X, inc=(rv, 1))
        kb.v.drain()
        kb.v.tensor_scalar(st[:, 2:3], st[:, 0:1], 1.0 / D, None, ALU.mult)
        kb.v.tensor_scalar(st[:, 3:4], st[:, 1:2], 1.0 / D, None, ALU.mult)
        kb.v.drain()
        kb.v.tensor_scalar(st[:, 4:5], st[:, 2:3], st[:, 2:3], -1.0,
                           ALU.mult, ALU.mult)
        kb.v.drain()
        kb.v.tensor_tensor(st[:, 5:6], st[:, 3:4], st[:, 4:5], ALU.add,
                           inc=(sv, 1))
        kb.s.wait(sv, sv.n)
        if t == 0:
            kb.s.wait(kb.CONST)
        kb.s.activation(st[:, 6:7], st[:, 5:6], AF.Sqrt, bias=epscol[:, :],
                        inc=(ss, 1))
        kb.v.wait(ss, ss.n)
        kb.v.reciprocal(st[:, 7:8], st[:, 6:7])
        kb.v.drain()
        kb.v.tensor_scalar(xnt[b][:, :], xt[b][:, :], st[:, 2:3], st[:, 7:8],
                           ALU.subtract, ALU.mult, inc=(sv, 1))
        kb.t.wait(sv, sv.n)
        if t >= 2:
            kb.t.wait(cpv, 2 * (t - 1))
        for g in range(NG):
            kb.t.transpose(ptb[b][:, g * 128:(g + 1) * 128],
                           xnt[b][:, g * 128:(g + 1) * 128], identb[:, :])
        for g in range(NG):
            last = (g == NG - 1)
            kb.t.transpose(ptf[b][:, g * 128:(g + 1) * 128],
                           xt[b][:, g * 128:(g + 1) * 128], identf[:, :],
                           inc=(tp, 1) if last else None)
        kb.v.wait(tp, t + 1)
        if t >= 2:
            kb.v.wait(pp[b], pp[b].n)
        kb.v.tensor_copy(stb[b][:, :], ptb[b][:, :], inc=(cpv, 1))
        kb.v.tensor_copy(stf[b][:, :], ptf[b][:, :], inc=(cpv, 1))
        kb.y.wait(cpv, cpv.n)
        src_b = stb[b][:, :].rearrange("p (g j) -> p g j", g=NG)
        dst_b = xn0[:, :, t * 128:(t + 1) * 128].rearrange("g p j -> p g j")
        kb.y.dma_start(out=dst_b, in_=src_b, inc=(pp[b], 16))
        src_f = stf[b][:, :].rearrange("p (g j) -> p g j", g=NG)
        dst_f = x_fm[:, :, t * 128:(t + 1) * 128].rearrange("g p j -> p g j")
        kb.y.dma_start(out=dst_f, in_=src_f, inc=(pp[b], 16))

    kb.stage_end([pp[0], pp[1]])
    kb.free_all_bufs()



SLAB_S = 8704


def stage_B(kb, Tn):
    xn0 = Tn["xn0"]
    wq_t, wk_t, wv_t = Tn["wq_t"], Tn["wk_t"], Tn["wv_t"]
    rbt = Tn["rbt"]
    s0 = Tn["s0"]
    nc = kb.nc
    text_d = nc.dram_tensor("text_d", [16, 8576], BF16, kind="Internal").ap()
    c_d = nc.dram_tensor("c_d", [16, 1], F32, kind="Internal").ap()
    wslab = nc.dram_tensor("wslab", [16, 128 * SLAB_S + 8576], BF16,
                           kind="Internal").ap()

    identf, sidf = build_identity(kb, "B_idf", F32)

    rb_sb = kb.sbuf("B_rb", [128, 512], F32)
    T_sb = kb.sbuf("B_T", [16, 4224], F32)
    text_sb = kb.sbuf("B_text", [16, 8576], BF16)
    ptr = [kb.psum(f"B_ptr{i}", [16, 512], F32) for i in range(2)]

    ldr = kb.sem("B_ldr")
    tt = kb.sem("B_tt")
    cp = kb.sem("B_cp")
    tw = kb.sem("B_tw")
    slw = kb.sem("B_slw")

    src1 = rbt[0:3968, :].rearrange("(c p) h -> p c h", p=128)
    kb.y.dma_start(out=rb_sb[:, 0:496].rearrange("p (c h) -> p c h", h=16),
                   in_=src1, inc=(ldr, 16))
    kb.y.dma_start(out=rb_sb[0:127, 496:512], in_=rbt[3968:4095, :],
                   inc=(ldr, 16))
    kb.t.wait(sidf)
    kb.t.wait(ldr)
    for c in range(32):
        if c >= 2:
            kb.t.wait(cp, c - 1)
        kb.t.transpose(ptr[c % 2][:, 0:128], rb_sb[:, c * 16:(c + 1) * 16],
                       identf[:, :], inc=(tt, 1))
        kb.v.wait(tt, tt.n)
        ncols = 128 if c < 31 else 127
        kb.v.tensor_copy(T_sb[:, c * 128:c * 128 + ncols],
                         ptr[c % 2][0:16, 0:ncols], inc=(cp, 1))
    kb.v.drain()
    kb.v.drain()
    kb.v.tensor_scalar(text_sb[:, 0:4095], T_sb[:, 0:4095],
                       T_sb[:, 4094:4095], None, ALU.subtract)
    kb.v.memset(text_sb[:, 4095:8576], 0.0, inc=(cp, 1))
    kb.y.wait(cp, cp.n)
    kb.y.dma_start(out=text_d, in_=text_sb[:, :], inc=(tw, 16))
    kb.y.dma_start(out=c_d, in_=T_sb[:, 4094:4095], inc=(tw, 16))
    kb.y.wait(tw)
    gsrc = bass.AP(text_d.tensor, 0, [[8576, 16], [0, 128], [1, 128]])
    gdst = bass.AP(wslab.tensor, 0, [[128 * SLAB_S, 16], [SLAB_S, 128],
                                     [1, 128]])
    kb.y.dma_start(out=gdst, in_=gsrc, inc=(tw, 16))
    kb.y.wait(tw)
    for h in range(16):
        src = bass.AP(text_d.tensor, h * 8576, [[0, 128], [1, 8576]])
        dst = bass.AP(wslab.tensor, h * 128 * SLAB_S, [[SLAB_S + 1, 128],
                                                       [1, 8576]])
        kb.y.dma_start(out=dst, in_=src, inc=(slw, 16))
    kb.free("B_ptr1", "B_ptr0", "B_text", "B_T", "B_rb")

    xn0_sb = kb.sbuf("B_xn0", [128, NG * T], BF16)
    qpair = kb.sbuf("B_q", [128, T], BF16)
    kpair = kb.sbuf("B_k", [128, T], BF16)
    vaug = [kb.sbuf(f"B_va{i}", [128, 32 * 65], BF16) for i in range(2)]
    esb = [kb.sbuf(f"B_e{i}", [128, 8192], BF16) for i in range(2)]
    slab_sb = [kb.sbuf(f"B_sl{i}", [128, 8576], BF16) for i in range(2)]
    wsl = [kb.sbuf(f"B_w{i}", [128, 1024], BF16) for i in range(3)]
    ones1f = kb.sbuf("B_o1f", [1, 128], F32)
    bb = kb.sbuf("B_bb", [1, 16], F32)
    sqbuf = [kb.sbuf(f"B_sqb{i}", [128, 512], F32) for i in range(2)]
    onescol = kb.sbuf("B_onescol", [128, 1], F32)
    negB = [kb.sbuf(f"B_nb{i}", [128, 1], F32) for i in range(2)]
    onescolb = kb.sbuf("B_ocb", [128, 1], BF16)
    onesrowb = kb.sbuf("B_orb", [1, 256], BF16)
    crow = kb.sbuf("B_crow", [1, 16], F32)
    vsumc = [kb.sbuf(f"B_vs{i}", [1, 65], BF16) for i in range(2)]
    srow = kb.sbuf("B_srow", [1, 256], F32)
    rrow = kb.sbuf("B_rrow", [1, 256], F32)
    rbc = kb.sbuf("B_rbc", [64, 256], F32)
    tmp = kb.sbuf("B_tmp", [64, 256], F32)
    osb = [kb.sbuf(f"B_osb{i}", [64, 256], BF16) for i in range(2)]

    ldx = kb.sem("B_ldx")
    wl = kb.sem("B_wl")
    sll = [kb.sem(f"B_sll{i}") for i in range(2)]
    mq = kb.sem("B_mq")
    vv = kb.sem("B_vv")
    qev = kb.sem("B_qev")
    sx = kb.sem("B_sx")
    ssq = kb.sem("B_ssq")
    sav = kb.sem("B_sav")
    fav = kb.sem("B_fav")
    rcp = kb.sem("B_rcp")
    prm = kb.sem("B_prm")
    ave = kb.sem("B_ave")
    bsem = kb.sem("B_bsem")
    po = [kb.sem(f"B_po{i}") for i in range(2)]

    kb.v.memset(ones1f[:, :], 1.0)
    kb.v.memset(onescol[:, :], 1.0)
    kb.v.memset(onescolb[:, :], 1.0)
    kb.v.memset(onesrowb[:, :], 1.0)
    for i in range(2):
        kb.v.memset(vaug[i][:, :].rearrange("p (c e) -> p c e", e=65)[:, :, 64:65],
                    1.0, inc=(qev, 1))
    kb.y.wait(tw)
    kb.y.dma_start(out=crow[0:1, 0:16],
                   in_=bass.AP(c_d.tensor, 0, [[0, 1], [1, 16]]),
                   inc=(ldx, 16))
    kb.y.dma_start(out=xn0_sb[:, :].rearrange("p (g t) -> p g t", t=T),
                   in_=xn0.rearrange("g p t -> p g t"), inc=(ldx, 16))

    NT1B = B_NT1B
    for pair in range(B_PAIRS):
        kb.g.wait(mq, mq.n)
        for i, w in enumerate((wq_t, wk_t, wv_t)):
            srcw = w[:, pair * 128:(pair + 1) * 128].rearrange(
                "(g p) e -> p g e", p=128)
            kb.g.dma_start(out=wsl[i][:, :].rearrange("p (g e) -> p g e", e=128),
                           in_=srcw, inc=(wl, 16))
        for i in range(2):
            if pair == 0:
                kb.y.wait(slw)
            else:
                kb.y.wait(fav, fav.n)
            srcs = bass.AP(wslab.tensor, (2 * pair + i) * 128 * SLAB_S,
                           [[SLAB_S, 128], [1, 8576]])
            kb.y.dma_start(out=slab_sb[i][:, :], in_=srcs, inc=(sll[i], 16))

        pq = [kb.psum(f"B_pq{pair}_{i}", [128, 512], F32) for i in range(2)]
        pv = [kb.psum(f"B_pv{pair}_{i}", [128, 512], F32) for i in range(2)]
        pbx = kb.psum(f"B_pbx{pair}", [128, 512], F32)
        pn = [kb.psum(f"B_pn{pair}_{i}", [1, 512], F32) for i in range(2)]
        kb.t.wait(wl)
        if pair == 0:
            kb.t.wait(ldx)
        for which, obuf in ((0, qpair), (1, kpair)):
            for tc in range(8):
                pqb = pq[tc % 2]
                kb.t.wait(qev, max(0, qev.n - 1))
                for g in range(NG):
                    kb.t.matmul(pqb[:, :],
                                wsl[which][:, g * 128:(g + 1) * 128],
                                xn0_sb[:, g * T + tc * 512:g * T + (tc + 1) * 512],
                                start=(g == 0), stop=(g == NG - 1),
                                inc=(mq, 1) if g == NG - 1 else None)
                kb.v.wait(mq, mq.n)
                kb.v.tensor_copy(obuf[:, tc * 512:(tc + 1) * 512], pqb[:, :],
                                 inc=(qev, 1))
        for t2c in range(32):
            pvb = pv[t2c % 2]
            kb.t.wait(vv, max(0, vv.n - 1))
            for g in range(NG):
                kb.t.matmul(pvb[:, 0:128],
                            xn0_sb[:, g * T + t2c * 128:g * T + (t2c + 1) * 128],
                            wsl[2][:, g * 128:(g + 1) * 128],
                            start=(g == 0), stop=(g == NG - 1),
                            inc=(mq, 1) if g == NG - 1 else None)
            kb.v.wait(mq, mq.n)
            kb.v.tensor_copy(vaug[0][:, t2c * 65:t2c * 65 + 64], pvb[:, 0:64])
            kb.v.tensor_copy(vaug[1][:, t2c * 65:t2c * 65 + 64], pvb[:, 64:128],
                             inc=(vv, 1))
        kb.v.wait(qev, qev.n)
        kb.v.wait(vv, vv.n)
        kb.v.wait(sx, sx.n)
        kb.v.memset(bb[:, 0:4], 0.0)
        kb.v.drain()
        kb.s.wait(qev, qev.n)
        kb.s.wait(vv, vv.n)
        rnd = 0
        for src_buf, c0 in ((qpair, 0), (kpair, 2)):
            for tc in range(8):
                sqb = sqbuf[rnd % 2]
                kb.s.wait(bsem, max(0, bsem.n - 2))
                kb.s.activation(sqb[:, :], src_buf[:, tc * 512:(tc + 1) * 512],
                                AF.Square, inc=(ssq, 1))
                kb.t.wait(ssq, ssq.n)
                kb.t.wait(qev, qev.n)
                for h in range(2):
                    kb.t.matmul(pn[h][0:1, 0:512], onescol[h * 64:(h + 1) * 64, :],
                                sqb[h * 64:(h + 1) * 64, :], start=True,
                                stop=True, tile_position=(64 * h, 0),
                                inc=(bsem, 1))
                rnd += 1
                kb.v.wait(bsem, bsem.n)
                for h in range(2):
                    kb.v.reduce_max(bb[:, 8 + h:9 + h], pn[h][0:1, 0:512],
                                    axis=AX.X)
                kb.v.drain()
                for h in range(2):
                    kb.v.tensor_tensor(bb[:, c0 + h:c0 + h + 1],
                                       bb[:, c0 + h:c0 + h + 1],
                                       bb[:, 8 + h:9 + h], ALU.max)
                kb.v.drain()
                kb.v.nop(inc=(qev, 1))
        kb.v.tensor_tensor(bb[:, 10:11], bb[:, 0:1], bb[:, 2:3], ALU.mult)
        kb.v.tensor_tensor(bb[:, 11:12], bb[:, 1:2], bb[:, 3:4], ALU.mult,
                           inc=(qev, 1))
        kb.s.wait(qev, qev.n)
        kb.s.activation(bb[:, 12:14], bb[:, 10:12], AF.Sqrt,
                        scale=1.0 / 1024.0, inc=(ssq, 1))
        kb.v.wait(ssq, ssq.n)
        kb.v.tensor_scalar(bb[:, 4:6], bb[:, 12:14], -1.0, None, ALU.mult,
                           inc=(qev, 1))
        kb.t.wait(qev, qev.n)
        for i in range(2):
            kb.t.matmul(pbx[:, 128 + i * 128:129 + i * 128], ones1f[:, :],
                        bb[:, 4 + i:5 + i], start=True, stop=True,
                        inc=(bsem, 1))
        kb.v.wait(bsem, bsem.n)
        kb.v.tensor_copy(negB[0][:, :], pbx[:, 128:129])
        kb.v.tensor_copy(negB[1][:, :], pbx[:, 256:257], inc=(qev, 1))
        kb.t.wait(vv, vv.n)
        for h in range(2):
            head = 2 * pair + h
            for c in range(32):
                kb.t.matmul(pn[h][0:1, 0:65], onescolb[:, :],
                            vaug[h][:, c * 65:(c + 1) * 65],
                            start=(c == 0), stop=(c == 31),
                            inc=(bsem, 1) if c == 31 else None)
            kb.v.wait(bsem, bsem.n)
            if pair == 0 and h == 0:
                kb.v.wait(ldx, 16)
            kb.v.tensor_scalar(vsumc[h][0:1, 0:65], pn[h][0:1, 0:65],
                               crow[0:1, head:head + 1], None, ALU.mult,
                               inc=(qev, 1))
        kb.free(f"B_pn{pair}_1", f"B_pn{pair}_0", f"B_pbx{pair}",
                f"B_pv{pair}_1", f"B_pv{pair}_0", f"B_pq{pair}_1",
                f"B_pq{pair}_0")

        ps_ = [[kb.psum(f"B_ps{pair}_{h}{i}", [128, 512], F32) for i in range(2)]
               for h in range(2)]
        pm = [kb.psum(f"B_pm{pair}_{h}", [65, 512], F32) for h in range(2)]
        pb = [kb.psum(f"B_pb{pair}_{h}", [65, 512], F32) for h in range(2)]
        kb.t.wait(qev, qev.n)
        for t1b in range(NT1B):
            t10 = t1b * 256
            kept = [c for c in range(32) if 128 * c + 126 > t10]
            for h in range(2):
                kb.t.wait(ave, ave.n)
                kb.t.wait(sll[h], sll[h].n)
                kb.t.matmul(pb[h][0:65, 0:256], vsumc[h][0:1, 0:65],
                            onesrowb[:, :], start=True, stop=False)
            fav_snap = [0, 0]
            sx_hist = []

            def emit_av(cp_):
                kb.t.wait(sx, sx_hist[cp_])
                for h in range(2):
                    for half in range(2):
                        t2c = 2 * cp_ + half
                        va = vaug[h][:, t2c * 65:(t2c + 1) * 65]
                        last = (cp_ == 15 and half == 1)
                        kb.t.matmul(pm[h][0:65, 0:256], va,
                                    esb[h][:, t2c * 256:(t2c + 1) * 256],
                                    start=(cp_ == 0 and half == 0),
                                    stop=last,
                                    inc=(fav, 1) if last else None)
                        if t2c in kept:
                            Boff = 4095 - 128 * t2c + t10
                            kb.t.matmul(pb[h][0:65, 0:256], va,
                                        slab_sb[h][:, Boff:Boff + 256],
                                        start=False, stop=last,
                                        inc=(fav, 1) if last else None)
                    if cp_ == 15:
                        fav_snap[h] = fav.n

            for cpair in range(16):
                for h in range(2):
                    buf = ps_[h][cpair % 2]
                    kb.t.wait(sx, max(0, sx.n - 2))
                    for half in range(2):
                        t2c = 2 * cpair + half
                        part = (0, 0) if h == 0 else (64, 0)
                        kb.t.matmul(
                            buf[:, half * 256:(half + 1) * 256],
                            kpair[h * 64:(h + 1) * 64,
                                  t2c * 128:(t2c + 1) * 128],
                            qpair[h * 64:(h + 1) * 64, t10:t10 + 256],
                            start=True, stop=True, tile_position=part,
                            inc=(sav, 1) if half == 1 else None)
                for h in range(2):
                    kb.s.wait(sav, sav.n - (1 - h))
                    if pair == 0 and t1b == 0 and cpair == 0:
                        kb.s.wait(qev, qev.n)
                    kb.s.activation(esb[h][:, cpair * 512:(cpair + 1) * 512],
                                    ps_[h][cpair % 2][:, :], AF.Exp,
                                    scale=1.0 / 32.0, bias=negB[h][:, :],
                                    inc=(sx, 1))
                sx_hist.append(sx.n)
                if cpair >= 1:
                    emit_av(cpair - 1)
            emit_av(15)
            for h in range(2):
                head = 2 * pair + h
                kb.v.wait(fav, fav_snap[h])
                kb.v.reciprocal(rrow[:, :], pm[h][64:65, 0:256], inc=(rcp, 1))
                kb.t.wait(rcp, rcp.n)
                kb.t.matmul(pm[h][0:64, 256:512], ones1f[:, 0:64], rrow[:, :],
                            start=True, stop=True, inc=(prm, 1))
                kb.v.wait(prm, prm.n)
                kb.v.wait(po[h], po[h].n)
                kb.v.tensor_copy(rbc[:, :], pm[h][0:64, 256:512])
                kb.v.drain()
                kb.v.tensor_tensor(tmp[:, :], pm[h][0:64, 0:256], rbc[:, :],
                                   ALU.mult)
                kb.v.drain()
                kb.v.tensor_tensor(osb[h][:, :], tmp[:, :], pb[h][0:64, 0:256],
                                   ALU.add, inc=(ave, 1))
                kb.y.wait(ave, ave.n)
                g, prow = head // 2, (head % 2) * 64
                kb.y.dma_start(out=s0[g, prow:prow + 64, t10:t10 + 256],
                               in_=osb[h][:, :], inc=(po[h], 16))
        kb.free(f"B_pb{pair}_1", f"B_pb{pair}_0",
                f"B_pm{pair}_1", f"B_pm{pair}_0",
                f"B_ps{pair}_11", f"B_ps{pair}_10",
                f"B_ps{pair}_01", f"B_ps{pair}_00")

    kb.stage_end([po[0], po[1]])
    kb.free_all_bufs()



def _bitlinear_weff(kb, wrow, wq, wcols, scol, n, sem):
    kb.v.reduce_sum(scol[:, 0:1], wrow[:, :], axis=AX.X,
                    apply_absolute_value=True)
    kb.v.drain()
    kb.v.tensor_scalar(scol[:, 1:2], scol[:, 0:1], 1.0 / n, None, ALU.mult)
    kb.v.drain()
    kb.v.tensor_scalar(scol[:, 2:3], scol[:, 1:2], 1e-5, None, ALU.max)
    kb.v.drain()
    kb.v.reciprocal(scol[:, 3:4], scol[:, 2:3])
    kb.v.drain()
    kb.v.tensor_scalar(wcols[:, :], wrow[:, :], scol[:, 3:4], None, ALU.mult)
    kb.v.drain()
    kb.v.tensor_scalar(wcols[:, :], wcols[:, :], MAGIC, MAGIC, ALU.add,
                       ALU.subtract)
    kb.v.drain()
    kb.v.tensor_scalar(wcols[:, :], wcols[:, :], 1.0, -1.0, ALU.min, ALU.max)
    kb.v.drain()
    kb.v.tensor_scalar(wq[:, :], wcols[:, :], scol[:, 2:3], None, ALU.mult,
                       inc=(sem, 1))


def stage_C(kb, Tn):
    xn0, s0, x_fm = Tn["xn0"], Tn["s0"], Tn["x_fm"]
    wfc, wproj = Tn["wfc"], Tn["wproj"]
    u_t, v_t, fb_t, gate = Tn["u_t"], Tn["v_t"], Tn["fb_t"], Tn["l0_gate"]
    lng, lnb = Tn["l0_lng"], Tn["l0_lnb"]
    x1_fm, x1stats = Tn["x1_fm"], Tn["x1stats"]
    nc = kb.nc
    s0mr = nc.dram_tensor("s0mr", [2, T], F32, kind="Internal").ap()

    epscol = kb.constcol("C_eps", 1e-5)
    identb, sid = build_identity(kb, "C_idb", BF16)
    ones1f = kb.sbuf("C_o1f", [1, 128], F32)
    onesb = kb.sbuf("C_onesb", [128, 1], BF16)
    onesf = kb.sbuf("C_onesf", [128, 1], F32)
    msem = kb.sem("C_ms")
    kb.v.memset(ones1f[:, :], 1.0)
    kb.v.memset(onesb[:, :], 1.0)
    kb.v.memset(onesf[:, :], 1.0, inc=(msem, 1))

    st_sb = kb.sbuf("C_st", [1, 8192], F32)
    s0t = [kb.sbuf(f"C_s0t{i}", [128, 512], BF16) for i in range(2)]
    s0sq = kb.sbuf("C_s0sq", [128, 512], BF16)
    pst = kb.psum("C_pst", [1, 512], F32)
    psq = kb.psum("C_psq", [1, 512], F32)
    l0 = [kb.sem(f"C_l0{i}") for i in range(2)]
    mm = kb.sem("C_mm")
    ev = kb.sem("C_ev")
    sc = kb.sem("C_sc")
    pw = kb.sem("C_pw")
    kb.t.wait(msem)
    for tc in range(8):
        for g in range(NG):
            b = (tc * NG + g) % 2
            kb.y.wait(mm, max(0, mm.n - 1))
            kb.y.wait(sc, max(0, sc.n - 1))
            kb.y.dma_start(out=s0t[b][:, :],
                           in_=s0[g, :, tc * 512:(tc + 1) * 512],
                           inc=(l0[b], 16))
            kb.s.wait(l0[b], l0[b].n)
            kb.s.wait(mm, mm.n)
            kb.s.activation(s0sq[:, :], s0t[b][:, :], AF.Square, inc=(sc, 1))
            kb.t.wait(l0[b], l0[b].n)
            kb.t.matmul(pst[0:1, :], onesb[:, :], s0t[b][:, :],
                        start=(g == 0), stop=(g == NG - 1))
            kb.t.wait(sc, sc.n)
            kb.t.matmul(psq[0:1, :], onesb[:, :], s0sq[:, :],
                        start=(g == 0), stop=(g == NG - 1), inc=(mm, 1))
        kb.v.wait(mm, mm.n)
        kb.v.tensor_copy(st_sb[:, tc * 512:(tc + 1) * 512], pst[0:1, :])
        kb.v.tensor_copy(st_sb[:, 4096 + tc * 512:4096 + (tc + 1) * 512],
                         psq[0:1, :], inc=(ev, 1))
        kb.t.wait(ev, ev.n)
    kb.v.drain()
    kb.v.tensor_scalar(st_sb[:, 0:4096], st_sb[:, 0:4096], 1.0 / D, None,
                       ALU.mult)
    kb.v.tensor_scalar(st_sb[:, 4096:8192], st_sb[:, 4096:8192], 1.0 / D,
                       None, ALU.mult)
    kb.v.drain()
    musq = kb.sbuf("C_musq", [1, 4096], F32)
    kb.v.tensor_tensor(musq[:, :], st_sb[:, 0:4096], st_sb[:, 0:4096],
                       ALU.mult)
    kb.v.drain()
    kb.v.tensor_tensor(st_sb[:, 4096:8192], st_sb[:, 4096:8192], musq[:, :],
                       ALU.subtract, inc=(ev, 1))
    kb.s.wait(ev, ev.n)
    kb.s.wait(kb.CONST)
    kb.s.activation(musq[:, :], st_sb[:, 4096:8192], AF.Sqrt,
                    bias=epscol[0:1, :], inc=(sc, 1))
    kb.v.wait(sc, sc.n)
    kb.v.reciprocal(st_sb[:, 4096:8192], musq[:, :], inc=(ev, 1))
    kb.y.wait(ev, ev.n)
    kb.y.dma_start(out=bass.AP(s0mr.tensor, 0, [[0, 1], [T, 2], [1, T]]),
                   in_=st_sb[:, :].rearrange("o (a t) -> o a t", a=2),
                   inc=(pw, 16))
    kb.stage_end([pw])
    kb.free("C_musq", "C_psq", "C_pst", "C_s0sq", "C_s0t1", "C_s0t0", "C_st")

    wfcT = kb.sbuf("C_wfcT", [128, NG * 4096], BF16)
    wprojT = kb.sbuf("C_wprojT", [128, 32 * 1024], BF16)
    wrow = [kb.sbuf(f"C_wrow{i}", [128, 4096], F32) for i in range(2)]
    wcols = kb.sbuf("C_wcols", [128, 4096], F32)
    wqb = kb.sbuf("C_wqb", [128, 4096], BF16)
    scol = kb.sbuf("C_scol", [128, 4], F32)
    ptw = kb.psum("C_ptw", [128, 1024], BF16)
    lw = [kb.sem(f"C_lw{i}") for i in range(2)]
    tb = kb.sem("C_tb")
    tt2 = kb.sem("C_tt2")
    tb2 = kb.sem("C_tb2")

    kb.t.wait(sid)
    for jg in range(32):
        b = jg % 2
        kb.y.wait(ev, ev.n)
        kb.y.dma_start(out=wrow[b][:, 0:1024],
                       in_=wfc[jg * 128:(jg + 1) * 128, :], inc=(lw[b], 16))
        kb.v.wait(lw[b], lw[b].n)
        kb.v.wait(tb, tb.n)
        _bitlinear_weff(kb, wrow[b][:, 0:1024], wqb[:, 0:1024],
                        wcols[:, 0:1024], scol, 1024, ev)
        kb.t.wait(ev, ev.n)
        kb.t.wait(tb, tb.n)
        for g in range(NG):
            kb.t.transpose(ptw[:, g * 128:(g + 1) * 128],
                           wqb[:, g * 128:(g + 1) * 128], identb[:, :],
                           inc=(tb, 1) if g == NG - 1 else None)
        kb.v.wait(tb, tb.n)
        for g in range(NG):
            kb.v.tensor_copy(wfcT[:, g * 4096 + jg * 128:g * 4096 + (jg + 1) * 128],
                             ptw[:, g * 128:(g + 1) * 128],
                             inc=(tb, 1) if g == NG - 1 else None)
    for dg in range(NG):
        b = dg % 2
        kb.y.wait(ev, ev.n)
        kb.y.dma_start(out=wrow[b][:, :],
                       in_=wproj[dg * 128:(dg + 1) * 128, :], inc=(lw[b], 16))
        kb.v.wait(lw[b], lw[b].n)
        kb.v.wait(tt2, tt2.n)
        _bitlinear_weff(kb, wrow[b][:, :], wqb[:, :], wcols[:, :], scol,
                        4096, ev)
        kb.t.wait(ev, ev.n)
        for jb in range(4):
            kb.t.wait(tb2, tb2.n)
            for jj in range(8):
                jc = jb * 8 + jj
                kb.t.transpose(ptw[:, jj * 128:(jj + 1) * 128],
                               wqb[:, jc * 128:(jc + 1) * 128], identb[:, :],
                               inc=(tt2, 1) if jj == 7 else None)
            kb.v.wait(tt2, tt2.n)
            for jj in range(8):
                jc = jb * 8 + jj
                kb.v.tensor_copy(
                    wprojT[:, jc * 1024 + dg * 128:jc * 1024 + (dg + 1) * 128],
                    ptw[:, jj * 128:(jj + 1) * 128],
                    inc=(tb2, 1) if jj == 7 else None)
        kb.v.nop(inc=(tb, 1))
    kb.stage_end()
    kb.free("C_ptw", "C_scol", "C_wqb", "C_wcols", "C_wrow1", "C_wrow0")

    usl = kb.sbuf("C_usl", [16, 4096], BF16)
    vsl = kb.sbuf("C_vsl", [128, 128], BF16)
    fbsl = kb.sbuf("C_fb", [16, 16], BF16)
    gcol = kb.sbuf("C_gcol", [16, 1], F32)
    b_sb = kb.sbuf("C_bsb", [16, 4096], BF16)
    xnt = kb.sbuf("C_xnt", [128, 4096], BF16)
    gf = kb.sbuf("C_gf", [1, 1], F32)
    a_sb = kb.sbuf("C_asb", [16, 4096], BF16)
    pfa = kb.psum("C_pfa", [16, 512], F32)
    pga = kb.psum("C_pga", [16, 512], F32)
    lf = kb.sem("C_lf")
    lx = kb.sem("C_lx")

    kb.g.dma_start(out=usl[:, :], in_=u_t, inc=(lf, 16))
    kb.g.dma_start(out=vsl[:, :].rearrange("p (g r) -> p g r", r=16),
                   in_=v_t.rearrange("(g p) r -> p g r", p=128), inc=(lf, 16))
    kb.g.dma_start(out=fbsl[:, :], in_=fb_t, inc=(lf, 16))
    kb.y.dma_start(out=gf[:, :], in_=gate, inc=(lx, 16))
    kb.t.wait(lx, lx.n)
    kb.t.wait(msem)
    kb.t.matmul(pga[0:16, 0:1], ones1f[:, 0:16], gf[:, :], start=True,
                stop=True, inc=(mm, 1))
    kb.v.wait(mm, mm.n)
    kb.v.tensor_copy(gcol[:, :], pga[0:16, 0:1], inc=(ev, 1))
    for tc in range(8):
        kb.y.wait(mm, mm.n)
        kb.y.dma_start(out=xnt[:, :].rearrange("p (g t) -> p g t", t=512),
                       in_=xn0[:, :, tc * 512:(tc + 1) * 512].rearrange(
                           "g p t -> p g t"), inc=(lx, 16))
        kb.t.wait(lx, lx.n)
        kb.t.wait(lf, lf.n)
        kb.t.wait(ev, ev.n)
        for g in range(NG):
            kb.t.matmul(pfa[0:16, :], vsl[:, g * 16:(g + 1) * 16],
                        xnt[:, g * 512:(g + 1) * 512],
                        start=(g == 0), stop=(g == NG - 1),
                        inc=(mm, 1) if g == NG - 1 else None)
        kb.v.wait(mm, mm.n)
        kb.v.tensor_copy(a_sb[:, tc * 512:(tc + 1) * 512], pfa[0:16, :],
                         inc=(ev, 1))
    for tc in range(8):
        kb.t.wait(ev, ev.n)
        kb.t.matmul(pga[0:16, :], fbsl[:, :], a_sb[:, tc * 512:(tc + 1) * 512],
                    start=True, stop=True, inc=(mm, 1))
        kb.v.wait(mm, mm.n)
        kb.v.tensor_scalar(b_sb[:, tc * 512:(tc + 1) * 512], pga[0:16, :],
                           gcol[:, :], None, ALU.mult, inc=(ev, 1))
    kb.stage_end()
    kb.free("C_pga", "C_pfa", "C_asb")

    rr = kb.sbuf("C_rr", [128, 32 * 512], BF16)
    rr1 = kb.sbuf("C_rr1", [128, 512], BF16)
    g0t = kb.sbuf("C_g0t", [128, 512], BF16)
    s0c = kb.sbuf("C_s0c", [128, 512], BF16)
    xtc = kb.sbuf("C_xtc", [128, 512], F32)
    mrow = kb.sbuf("C_mrow", [1, 1024], F32)
    mub = kb.sbuf("C_mub", [128, 512], F32)
    rsb = kb.sbuf("C_rsb", [128, 512], F32)
    gcols = kb.sbuf("C_gcols", [128, 8], F32)
    bcols = kb.sbuf("C_bcols", [128, 8], F32)
    t1 = kb.sbuf("C_t1", [128, 512], F32)
    t2 = kb.sbuf("C_t2", [128, 512], F32)
    xn1t = kb.sbuf("C_xn1t", [128, 512], BF16)
    x1t = kb.sbuf("C_x1t", [128, 512], F32)
    x1sq = kb.sbuf("C_x1sq", [128, 512], F32)
    ph = [kb.psum(f"C_ph{i}", [128, 512], F32) for i in range(2)]
    po = [kb.psum(f"C_po{i}", [128, 512], F32) for i in range(2)]
    pb2 = kb.psum("C_pb2", [128, 1024], F32)
    pst2 = kb.psum("C_pst2", [1, 1024], F32)
    lm = kb.sem("C_lm")
    lsx = [kb.sem(f"C_lsx{i}") for i in range(3)]
    rvv = kb.sem("C_rvv")
    scc = kb.sem("C_scc")
    pxw = kb.sem("C_pxw")

    kb.g.dma_start(out=gcols[:, :],
                   in_=bass.AP(lng.tensor, 0, [[1, 128], [128, 8]]),
                   allow_slow_non_contiguous=True, inc=(lm, 16))
    kb.g.dma_start(out=bcols[:, :],
                   in_=bass.AP(lnb.tensor, 0, [[1, 128], [128, 8]]),
                   allow_slow_non_contiguous=True, inc=(lm, 16))

    for tc in range(8):
        kb.y.wait(mm, mm.n)
        kb.y.dma_start(out=xnt[:, :].rearrange("p (g t) -> p g t", t=512),
                       in_=xn0[:, :, tc * 512:(tc + 1) * 512].rearrange(
                           "g p t -> p g t"), inc=(lx, 16))
        kb.y.wait(pw, pw.n)
        kb.y.wait(pxw, pxw.n)
        kb.y.dma_start(out=mrow[:, :].rearrange("o (a t) -> o a t", a=2),
                       in_=bass.AP(s0mr.tensor, tc * 512,
                                   [[0, 1], [T, 2], [1, 512]]),
                       inc=(lsx[2], 16))
        kb.t.wait(lsx[2], lsx[2].n)
        kb.t.wait(rvv, rvv.n)
        kb.t.matmul(pb2[:, 0:512], ones1f[:, :], mrow[:, 0:512], start=True,
                    stop=True)
        kb.t.matmul(pb2[:, 512:1024], ones1f[:, :], mrow[:, 512:1024],
                    start=True, stop=True, inc=(mm, 1))
        kb.v.wait(mm, mm.n)
        kb.v.tensor_copy(mub[:, :], pb2[:, 0:512])
        kb.v.tensor_copy(rsb[:, :], pb2[:, 512:1024], inc=(rvv, 1))
        kb.t.wait(lx, lx.n)
        for jsl in range(32):
            pp = ph[jsl % 2]
            kb.t.wait(rvv, max(0, rvv.n - 1))
            for g in range(NG):
                kb.t.matmul(pp[:, :],
                            wfcT[:, g * 4096 + jsl * 128:g * 4096 + (jsl + 1) * 128],
                            xnt[:, g * 512:(g + 1) * 512],
                            start=(g == 0), stop=False)
            kb.t.matmul(pp[:, :], usl[:, jsl * 128:(jsl + 1) * 128],
                        b_sb[:, tc * 512:(tc + 1) * 512],
                        start=False, stop=True, inc=(mm, 1))
            kb.v.wait(mm, mm.n)
            kb.v.wait(scc, scc.n)
            kb.v.tensor_scalar(rr1[:, :], pp[:, :], 0.0, None, ALU.max,
                               inc=(rvv, 1))
            kb.s.wait(rvv, rvv.n)
            kb.s.activation(rr[:, jsl * 512:(jsl + 1) * 512], rr1[:, :],
                            AF.Square, inc=(scc, 1))
        kb.y.wait(scc, scc.n)
        kb.y.wait(pxw, pxw.n)
        for ds in range(NG):
            kb.y.dma_start(out=s0c[:, :],
                           in_=s0[ds, :, tc * 512:(tc + 1) * 512],
                           inc=(lsx[0], 16))
            kb.y.dma_start(out=xtc[:, :],
                           in_=x_fm[ds, :, tc * 512:(tc + 1) * 512],
                           inc=(lsx[1], 16))
            pq = po[ds % 2]
            kb.t.wait(scc, scc.n)
            kb.t.wait(rvv, rvv.n)
            for jc in range(32):
                kb.t.matmul(pq[:, :],
                            wprojT[:, jc * 1024 + ds * 128:jc * 1024 + (ds + 1) * 128],
                            rr[:, jc * 512:(jc + 1) * 512],
                            start=(jc == 0), stop=(jc == 31),
                            inc=(mm, 1) if jc == 31 else None)
            kb.s.wait(mm, mm.n)
            kb.s.activation(g0t[:, :], pq[:, :], AF.Silu, inc=(scc, 1))
            kb.v.wait(scc, scc.n)
            kb.v.wait(lsx[0], lsx[0].n)
            kb.v.wait(lsx[1], lsx[1].n)
            if tc == 0 and ds == 0:
                kb.v.wait(lm, lm.n)
            kb.v.wait(mm, mm.n)
            kb.v.wait(pxw, pxw.n)
            kb.v.tensor_tensor(t1[:, :], s0c[:, :], mub[:, :], ALU.subtract)
            kb.v.drain()
            kb.v.tensor_tensor(t2[:, :], t1[:, :], rsb[:, :], ALU.mult)
            kb.v.drain()
            kb.v.tensor_scalar(t1[:, :], t2[:, :], gcols[:, ds:ds + 1],
                               bcols[:, ds:ds + 1], ALU.mult, ALU.add)
            kb.v.drain()
            kb.v.tensor_tensor(t2[:, :], t1[:, :], g0t[:, :], ALU.mult)
            kb.v.drain()
            kb.v.tensor_tensor(x1t[:, :], t2[:, :], xtc[:, :], ALU.add,
                               inc=(rvv, 1))
            kb.y.wait(rvv, rvv.n)
            kb.y.dma_start(out=x1_fm[ds, :, tc * 512:(tc + 1) * 512],
                           in_=x1t[:, :], inc=(pxw, 16))
            kb.s.wait(rvv, rvv.n)
            kb.s.activation(x1sq[:, :], x1t[:, :], AF.Square, inc=(scc, 1))
            kb.t.wait(rvv, rvv.n)
            kb.t.matmul(pst2[0:1, 0:512], onesf[:, :], x1t[:, :],
                        start=(ds == 0), stop=(ds == NG - 1))
            kb.t.wait(scc, scc.n)
            kb.t.matmul(pst2[0:1, 512:1024], onesf[:, :], x1sq[:, :],
                        start=(ds == 0), stop=(ds == NG - 1), inc=(mm, 1))
        kb.v.wait(mm, mm.n)
        kb.v.tensor_copy(mrow[:, :], pst2[0:1, :], inc=(rvv, 1))
        kb.t.wait(rvv, rvv.n)
        kb.v.drain()
        kb.v.tensor_scalar(mrow[:, :], mrow[:, :], 1.0 / D, None, ALU.mult)
        kb.v.drain()
        kb.v.tensor_tensor(x1sq[0:1, 0:512], mrow[:, 0:512], mrow[:, 0:512],
                           ALU.mult)
        kb.v.drain()
        kb.v.tensor_tensor(mrow[:, 512:1024], mrow[:, 512:1024],
                           x1sq[0:1, 0:512], ALU.subtract, inc=(rvv, 1))
        kb.s.wait(rvv, rvv.n)
        kb.s.activation(x1sq[0:1, 0:512], mrow[:, 512:1024], AF.Sqrt,
                        bias=epscol[0:1, :], inc=(scc, 1))
        kb.v.wait(scc, scc.n)
        kb.v.reciprocal(mrow[:, 512:1024], x1sq[0:1, 0:512], inc=(rvv, 1))
        kb.t.wait(rvv, rvv.n)
        kb.t.matmul(pb2[:, 0:512], ones1f[:, :], mrow[0:1, 0:512], start=True,
                    stop=True)
        kb.t.matmul(pb2[:, 512:1024], ones1f[:, :], mrow[0:1, 512:1024],
                    start=True, stop=True, inc=(mm, 1))
        kb.v.wait(mm, mm.n)
        kb.v.tensor_copy(mub[:, :], pb2[:, 0:512])
        kb.v.tensor_copy(rsb[:, :], pb2[:, 512:1024], inc=(rvv, 1))
        kb.y.wait(pxw, pxw.n)
        for g in range(NG):
            kb.y.wait(rvv, rvv.n)
            kb.y.dma_start(out=xtc[:, :],
                           in_=x1_fm[g, :, tc * 512:(tc + 1) * 512],
                           inc=(lsx[1], 16))
            kb.v.wait(lsx[1], lsx[1].n)
            kb.v.wait(pxw, pxw.n)
            kb.v.tensor_tensor(t1[:, :], xtc[:, :], mub[:, :], ALU.subtract)
            kb.v.drain()
            kb.v.tensor_tensor(xn1t[:, :], t1[:, :], rsb[:, :], ALU.mult,
                               inc=(rvv, 1))
            kb.y.wait(rvv, rvv.n)
            kb.y.dma_start(out=Tn["xn1"][g, :, tc * 512:(tc + 1) * 512],
                           in_=xn1t[:, :], inc=(pxw, 16))

    kb.stage_end([pxw, pw])
    kb.free_all_bufs()



def stage_D(kb, Tn):
    x1_fm, x1stats, xn1 = Tn["x1_fm"], Tn["x1stats"], Tn["xn1"]
    epscol = kb.constcol("D_eps", 1e-5)
    ones1f = kb.sbuf("D_o1f", [1, 128], F32)
    strow = kb.sbuf("D_st", [1, 8192], F32)
    musq = kb.sbuf("D_musq", [1, 4096], F32)
    mub = kb.sbuf("D_mub", [128, 512], F32)
    rsb = kb.sbuf("D_rsb", [128, 512], F32)
    x1t = [kb.sbuf(f"D_x1t{i}", [128, 512], F32) for i in range(2)]
    xnt = [kb.sbuf(f"D_xnt{i}", [128, 512], BF16) for i in range(2)]
    t1 = kb.sbuf("D_t1", [128, 512], F32)
    pb2 = kb.psum("D_pb2", [128, 1024], F32)
    ls = kb.sem("D_ls")
    lx = [kb.sem(f"D_lx{i}") for i in range(2)]
    mm = kb.sem("D_mm")
    ev = kb.sem("D_ev")
    sc = kb.sem("D_sc")
    pw = kb.sem("D_pw")

    kb.v.memset(ones1f[:, :], 1.0, inc=(kb.CONST, 1))
    kb.y.dma_start(out=strow[:, :].rearrange("o (a t) -> o a t", a=2),
                   in_=bass.AP(x1stats.tensor, 0, [[0, 1], [T, 2], [1, T]]),
                   inc=(ls, 16))
    kb.v.wait(ls, ls.n)
    kb.v.tensor_scalar(strow[:, 0:4096], strow[:, 0:4096], 1.0 / D, None,
                       ALU.mult)
    kb.v.tensor_scalar(strow[:, 4096:8192], strow[:, 4096:8192], 1.0 / D,
                       None, ALU.mult)
    kb.v.drain()
    kb.v.tensor_tensor(musq[:, :], strow[:, 0:4096], strow[:, 0:4096],
                       ALU.mult)
    kb.v.drain()
    kb.v.tensor_tensor(strow[:, 4096:8192], strow[:, 4096:8192], musq[:, :],
                       ALU.subtract, inc=(ev, 1))
    kb.s.wait(ev, ev.n)
    kb.s.wait(kb.CONST)
    kb.s.activation(musq[:, :], strow[:, 4096:8192], AF.Sqrt,
                    bias=epscol[0:1, :], inc=(sc, 1))
    kb.v.wait(sc, sc.n)
    kb.v.reciprocal(strow[:, 4096:8192], musq[:, :], inc=(ev, 1))
    for tc in range(8):
        kb.t.wait(ev, ev.n)
        kb.t.matmul(pb2[:, 0:512], ones1f[:, :],
                    strow[0:1, tc * 512:(tc + 1) * 512], start=True, stop=True)
        kb.t.matmul(pb2[:, 512:1024], ones1f[:, :],
                    strow[0:1, 4096 + tc * 512:4096 + (tc + 1) * 512],
                    start=True, stop=True, inc=(mm, 1))
        kb.v.wait(mm, mm.n)
        kb.v.tensor_copy(mub[:, :], pb2[:, 0:512])
        kb.v.tensor_copy(rsb[:, :], pb2[:, 512:1024], inc=(ev, 1))
        for g in range(NG):
            b = g % 2
            kb.y.wait(ev, ev.n)
            kb.y.dma_start(out=x1t[b][:, :],
                           in_=x1_fm[g, :, tc * 512:(tc + 1) * 512],
                           inc=(lx[b], 16))
            kb.v.wait(lx[b], lx[b].n)
            kb.v.wait(pw, pw.n)
            kb.v.tensor_tensor(t1[:, :], x1t[b][:, :], mub[:, :], ALU.subtract)
            kb.v.drain()
            kb.v.tensor_tensor(xnt[b][:, :], t1[:, :], rsb[:, :], ALU.mult,
                               inc=(ev, 1))
            kb.y.wait(ev, ev.n)
            kb.y.dma_start(out=xn1[g, :, tc * 512:(tc + 1) * 512],
                           in_=xnt[b][:, :], inc=(pw, 16))
    kb.stage_end([pw])
    kb.free_all_bufs()



def stage_E(kb, Tn):
    xn1 = Tn["xn1"]
    win, bin_, plog = Tn["l1_win"], Tn["l1_bin"], Tn["l1_plog"]
    wmid, bmid = Tn["l1_wmid"], Tn["l1_bmid"]
    w1, w1b, w2, w2b = Tn["l1_w1"], Tn["l1_w1b"], Tn["l1_w2"], Tn["l1_w2b"]
    lng, lnb = Tn["l1_lng"], Tn["l1_lnb"]
    s1 = Tn["s1"]
    nc = kb.nc
    hpark = nc.dram_tensor("hpark", [4, NG, 128, T], BF16,
                           kind="ExternalOutput" if DBG_PARKS else "Internal").ap()

    PI = float(np.pi)
    epscol = kb.constcol("E_eps", 1e-5)
    pi2col = kb.constcol("E_pi2", PI / 2.0)
    identf, sidf = build_identity(kb, "E_idf", F32)
    ones1f = kb.sbuf("E_o1f", [1, 128], F32)
    onesrow = kb.sbuf("E_orow", [1, 4096], BF16)
    onesf = kb.sbuf("E_onesf", [128, 1], F32)
    msem = kb.sem("E_ms")
    kb.v.memset(ones1f[:, :], 1.0)
    kb.v.memset(onesrow[:, :], 1.0)
    kb.v.memset(onesf[:, :], 1.0, inc=(msem, 1))

    plog_sb = kb.sbuf("E_plog", [3, 1024], F32)
    pp = kb.sbuf("E_pp", [128, 24], F32)
    nu8 = kb.sbuf("E_nu8", [128, 8], F32)
    rho8 = kb.sbuf("E_rho8", [128, 8], F32)
    th8 = kb.sbuf("E_th8", [128, 8], F32)
    ga8 = kb.sbuf("E_ga8", [128, 8], F32)
    cth8 = kb.sbuf("E_cth8", [128, 8], F32)
    sth8 = kb.sbuf("E_sth8", [128, 8], F32)
    tsc = kb.sbuf("E_tsc", [128, 24], F32)
    ptp = kb.psum("E_ptp", [128, 512], F32)
    lp = kb.sem("E_lp")
    tt0 = kb.sem("E_tt0")
    ev = kb.sem("E_ev")
    sc = kb.sem("E_sc")

    kb.y.dma_start(out=plog_sb[:, :], in_=plog, inc=(lp, 16))
    kb.t.wait(sidf)
    kb.t.wait(lp)
    for g in range(NG):
        kb.t.transpose(ptp[:, g * 4:g * 4 + 3], plog_sb[:, g * 128:(g + 1) * 128],
                       identf[0:3, 0:3], inc=(tt0, 1) if g == NG - 1 else None)
    kb.v.wait(tt0, tt0.n)
    kb.v.tensor_copy(pp[:, :].rearrange("p (g e) -> p g e", e=3),
                     ptp[:, 0:32].rearrange("p (g e) -> p g e", e=4)[:, :, 0:3],
                     inc=(ev, 1))
    kb.v.drain()
    kb.v.tensor_copy(nu8[:, :], pp[:, :].rearrange("p (g e) -> p e g", e=3)[:, 0, :])
    kb.v.tensor_copy(th8[:, :], pp[:, :].rearrange("p (g e) -> p e g", e=3)[:, 1, :])
    kb.v.tensor_copy(ga8[:, :], pp[:, :].rearrange("p (g e) -> p e g", e=3)[:, 2, :],
                     inc=(ev, 1))
    kb.s.wait(ev, ev.n)
    kb.s.activation(nu8[:, :], nu8[:, :], AF.Exp)
    kb.s.drain()
    kb.s.activation(rho8[:, :], nu8[:, :], AF.Exp, scale=-1.0)
    kb.s.activation(th8[:, :], th8[:, :], AF.Exp)
    kb.s.drain()
    kb.s.activation(ga8[:, :], ga8[:, :], AF.Exp, inc=(sc, 1))
    kb.v.wait(sc, sc.n)
    kb.v.tensor_scalar(tsc[:, 0:8], th8[:, :], 1.0 / (2 * PI), MAGIC, ALU.mult,
                       ALU.add)
    kb.v.drain()
    kb.v.tensor_scalar(tsc[:, 0:8], tsc[:, 0:8], MAGIC, -2.0 * PI,
                       ALU.subtract, ALU.mult)
    kb.v.drain()
    kb.v.tensor_tensor(th8[:, :], th8[:, :], tsc[:, 0:8], ALU.add,
                       inc=(ev, 1))
    kb.s.wait(ev, ev.n)
    kb.s.wait(kb.CONST)
    kb.s.activation(tsc[:, 8:16], th8[:, :], AF.Sin, scale=0.5)
    kb.s.drain()
    kb.s.activation(tsc[:, 16:24], th8[:, :], AF.Sin, scale=-0.5,
                    bias=pi2col[:, :])
    kb.s.drain()
    kb.s.activation(tsc[:, 0:8], tsc[:, 8:16], AF.Square, inc=(sc, 1))
    kb.v.wait(sc, sc.n)
    kb.v.scalar_tensor_tensor(sth8[:, :], tsc[:, 8:16], 2.0, tsc[:, 16:24],
                              ALU.mult, ALU.mult)
    kb.v.tensor_scalar(cth8[:, :], tsc[:, 0:8], -2.0, 1.0, ALU.mult, ALU.add,
                       inc=(ev, 1))
    kb.free("E_ptp")

    xn1_sb = kb.sbuf("E_xn1", [128, NG * T], BF16)
    lxx = kb.sem("E_lxx")
    kb.y.dma_start(out=xn1_sb[:, :].rearrange("p (g t) -> p g t", t=T),
                   in_=xn1.rearrange("g p t -> p g t"), inc=(lxx, 16))

    ur = kb.sbuf("E_ur", [128, T], F32)
    ui = kb.sbuf("E_ui", [128, T], F32)
    gr = kb.sbuf("E_gr", [128, T], F32)
    gi = kb.sbuf("E_gi", [128, T], F32)
    cs = kb.sbuf("E_cs", [128, 520], F32)
    sn = kb.sbuf("E_sn", [128, 520], F32)
    rho_rep = kb.sbuf("E_rhor", [128, 512], F32)
    sc1 = kb.sbuf("E_sc1", [128, 520], F32)
    sc2 = kb.sbuf("E_sc2", [128, 520], F32)
    sc3 = kb.sbuf("E_sc3", [128, 520], F32)
    sc4 = kb.sbuf("E_sc4", [128, 520], F32)
    gcr = kb.sbuf("E_gcr", [128, 1], F32)
    gci = kb.sbuf("E_gci", [128, 1], F32)
    hb = kb.sbuf("E_hb", [128, 2048], BF16)
    wine = [kb.sbuf(f"E_wine{i}", [128, 1024], BF16) for i in range(2)]
    brow = kb.sbuf("E_brow", [1, 256], BF16)
    pu = [kb.psum(f"E_pu{i}", [128, 512], F32) for i in range(2)]
    lw = kb.sem("E_lw")
    warm = kb.sem("E_warm")
    mm = kb.sem("E_mm")
    pwh = kb.sem("E_pwh")
    gsm = kb.sem("E_gsm")
    psm = kb.sem("E_psm")

    for pg in range(NG):
        kb.g.wait(mm, mm.n)
        for e in range(2):
            srcw = win[e, :, pg * 128:(pg + 1) * 128].rearrange(
                "(g p) h -> p g h", p=128)
            kb.g.dma_start(out=wine[e][:, :].rearrange("p (g h) -> p g h", h=128),
                           in_=srcw, inc=(lw, 16))
            kb.g.dma_start(out=brow[:, e * 128:(e + 1) * 128],
                           in_=bin_[e:e + 1, pg * 128:(pg + 1) * 128],
                           inc=(lw, 16))
        RC = 512
        kb.v.wait(ev, ev.n)
        kb.v.wait(psm, psm.n)
        kb.v.memset(cs[:, 0:1], 1.0)
        kb.v.memset(sn[:, 0:1], 0.0)
        kb.v.tensor_copy(cs[:, 1:2], cth8[:, pg:pg + 1])
        kb.v.tensor_copy(sn[:, 1:2], sth8[:, pg:pg + 1])
        kb.v.drain()
        n = 2
        while n < RC + 1:
            m = min(n, RC + 1 - n)
            kb.v.tensor_scalar(sc1[:, 0:1], cs[:, n - 1:n], cth8[:, pg:pg + 1],
                               None, ALU.mult)
            kb.v.tensor_scalar(sc3[:, 0:1], sn[:, n - 1:n], sth8[:, pg:pg + 1],
                               None, ALU.mult)
            kb.v.tensor_scalar(sc2[:, 0:1], sn[:, n - 1:n], cth8[:, pg:pg + 1],
                               None, ALU.mult)
            kb.v.tensor_scalar(sc4[:, 0:1], cs[:, n - 1:n], sth8[:, pg:pg + 1],
                               None, ALU.mult)
            kb.v.drain()
            kb.v.tensor_tensor(sc1[:, 0:1], sc1[:, 0:1], sc3[:, 0:1],
                               ALU.subtract)
            kb.v.tensor_tensor(sc2[:, 0:1], sc2[:, 0:1], sc4[:, 0:1], ALU.add)
            kb.v.drain()
            kb.v.tensor_scalar(cs[:, n:n + m], cs[:, 0:m], sc1[:, 0:1], None,
                               ALU.mult)
            kb.v.tensor_scalar(sn[:, n:n + m], sn[:, 0:m], sc1[:, 0:1], None,
                               ALU.mult)
            kb.v.tensor_scalar(gr[:, 0:m], sn[:, 0:m], sc2[:, 0:1], None,
                               ALU.mult)
            kb.v.tensor_scalar(gi[:, 0:m], cs[:, 0:m], sc2[:, 0:1], None,
                               ALU.mult)
            kb.v.drain()
            kb.v.tensor_tensor(cs[:, n:n + m], cs[:, n:n + m], gr[:, 0:m],
                               ALU.subtract)
            kb.v.tensor_tensor(sn[:, n:n + m], sn[:, n:n + m], gi[:, 0:m],
                               ALU.add)
            kb.v.drain()
            n *= 2
        kb.v.tensor_tensor(sc1[:, 0:RC + 1], cs[:, 0:RC + 1], cs[:, 0:RC + 1],
                           ALU.mult)
        kb.v.tensor_tensor(sc2[:, 0:RC + 1], sn[:, 0:RC + 1], sn[:, 0:RC + 1],
                           ALU.mult)
        kb.v.drain()
        kb.v.tensor_tensor(sc1[:, 0:RC + 1], sc1[:, 0:RC + 1],
                           sc2[:, 0:RC + 1], ALU.add, inc=(ev, 1))
        kb.s.wait(ev, ev.n)
        kb.s.activation(sc2[:, 0:RC + 1], sc1[:, 0:RC + 1], AF.Sqrt,
                        inc=(sc, 1))
        kb.v.wait(sc, sc.n)
        kb.v.reciprocal(sc1[:, 0:RC + 1], sc2[:, 0:RC + 1])
        kb.v.drain()
        kb.v.tensor_tensor(cs[:, 0:RC + 1], cs[:, 0:RC + 1], sc1[:, 0:RC + 1],
                           ALU.mult)
        kb.v.tensor_tensor(sn[:, 0:RC + 1], sn[:, 0:RC + 1], sc1[:, 0:RC + 1],
                           ALU.mult)
        kb.v.drain()
        kb.v.memset(rho_rep[:, 0:RC], 0.0)
        kb.v.drain()
        kb.v.tensor_scalar(rho_rep[:, 0:RC], rho_rep[:, 0:RC],
                           rho8[:, pg:pg + 1], None, ALU.add)
        kb.v.drain()
        kb.t.wait(lw)
        if pg == 0:
            kb.t.wait(lxx)
            kb.t.wait(msem)
            kb.t.wait(ev, ev.n)
        for e, ubuf in ((0, ur), (1, ui)):
            for tc in range(8):
                pb = pu[tc % 2]
                kb.t.wait(ev, max(0, ev.n - 1))
                for g in range(NG):
                    kb.t.matmul(pb[:, :], wine[e][:, g * 128:(g + 1) * 128],
                                xn1_sb[:, g * T + tc * 512:g * T + (tc + 1) * 512],
                                start=(g == 0), stop=False)
                kb.t.matmul(pb[:, :], brow[:, e * 128:(e + 1) * 128],
                            onesrow[:, tc * 512:(tc + 1) * 512],
                            start=False, stop=True, inc=(mm, 1))
                kb.v.wait(mm, mm.n)
                kb.v.tensor_copy(ubuf[:, tc * 512:(tc + 1) * 512], pb[:, :],
                                 inc=(ev, 1))
        for ch in range(T // RC):
            s_ = slice(ch * RC, (ch + 1) * RC)
            kb.v.tensor_tensor(sc1[:, 0:RC], ur[:, s_], cs[:, 0:RC], ALU.mult,
                               inc=(warm, 1))
            kb.t.wait(warm, warm.n)
            for _ in range(12):
                kb.t.matmul(pu[0][0:64, 0:64], wine[0][0:64, 0:64],
                            onesrow[0:1, 0:64] if False else wine[1][0:64, 0:64],
                            start=True, stop=True)
            kb.v.drain()
            kb.v.tensor_tensor(sc1[:, 0:RC], ur[:, s_], cs[:, 0:RC], ALU.mult)
            kb.v.tensor_tensor(sc2[:, 0:RC], ui[:, s_], sn[:, 0:RC], ALU.mult)
            kb.v.tensor_tensor(sc3[:, 0:RC], ui[:, s_], cs[:, 0:RC], ALU.mult)
            kb.v.tensor_tensor(sc4[:, 0:RC], ur[:, s_], sn[:, 0:RC], ALU.mult)
            kb.v.drain()
            kb.v.tensor_tensor(sc1[:, 0:RC], sc1[:, 0:RC], sc2[:, 0:RC],
                               ALU.add)
            kb.v.tensor_tensor(sc3[:, 0:RC], sc3[:, 0:RC], sc4[:, 0:RC],
                               ALU.subtract)
            kb.v.drain()
            kb.v.tensor_scalar(ur[:, s_], sc1[:, 0:RC], ga8[:, pg:pg + 1],
                               None, ALU.mult)
            kb.v.tensor_scalar(ui[:, s_], sc3[:, 0:RC], ga8[:, pg:pg + 1],
                               None, ALU.mult)
            kb.v.drain()
        kb.v.memset(gcr[:, :], 0.0)
        kb.v.memset(gci[:, :], 0.0)
        kb.v.drain()
        for ch in range(T // RC):
            s_ = slice(ch * RC, (ch + 1) * RC)
            kb.v.tensor_tensor_scan(gr[:, s_], rho_rep[:, 0:RC], ur[:, s_],
                                    gcr[:, :], ALU.mult, ALU.add)
            kb.v.tensor_tensor_scan(gi[:, s_], rho_rep[:, 0:RC], ui[:, s_],
                                    gci[:, :], ALU.mult, ALU.add,
                                    inc=(warm, 1))
            kb.t.wait(warm, warm.n)
            for _ in range(12):
                kb.t.matmul(pu[0][0:64, 0:64], wine[0][0:64, 0:64],
                            wine[1][0:64, 0:64], start=True, stop=True)
            kb.v.drain()
            if ch < T // RC - 1:
                e_ = slice((ch + 1) * RC - 1, (ch + 1) * RC)
                kb.v.tensor_tensor(sc1[:, 0:1], gr[:, e_], cs[:, RC:RC + 1],
                                   ALU.mult)
                kb.v.tensor_tensor(sc2[:, 0:1], gi[:, e_], sn[:, RC:RC + 1],
                                   ALU.mult)
                kb.v.tensor_tensor(sc3[:, 0:1], gr[:, e_], sn[:, RC:RC + 1],
                                   ALU.mult)
                kb.v.tensor_tensor(sc4[:, 0:1], gi[:, e_], cs[:, RC:RC + 1],
                                   ALU.mult)
                kb.v.drain()
                kb.v.tensor_tensor(gcr[:, :], sc1[:, 0:1], sc2[:, 0:1],
                                   ALU.subtract)
                kb.v.tensor_tensor(gci[:, :], sc3[:, 0:1], sc4[:, 0:1],
                                   ALU.add)
                kb.v.drain()
        for tc in range(8):
            kb.v.wait(pwh, pwh.n, )
            kb.v.nop(inc=(warm, 1))
            kb.t.wait(warm, warm.n)
            for _ in range(12):
                kb.t.matmul(pu[0][0:64, 0:64], wine[0][0:64, 0:64],
                            wine[1][0:64, 0:64], start=True, stop=True)
            for half in range(1):
                s_ = slice(tc * 512, (tc + 1) * 512)
                o_ = slice(0, 512)
                kb.v.tensor_tensor(sc1[:, o_], gr[:, s_], cs[:, 0:RC], ALU.mult)
                kb.v.tensor_tensor(sc2[:, o_], gi[:, s_], sn[:, 0:RC], ALU.mult)
                kb.v.tensor_tensor(sc3[:, o_], gr[:, s_], sn[:, 0:RC], ALU.mult)
                kb.v.tensor_tensor(sc4[:, o_], gi[:, s_], cs[:, 0:RC], ALU.mult)
            kb.v.drain()
            kb.v.tensor_tensor(sc1[:, 0:512], sc1[:, 0:512], sc2[:, 0:512],
                               ALU.subtract)
            kb.v.tensor_tensor(sc3[:, 0:512], sc3[:, 0:512], sc4[:, 0:512],
                               ALU.add)
            kb.v.drain()
            kb.v.tensor_copy(hb[:, 0:512], sc1[:, 0:512])
            kb.v.tensor_copy(hb[:, 512:1024], sc3[:, 0:512])
            kb.v.drain()
            kb.v.tensor_tensor(hb[:, 1024:1536], sc1[:, 0:512], hb[:, 0:512],
                               ALU.subtract)
            kb.v.tensor_tensor(hb[:, 1536:2048], sc3[:, 0:512], hb[:, 512:1024],
                               ALU.subtract, inc=(ev, 1))
            kb.y.wait(ev, ev.n)
            kb.y.dma_start(out=hpark[:, pg, :, tc * 512:(tc + 1) * 512]
                           .rearrange("e p t -> p e t"),
                           in_=hb[:, :].rearrange("p (e t) -> p e t", t=512),
                           inc=(pwh, 16))
    kb.stage_end([pwh])
    kb.free("E_pu1", "E_pu0", "E_brow", "E_wine1", "E_wine0", "E_hb",
            "E_hb" if False else "E_gci", "E_gcr",
            "E_sc4", "E_sc3", "E_sc2", "E_sc1", "E_rhor", "E_sn", "E_cs",
            "E_gi", "E_gr", "E_ui", "E_ur", "E_xn1", "E_tsc", "E_sth8",
            "E_cth8", "E_ga8", "E_th8", "E_rho8", "E_nu8", "E_pp", "E_plog")

    wm0 = kb.sbuf("E_wm0", [128, 8192], BF16)
    wm1 = kb.sbuf("E_wm1", [128, 8192], BF16)
    wi2 = kb.sbuf("E_wi2", [128, 8192], BF16)
    w1s = kb.sbuf("E_w1s", [128, 8192], BF16)
    w2s = kb.sbuf("E_w2s", [128, 8192], BF16)
    brows = kb.sbuf("E_brows", [1, 4 * 1024], F32)
    bcomb = kb.sbuf("E_bcomb", [1, 1024], BF16)
    w1br = kb.sbuf("E_w1br", [1, 1024], BF16)
    w2br = kb.sbuf("E_w2br", [1, 1024], BF16)
    gcols = kb.sbuf("E_gcols", [128, 8], F32)
    bcols = kb.sbuf("E_bcols", [128, 8], F32)
    hrt = kb.sbuf("E_hrt", [128, 4096], BF16)
    hit = kb.sbuf("E_hit", [128, 4096], BF16)
    hrtl = kb.sbuf("E_hrtl", [128, 4096], BF16)
    hitl = kb.sbuf("E_hitl", [128, 4096], BF16)
    xntc = kb.sbuf("E_xntc", [128, 4096], BF16)
    outb = kb.sbuf("E_outb", [128, 4096], BF16)
    outlo = kb.sbuf("E_outlo", [128, 4096], BF16)
    y1s = kb.sbuf("E_y1s", [128, 4096], BF16)
    ysb = kb.sbuf("E_ysb", [128, 4096], F32)
    ysq = kb.sbuf("E_ysq", [128, 512], F32)
    mrow = kb.sbuf("E_mrow", [1, 1024], F32)
    mub = kb.sbuf("E_mub", [128, 512], F32)
    rsb = kb.sbuf("E_rsb", [128, 512], F32)
    s1t = kb.sbuf("E_s1t", [128, 512], BF16)
    t1 = kb.sbuf("E_t1", [128, 512], F32)
    po2 = [kb.psum(f"E_po{i}", [128, 512], F32) for i in range(2)]
    pst2 = kb.psum("E_pst2", [1, 1024], F32)
    pb2 = kb.psum("E_pb2", [128, 1024], F32)
    lw2 = kb.sem("E_lw2")
    lh = kb.sem("E_lh")
    rvv = kb.sem("E_rvv")
    scc = kb.sem("E_scc")
    psw = kb.sem("E_psw")

    for name, srcw in (("E_wm0", wmid[0]), ("E_wm1", wmid[1]),
                       ("E_wi2", win[2]), ("E_w1s", w1), ("E_w2s", w2)):
        dst = {"E_wm0": wm0, "E_wm1": wm1, "E_wi2": wi2, "E_w1s": w1s,
               "E_w2s": w2s}[name]
        kb.g.dma_start(out=dst[:, :].rearrange("p (g h) -> p g h", h=1024),
                       in_=srcw.rearrange("(g p) h -> p g h", p=128),
                       inc=(lw2, 16))
    kb.y.dma_start(out=brows[:, 0:1024], in_=bmid[0:1, :], inc=(lh, 16))
    kb.y.dma_start(out=brows[:, 1024:2048], in_=bmid[1:2, :], inc=(lh, 16))
    kb.y.dma_start(out=brows[:, 2048:3072], in_=bin_[2:3, :], inc=(lh, 16))
    kb.g.dma_start(out=w1br[:, :], in_=w1b, inc=(lw2, 16))
    kb.g.dma_start(out=w2br[:, :], in_=w2b, inc=(lw2, 16))
    kb.g.dma_start(out=gcols[:, :],
                   in_=bass.AP(lng.tensor, 0, [[1, 128], [128, 8]]),
                   allow_slow_non_contiguous=True, inc=(lw2, 16))
    kb.g.dma_start(out=bcols[:, :],
                   in_=bass.AP(lnb.tensor, 0, [[1, 128], [128, 8]]),
                   allow_slow_non_contiguous=True, inc=(lw2, 16))
    kb.v.wait(lw2, lw2.n)
    kb.v.tensor_scalar(wm1[:, :], wm1[:, :], -1.0, None, ALU.mult, inc=(ev, 1))
    kb.v.wait(lh, lh.n)
    kb.v.tensor_tensor(brows[:, 3072:4096], brows[:, 0:1024],
                       brows[:, 1024:2048], ALU.subtract)
    kb.v.drain()
    kb.v.tensor_tensor(brows[:, 3072:4096], brows[:, 3072:4096],
                       brows[:, 2048:3072], ALU.add)
    kb.v.drain()
    kb.v.tensor_copy(bcomb[:, :], brows[:, 3072:4096], inc=(ev, 1))

    for tc in range(8):
        ts_ = slice(tc * 512, (tc + 1) * 512)
        kb.y.wait(mm, mm.n)
        kb.y.dma_start(out=hrt[:, :].rearrange("p (g t) -> p g t", t=512),
                       in_=hpark[0, :, :, ts_].rearrange("g p t -> p g t"),
                       inc=(lh, 16))
        kb.y.dma_start(out=hit[:, :].rearrange("p (g t) -> p g t", t=512),
                       in_=hpark[1, :, :, ts_].rearrange("g p t -> p g t"),
                       inc=(lh, 16))
        kb.y.dma_start(out=hrtl[:, :].rearrange("p (g t) -> p g t", t=512),
                       in_=hpark[2, :, :, ts_].rearrange("g p t -> p g t"),
                       inc=(lh, 16))
        kb.y.dma_start(out=hitl[:, :].rearrange("p (g t) -> p g t", t=512),
                       in_=hpark[3, :, :, ts_].rearrange("g p t -> p g t"),
                       inc=(lh, 16))
        kb.y.dma_start(out=xntc[:, :].rearrange("p (g t) -> p g t", t=512),
                       in_=xn1[:, :, ts_].rearrange("g p t -> p g t"),
                       inc=(lh, 16))
        kb.t.wait(lh, lh.n)
        kb.t.wait(ev, ev.n)
        for hm in range(NG):
            pb = po2[hm % 2]
            kb.t.wait(rvv, max(0, rvv.n - 1))
            for g in range(NG):
                kb.t.matmul(pb[:, :], wm0[:, g * 1024 + hm * 128:g * 1024 + (hm + 1) * 128],
                            hrt[:, g * 512:(g + 1) * 512],
                            start=(g == 0), stop=False)
            for g in range(NG):
                kb.t.matmul(pb[:, :], wm0[:, g * 1024 + hm * 128:g * 1024 + (hm + 1) * 128],
                            hrtl[:, g * 512:(g + 1) * 512],
                            start=False, stop=False)
            for g in range(NG):
                kb.t.matmul(pb[:, :], wm1[:, g * 1024 + hm * 128:g * 1024 + (hm + 1) * 128],
                            hit[:, g * 512:(g + 1) * 512],
                            start=False, stop=False)
            for g in range(NG):
                kb.t.matmul(pb[:, :], wm1[:, g * 1024 + hm * 128:g * 1024 + (hm + 1) * 128],
                            hitl[:, g * 512:(g + 1) * 512],
                            start=False, stop=False)
            for g in range(NG):
                kb.t.matmul(pb[:, :], wi2[:, g * 1024 + hm * 128:g * 1024 + (hm + 1) * 128],
                            xntc[:, g * 512:(g + 1) * 512],
                            start=False, stop=False)
            kb.t.matmul(pb[:, :], bcomb[:, hm * 128:(hm + 1) * 128],
                        onesrow[:, 0:512], start=False, stop=True, inc=(mm, 1))
            kb.v.wait(mm, mm.n)
            kb.v.tensor_copy(outb[:, hm * 512:(hm + 1) * 512], pb[:, :])
            kb.v.drain()
            kb.v.tensor_tensor(outlo[:, hm * 512:(hm + 1) * 512], pb[:, :],
                               outb[:, hm * 512:(hm + 1) * 512], ALU.subtract,
                               inc=(rvv, 1))
        for jsl in range(NG):
            pb = po2[jsl % 2]
            kb.t.wait(rvv, rvv.n)
            kb.t.wait(scc, max(0, scc.n - 1))
            for g in range(NG):
                kb.t.matmul(pb[:, :], w1s[:, g * 1024 + jsl * 128:g * 1024 + (jsl + 1) * 128],
                            outb[:, g * 512:(g + 1) * 512],
                            start=(g == 0), stop=False)
            kb.t.matmul(pb[:, :], w1br[:, jsl * 128:(jsl + 1) * 128],
                        onesrow[:, 0:512], start=False, stop=True, inc=(mm, 1))
            kb.s.wait(mm, mm.n)
            kb.s.activation(y1s[:, jsl * 512:(jsl + 1) * 512], pb[:, :],
                            AF.Gelu, inc=(scc, 1))
        for ds in range(NG):
            pb = po2[ds % 2]
            kb.t.wait(scc, scc.n)
            kb.t.wait(rvv, max(0, rvv.n - 1))
            for g in range(NG):
                kb.t.matmul(pb[:, :], w2s[:, g * 1024 + ds * 128:g * 1024 + (ds + 1) * 128],
                            y1s[:, g * 512:(g + 1) * 512],
                            start=(g == 0), stop=False)
            kb.t.matmul(pb[:, :], w2br[:, ds * 128:(ds + 1) * 128],
                        onesrow[:, 0:512], start=False, stop=True, inc=(mm, 1))
            kb.v.wait(mm, mm.n)
            kb.v.tensor_tensor(ysb[:, ds * 512:(ds + 1) * 512], pb[:, :],
                               outb[:, ds * 512:(ds + 1) * 512], ALU.add)
            kb.v.drain()
            kb.v.tensor_tensor(ysb[:, ds * 512:(ds + 1) * 512],
                               ysb[:, ds * 512:(ds + 1) * 512],
                               outlo[:, ds * 512:(ds + 1) * 512], ALU.add,
                               inc=(rvv, 1))
            kb.s.wait(rvv, rvv.n)
            kb.s.activation(ysq[:, :], ysb[:, ds * 512:(ds + 1) * 512],
                            AF.Square, inc=(scc, 1))
            kb.t.wait(rvv, rvv.n)
            kb.t.matmul(pst2[0:1, 0:512], onesf[:, :],
                        ysb[:, ds * 512:(ds + 1) * 512],
                        start=(ds == 0), stop=(ds == NG - 1))
            kb.t.wait(scc, scc.n)
            kb.t.matmul(pst2[0:1, 512:1024], onesf[:, :], ysq[:, :],
                        start=(ds == 0), stop=(ds == NG - 1), inc=(mm, 1))
        kb.v.wait(mm, mm.n)
        kb.v.tensor_scalar(mrow[:, :], pst2[0:1, :], 1.0 / D, None, ALU.mult,
                           inc=(rvv, 1))
        kb.v.drain()
        kb.v.tensor_tensor(ysq[0:1, 0:512], mrow[:, 0:512], mrow[:, 0:512],
                           ALU.mult)
        kb.v.drain()
        kb.v.tensor_tensor(mrow[:, 512:1024], mrow[:, 512:1024],
                           ysq[0:1, 0:512], ALU.subtract, inc=(rvv, 1))
        kb.s.wait(rvv, rvv.n)
        kb.s.activation(ysq[0:1, 0:512], mrow[:, 512:1024], AF.Sqrt,
                        bias=epscol[0:1, :], inc=(scc, 1))
        kb.v.wait(scc, scc.n)
        kb.v.reciprocal(mrow[:, 512:1024], ysq[0:1, 0:512], inc=(rvv, 1))
        kb.t.wait(rvv, rvv.n)
        kb.t.matmul(pb2[:, 0:512], ones1f[:, :], mrow[:, 0:512], start=True,
                    stop=True)
        kb.t.matmul(pb2[:, 512:1024], ones1f[:, :], mrow[:, 512:1024],
                    start=True, stop=True, inc=(mm, 1))
        kb.v.wait(mm, mm.n)
        kb.v.tensor_copy(mub[:, :], pb2[:, 0:512])
        kb.v.tensor_copy(rsb[:, :], pb2[:, 512:1024])
        kb.v.drain()
        for ds in range(NG):
            kb.v.wait(psw, psw.n)
            kb.v.tensor_tensor(t1[:, :], ysb[:, ds * 512:(ds + 1) * 512],
                               mub[:, :], ALU.subtract)
            kb.v.drain()
            kb.v.tensor_tensor(t1[:, :], t1[:, :], rsb[:, :], ALU.mult)
            kb.v.drain()
            kb.v.tensor_scalar(s1t[:, :], t1[:, :], gcols[:, ds:ds + 1],
                               bcols[:, ds:ds + 1], ALU.mult, ALU.add,
                               inc=(rvv, 1))
            kb.y.wait(rvv, rvv.n)
            kb.y.dma_start(out=s1[ds, :, ts_], in_=s1t[:, :], inc=(psw, 16))
        kb.t.wait(rvv, rvv.n)

    kb.stage_end([psw])
    kb.free_all_bufs()



GELU_C = float(np.sqrt(2.0 / np.pi))
GELU_S = float(np.sqrt(0.044715))


def stage_F(kb, Tn):
    xn1, x1_fm, s1 = Tn["xn1"], Tn["x1_fm"], Tn["s1"]
    wfcs, bfcs = Tn["m_wfcs"], Tn["m_bfcs"]
    wprojs, bprojs = Tn["m_wprojs"], Tn["m_bprojs"]
    wr = Tn["m_wr"]
    wfc, bfc = Tn["m_wfc"], Tn["m_bfc"]
    wproj, bproj = Tn["m_wproj"], Tn["m_bproj"]
    out = Tn["out"]
    nc = kb.nc
    gate_d = nc.dram_tensor("gate_d", [8, T], F32, kind="Internal").ap()

    identf, sidf = build_identity(kb, "F_idf", F32)
    ones1f = kb.sbuf("F_o1f", [1, 128], F32)
    ones1b = kb.sbuf("F_o1b", [1, 128], BF16)
    onesrow = kb.sbuf("F_orow", [1, 512], BF16)
    msem = kb.sem("F_ms")
    kb.v.memset(ones1f[:, :], 1.0)
    kb.v.memset(ones1b[:, :], 1.0)
    kb.v.memset(onesrow[:, :], 1.0, inc=(msem, 1))

    wr_sb = kb.sbuf("F_wr", [128, 64], BF16)
    gate_fm = kb.sbuf("F_gfm", [8, T], F32)
    xg = [kb.sbuf(f"F_xg{i}", [128, 1024], BF16) for i in range(2)]
    gt = kb.sbuf("F_gt", [128, 24], F32)
    gtm = kb.sbuf("F_gtm", [128, 8], F32)
    pl = kb.psum("F_pl", [128, 512], F32)
    pgt = kb.psum("F_pgt", [8, 512], F32)
    lg = [kb.sem(f"F_lg{i}") for i in range(2)]
    mm = kb.sem("F_mm")
    ev = kb.sem("F_ev")
    sc = kb.sem("F_sc")

    kb.g.dma_start(out=wr_sb[:, :].rearrange("p (g e) -> p g e", e=8),
                   in_=wr.rearrange("(g p) e -> p g e", p=128), inc=(lg[0], 16))
    for tt in range(32):
        b = tt % 2
        kb.y.wait(mm, mm.n)
        kb.y.dma_start(out=xg[b][:, :].rearrange("p (g t) -> p g t", t=128),
                       in_=xn1[:, :, tt * 128:(tt + 1) * 128].rearrange(
                           "g p t -> p g t"), inc=(lg[b], 16))
        kb.t.wait(lg[b], lg[b].n)
        if tt == 0:
            kb.t.wait(lg[0], lg[0].n)
        kb.t.wait(ev, ev.n)
        for g in range(NG):
            kb.t.matmul(pl[0:128, 0:8], xg[b][:, g * 128:(g + 1) * 128],
                        wr_sb[:, g * 8:(g + 1) * 8],
                        start=(g == 0), stop=(g == NG - 1), inc=(mm, 1))
        kb.v.wait(mm, mm.n)
        kb.v.tensor_copy(gt[:, 0:8], pl[0:128, 0:8])
        kb.v.drain()
        kb.v.reduce_max(gt[:, 8:9], gt[:, 0:8], axis=AX.X)
        kb.v.drain()
        kb.v.tensor_scalar(gt[:, 9:10], gt[:, 8:9], -1.0, None, ALU.mult)
        kb.v.tensor_scalar(gt[:, 16:24], gt[:, 0:8], gt[:, 8:9], None,
                           ALU.is_ge)
        kb.v.drain()
        kb.v.scalar_tensor_tensor(gt[:, 16:24], gt[:, 16:24], -1e30,
                                  gt[:, 0:8], ALU.mult, ALU.add)
        kb.v.drain()
        kb.v.reduce_max(gt[:, 10:11], gt[:, 16:24], axis=AX.X, inc=(ev, 1))
        kb.s.wait(ev, ev.n)
        kb.s.activation(gt[:, 16:24], gt[:, 0:8], AF.Exp, bias=gt[:, 9:10],
                        inc=(sc, 1))
        kb.v.wait(sc, sc.n)
        kb.v.tensor_scalar(gt[:, 0:8], gt[:, 0:8], gt[:, 10:11], None,
                           ALU.is_ge)
        kb.v.drain()
        kb.v.tensor_tensor(gtm[:, :], gt[:, 16:24], gt[:, 0:8], ALU.mult)
        kb.v.drain()
        kb.v.reduce_sum(gt[:, 11:12], gtm[:, :], axis=AX.X)
        kb.v.drain()
        kb.v.reciprocal(gt[:, 12:13], gt[:, 11:12])
        kb.v.drain()
        kb.v.tensor_scalar(gtm[:, :], gtm[:, :], gt[:, 12:13], None, ALU.mult,
                           inc=(ev, 1))
        kb.t.wait(ev, ev.n)
        kb.t.transpose(pgt[0:8, 0:128], gtm[:, :], identf[:, :], inc=(mm, 1))
        kb.v.wait(mm, mm.n)
        kb.v.tensor_copy(gate_fm[:, tt * 128:(tt + 1) * 128], pgt[0:8, 0:128],
                         inc=(ev, 1))
        kb.t.wait(ev, ev.n)
    kb.y.wait(ev, ev.n)
    gw = kb.sem("F_gw")
    kb.y.dma_start(out=gate_d, in_=gate_fm[:, :], inc=(gw, 16))
    kb.stage_end([gw])
    kb.free("F_pgt", "F_pl", "F_gtm", "F_gt", "F_xg1", "F_xg0", "F_gfm")

    TH = 2048
    routed = kb.sbuf("F_rout", [128, NG * TH], F32)
    xnth = kb.sbuf("F_xnth", [128, NG * TH], BF16)
    wfcq = [kb.sbuf(f"F_wfcq{i}", [128, 8192], BF16) for i in range(2)]
    wprojq = [kb.sbuf(f"F_wprojq{i}", [128, 8192], BF16) for i in range(2)]
    hq = [kb.sbuf(f"F_hq{i}", [128, 8 * 512], BF16) for i in range(2)]
    gb = kb.sbuf("F_gb", [128, TH], BF16)
    grow = kb.sbuf("F_grow", [1, TH], BF16)
    bfcr = [kb.sbuf(f"F_bfcr{i}", [1, 1024], BF16) for i in range(2)]
    bprojr0 = kb.sbuf("F_bprojr", [1, 1024], BF16)
    bprojr = [bprojr0, bprojr0]
    asb = [kb.sbuf(f"F_asb{i}", [128, 512], F32) for i in range(2)]
    bsb = [kb.sbuf(f"F_bsb{i}", [128, 512], F32) for i in range(2)]
    usb = [kb.sbuf(f"F_usb{i}", [128, 512], F32) for i in range(2)]
    g1t = kb.sbuf("F_g1t", [128, 512], BF16)
    s1t = kb.sbuf("F_s1t", [128, 512], BF16)
    x1t = asb[0]
    x2t = bsb[0]
    ost = usb[0]
    ph = [kb.psum(f"F_ph{i}", [128, 512], F32) for i in range(4)]
    pp = [kb.psum(f"F_pp{i}", [128, 512], F32) for i in range(2)]
    pgb = kb.psum("F_pgb", [128, 512], F32)
    ptr = pgb
    lw = [kb.sem(f"F_lw{i}") for i in range(2)]
    vch = kb.sem("F_vch")
    lx = kb.sem("F_lx")
    lxs = kb.sem("F_lxs")
    rvv = kb.sem("F_rvv")
    scc = kb.sem("F_scc")
    pow_ = kb.sem("F_pow")

    iters = [(th, e, jq) for th in range(2) for e in range(9)
             for jq in range(4)]
    itr = kb.sem("F_itr")

    def issue_load(k, part):
        th, e, jq = iters[k]
        b = k % 2
        if part == 0:
            kb.g.wait(itr, max(0, k - 1))
            wsrc = wfcs if e == 0 else wfc[e - 1]
            kb.g.dma_start(
                out=wfcq[b][:, :].rearrange("p (g j) -> p g j", j=1024),
                in_=wsrc[:, jq * 1024:(jq + 1) * 1024].rearrange(
                    "(g p) j -> p g j", p=128), inc=(lw[b], 16))
        elif part == 1:
            psrc = wprojs if e == 0 else wproj[e - 1]
            kb.g.dma_start(
                out=wprojq[b][:, :].rearrange("p (c d) -> p c d", d=1024),
                in_=psrc[jq * 1024:(jq + 1) * 1024, :].rearrange(
                    "(c p) d -> p c d", p=128), inc=(lw[b], 16))
        else:
            bsrc = bfcs if e == 0 else bfc[e - 1:e]
            kb.g.dma_start(out=bfcr[b][:, :],
                           in_=bsrc[0:1, jq * 1024:(jq + 1) * 1024],
                           inc=(lw[b], 16))
            if jq == 0:
                bp = bprojs if e == 0 else bproj[e - 1:e]
                kb.g.dma_start(out=bprojr0[:, :], in_=bp[0:1, :],
                               inc=(lw[b], 16))

    for part in range(3):
        issue_load(0, part)
    lw_hist = [lw[0].n]
    for k, (th, e, jq) in enumerate(iters):
        b = k % 2
        t0 = th * TH
        if jq == 0:
            if e == 0:
                kb.y.wait(mm, mm.n)
                kb.y.dma_start(
                    out=xnth[:, :].rearrange("p (g t) -> p g t", t=TH),
                    in_=xn1[:, :, t0:t0 + TH].rearrange("g p t -> p g t"),
                    inc=(lx, 16))
                kb.v.wait(scc, scc.n)
                kb.v.memset(gb[:, :], 0.5, inc=(ev, 1))
            else:
                kb.g.wait(mm, mm.n)
                kb.g.dma_start(out=grow[:, :],
                               in_=gate_d[e - 1:e, t0:t0 + TH], inc=(ev, 16))
                for q in range(4):
                    kb.t.wait(ev, ev.n)
                    kb.t.wait(rvv, rvv.n)
                    kb.t.matmul(pgb[:, :], ones1b[:, :],
                                grow[:, q * 512:(q + 1) * 512], start=True,
                                stop=True, inc=(mm, 1))
                    kb.v.wait(mm, mm.n)
                    kb.v.wait(scc, scc.n)
                    kb.v.tensor_scalar(gb[:, q * 512:(q + 1) * 512], pgb[:, :],
                                       0.5, None, ALU.mult, inc=(rvv, 1))
        fc_scc = {}

        def fc_block(tc4):
            c0 = tc4 * 512
            hb_ = hq[tc4 % 2]
            kb.t.wait(lw[b], lw_hist[k])
            if e == 0 and jq == 0 and tc4 == 0:
                kb.t.wait(lx, lx.n)
                kb.t.wait(msem)
                kb.v.wait(pow_, pow_.n)
            for jsl in range(NG):
                i2 = (tc4 * NG + jsl) % 2
                pb = ph[(tc4 * NG + jsl) % 4]
                kb.t.wait(scc, max(0, scc.n - 3))
                for g in range(NG):
                    kb.t.matmul(
                        pb[:, :],
                        wfcq[b][:, g * 1024 + jsl * 128:g * 1024 + (jsl + 1) * 128],
                        xnth[:, g * TH + c0:g * TH + c0 + 512],
                        start=(g == 0), stop=False)
                kb.t.matmul(pb[:, :], bfcr[b][:, jsl * 128:(jsl + 1) * 128],
                            onesrow[:, 0:512], start=False, stop=True,
                            inc=(mm, 1))
                kb.s.wait(mm, mm.n)
                kb.s.wait(vch, max(0, vch.n - 2))
                kb.s.activation(asb[i2][:, :], pb[:, :], AF.Square,
                                scale=GELU_S, inc=(sc, 1))
                kb.v.wait(sc, sc.n)
                kb.v.scalar_tensor_tensor(bsb[i2][:, :], asb[i2][:, :], 1.0,
                                          pb[:, :], ALU.add, ALU.mult,
                                          inc=(vch, 1))
                kb.s.wait(vch, vch.n)
                kb.s.activation(asb[i2][:, :], bsb[i2][:, :], AF.Tanh,
                                scale=GELU_C, inc=(sc, 1))
                kb.v.wait(sc, sc.n)
                kb.v.wait(scc, max(0, scc.n - 1))
                kb.v.scalar_tensor_tensor(usb[i2][:, :], asb[i2][:, :], 1.0,
                                          pb[:, :], ALU.add, ALU.mult,
                                          inc=(vch, 1))
                kb.g.wait(vch, vch.n)
                kb.g.wait(mm, mm.n)
                kb.g.tensor_tensor(hb_[:, jsl * 512:(jsl + 1) * 512],
                                   usb[i2][:, :], gb[:, c0:c0 + 512], ALU.mult,
                                   inc=(scc, 1))
            fc_scc[tc4] = scc.n

        def proj_block(tc4):
            c0 = tc4 * 512
            hb_ = hq[tc4 % 2]
            for ds in range(NG):
                pb = pp[ds % 2]
                kb.t.wait(scc, fc_scc[tc4])
                kb.t.wait(rvv, max(0, rvv.n - 1))
                for jc in range(NG):
                    last = (jq != 0 and jc == NG - 1)
                    kb.t.matmul(
                        pb[:, :],
                        wprojq[b][:, jc * 1024 + ds * 128:jc * 1024 + (ds + 1) * 128],
                        hb_[:, jc * 512:(jc + 1) * 512],
                        start=(jc == 0), stop=last,
                        inc=(mm, 1) if last else None)
                if jq == 0:
                    rrhs = onesrow[:, 0:512] if e == 0 else \
                        grow[:, c0:c0 + 512]
                    kb.t.matmul(pb[:, :],
                                bprojr[e % 2][:, ds * 128:(ds + 1) * 128],
                                rrhs, start=False, stop=True, inc=(mm, 1))
                if tc4 == 3 and ds == NG - 1:
                    kb.t.nop(inc=(itr, 1))
                rsl = routed[:, ds * TH + c0:ds * TH + c0 + 512]
                kb.v.wait(mm, mm.n)
                if e == 0 and jq == 0:
                    kb.v.tensor_copy(rsl, pb[:, :], inc=(rvv, 1))
                else:
                    kb.v.tensor_tensor(rsl, rsl, pb[:, :], ALU.add,
                                       inc=(rvv, 1))

        fc_block(0)
        if k + 1 < len(iters):
            issue_load(k + 1, 0)
        fc_block(1)
        proj_block(0)
        if k + 1 < len(iters):
            issue_load(k + 1, 1)
        fc_block(2)
        proj_block(1)
        if k + 1 < len(iters):
            issue_load(k + 1, 2)
            lw_hist.append(lw[(k + 1) % 2].n)
        fc_block(3)
        proj_block(2)
        proj_block(3)
        if jq == 3 and e == 8:
            kb.t.wait(sidf)
            for ds in range(NG):
                for tc4 in range(4):
                    c0 = tc4 * 512
                    kb.y.wait(rvv, rvv.n)
                    kb.y.wait(scc, scc.n)
                    kb.y.wait(vch, vch.n)
                    kb.y.dma_start(out=x1t[:, :],
                                   in_=x1_fm[ds, :, t0 + c0:t0 + c0 + 512],
                                   inc=(lxs, 16))
                    kb.y.dma_start(out=s1t[:, :],
                                   in_=s1[ds, :, t0 + c0:t0 + c0 + 512],
                                   inc=(lxs, 16))
                    kb.s.wait(rvv, rvv.n)
                    kb.s.wait(scc, scc.n)
                    kb.s.activation(g1t[:, :],
                                    routed[:, ds * TH + c0:ds * TH + c0 + 512],
                                    AF.Silu, inc=(sc, 1))
                    kb.v.wait(sc, sc.n)
                    kb.v.wait(lxs, lxs.n)
                    kb.v.tensor_tensor(x2t[:, :], s1t[:, :], g1t[:, :],
                                       ALU.mult)
                    kb.v.drain()
                    kb.v.tensor_tensor(x2t[:, :], x2t[:, :], x1t[:, :],
                                       ALU.add, inc=(rvv, 1))
                    for q in range(4):
                        kb.t.wait(rvv, rvv.n)
                        kb.t.wait(ev, max(0, ev.n))
                        kb.t.transpose(ptr[:, 0:128],
                                       x2t[:, q * 128:(q + 1) * 128],
                                       identf[:, :], inc=(mm, 1))
                        kb.v.wait(mm, mm.n)
                        kb.v.tensor_copy(ost[:, q * 128:(q + 1) * 128],
                                         ptr[:, 0:128], inc=(ev, 1))
                    kb.y.wait(ev, ev.n)
                    src_o = ost[:, :].rearrange("p (q j) -> p q j", j=128)
                    dst_o = bass.AP(out.tensor, (t0 + c0) * D + ds * 128,
                                    [[D, 128], [128 * D, 4], [1, 128]])
                    kb.y.dma_start(out=dst_o, in_=src_o, inc=(pow_, 16))

    kb.stage_end([pow_])
    kb.free_all_bufs()



def core_input_arrays(inputs, core):
    A = {k: np.asarray(v) for k, v in inputs.items()}
    d = {}
    d["x"] = np.ascontiguousarray(A["L0_attn_x"][core])
    d["wq_t"] = np.ascontiguousarray(A["L0_attn_Wq"].T)
    d["wk_t"] = np.ascontiguousarray(A["L0_attn_Wk"].T)
    d["wv_t"] = np.ascontiguousarray(A["L0_attn_Wv"].T)
    d["rbt"] = np.ascontiguousarray(A["L0_attn_rel_bias_table"])
    d["l0_lng"] = A["L0_attn_ln_gamma"].reshape(1, D)
    d["l0_lnb"] = A["L0_attn_ln_beta"].reshape(1, D)
    d["fb_t"] = np.ascontiguousarray(A["L0_ffn_fast_b"].T)
    d["wfc"] = np.ascontiguousarray(A["L0_ffn_w_fc"])
    d["wproj"] = np.ascontiguousarray(A["L0_ffn_w_proj"])
    d["u_t"] = np.ascontiguousarray(A["L0_ffn_u"].T)
    d["v_t"] = np.ascontiguousarray(A["L0_ffn_v"].T)
    d["l0_gate"] = A["L0_ffn_gate"].reshape(1, 1)
    d["l1_win"] = np.ascontiguousarray(A["L1_attn_W_in"])
    d["l1_bin"] = np.ascontiguousarray(A["L1_attn_b_in"])
    d["l1_plog"] = np.ascontiguousarray(A["L1_attn_params_log"])
    d["l1_wmid"] = np.ascontiguousarray(A["L1_attn_W_mid"])
    d["l1_bmid"] = np.ascontiguousarray(A["L1_attn_b_mid"])
    d["l1_w1"] = np.ascontiguousarray(A["L1_attn_w1_w"])
    d["l1_w1b"] = A["L1_attn_w1_b"].reshape(1, D)
    d["l1_w2"] = np.ascontiguousarray(A["L1_attn_w2_w"])
    d["l1_w2b"] = A["L1_attn_w2_b"].reshape(1, D)
    d["l1_lng"] = A["L1_attn_ln_g"].reshape(1, D)
    d["l1_lnb"] = A["L1_attn_ln_b"].reshape(1, D)
    d["m_wfcs"] = np.ascontiguousarray(A["L1_ffn_Wfc_s"])
    d["m_bfcs"] = A["L1_ffn_bfc_s"].reshape(1, 4096)
    d["m_wprojs"] = np.ascontiguousarray(A["L1_ffn_Wproj_s"])
    d["m_bprojs"] = A["L1_ffn_bproj_s"].reshape(1, D)
    d["m_wr"] = np.ascontiguousarray(A["L1_ffn_Wr"])
    d["m_wfc"] = np.ascontiguousarray(A["L1_ffn_Wfc"])
    d["m_bfc"] = np.ascontiguousarray(A["L1_ffn_bfc"])
    d["m_wproj"] = np.ascontiguousarray(A["L1_ffn_Wproj"])
    d["m_bproj"] = np.ascontiguousarray(A["L1_ffn_bproj"])
    return d


PARK_SPECS = {
    "x_fm": ([NG, 128, T], F32),
    "xn0": ([NG, 128, T], BF16),
    "s0": ([NG, 128, T], BF16),
    "s0stats": ([2, T], F32),
    "x1_fm": ([NG, 128, T], F32),
    "x1stats": ([2, T], F32),
    "xn1": ([NG, 128, T], BF16),
    "s1": ([NG, 128, T], BF16),
}

INPUT_SPECS = {
    "x": ([T, D], F32),
    "wq_t": ([D, D], F32), "wk_t": ([D, D], F32), "wv_t": ([D, D], F32),
    "rbt": ([4095, 16], F32),
    "l0_lng": ([1, D], F32), "l0_lnb": ([1, D], F32),
    "fb_t": ([16, 16], F32), "wfc": ([4096, D], F32), "wproj": ([D, 4096], F32),
    "u_t": ([16, 4096], F32), "v_t": ([D, 16], F32), "l0_gate": ([1, 1], F32),
    "l1_win": ([3, D, D], F32), "l1_bin": ([3, D], F32), "l1_plog": ([3, D], F32),
    "l1_wmid": ([2, D, D], F32), "l1_bmid": ([2, D], F32),
    "l1_w1": ([D, D], F32), "l1_w1b": ([1, D], F32),
    "l1_w2": ([D, D], F32), "l1_w2b": ([1, D], F32),
    "l1_lng": ([1, D], F32), "l1_lnb": ([1, D], F32),
    "m_wfcs": ([D, 4096], F32), "m_bfcs": ([1, 4096], F32),
    "m_wprojs": ([4096, D], F32), "m_bprojs": ([1, D], F32),
    "m_wr": ([D, 8], F32),
    "m_wfc": ([8, D, 4096], F32), "m_bfc": ([8, 4096], F32),
    "m_wproj": ([8, 4096, D], F32), "m_bproj": ([8, D], F32),
}


def declare(nc, names, kinds):
    out = {}
    specs = dict(INPUT_SPECS)
    specs.update(PARK_SPECS)
    specs["out"] = ([T, D], F32)
    for nm in names:
        shape, dt = specs[nm]
        kind = {"in": "ExternalInput", "out": "ExternalOutput",
                "internal": "Internal"}[kinds[nm]]
        out[nm] = nc.dram_tensor(nm, shape, dt, kind=kind).ap()
    return out


STAGE_IO = [
    (stage_A, ["x"], ["x_fm", "xn0"]),
    (stage_B, ["xn0", "wq_t", "wk_t", "wv_t", "rbt"], ["s0"]),
    (stage_C, ["xn0", "s0", "x_fm", "wfc", "wproj", "u_t", "v_t", "fb_t",
               "l0_gate", "l0_lng", "l0_lnb"], ["x1_fm", "x1stats", "xn1"]),
    (stage_E, ["xn1", "l1_win", "l1_bin", "l1_plog", "l1_wmid", "l1_bmid",
               "l1_w1", "l1_w1b", "l1_w2", "l1_w2b", "l1_lng", "l1_lnb"],
     ["s1"]),
    (stage_F, ["xn1", "x1_fm", "s1", "m_wfcs", "m_bfcs", "m_wprojs",
               "m_bprojs", "m_wr", "m_wfc", "m_bfc", "m_wproj", "m_bproj"],
     ["out"]),
]


FUSED = True


def kernel(**inputs):
    per_core = [core_input_arrays(inputs, c) for c in range(NCORES)]
    if not FUSED:
        acc = [dict() for _ in range(NCORES)]
        for fn, ins, outs in STAGE_IO:
            nc = bass.Bass("TRN2", target_bir_lowering=False)
            kb = KB(nc)
            kinds = {n: "in" for n in ins}
            for n in outs:
                kinds[n] = "out"
            Tn = declare(nc, ins + outs, kinds)
            fn(kb, Tn)
            kb.emit()
            kb.close()
            maps = [{n: (acc[c][n] if n in acc[c] else per_core[c][n])
                     for n in ins} for c in range(NCORES)]
            res = run_bass_kernel_spmd(nc, maps, core_ids=list(range(NCORES)))
            for c in range(NCORES):
                for n in outs:
                    acc[c][n] = np.asarray(res.results[c][n])
        return np.stack([acc[c]["out"] for c in range(NCORES)])

    in_names = list(INPUT_SPECS)
    park_names = list(PARK_SPECS)
    nc = bass.Bass("TRN2", target_bir_lowering=False)
    kb = KB(nc)
    kinds = {n: "in" for n in in_names}
    kinds.update({n: "internal" for n in park_names})
    kinds["out"] = "out"
    Tn = declare(nc, in_names + park_names + ["out"], kinds)
    for fn, ins, outs in STAGE_IO:
        fn(kb, Tn)
    kb.emit()
    kb.close()
    maps = [{n: per_core[c][n] for n in in_names} for c in range(NCORES)]
    res = run_bass_kernel_spmd(nc, maps, core_ids=list(range(NCORES)))
    return np.stack([np.asarray(res.results[c]["out"])
                     for c in range(NCORES)])


# version 16
# speedup vs baseline: 1.0043x; 1.0043x over previous
import numpy as np
import concourse.bass as bass
import concourse.mybir as mybir
from concourse.bass_utils import run_bass_kernel_spmd

F32 = mybir.dt.float32
BF16 = mybir.dt.bfloat16
AF = mybir.ActivationFunctionType
ALU = mybir.AluOpType
AX = mybir.AxisListType

NCORES = 8
T = 4096
D = 1024
NG = D // 128
MAGIC = 1.5 * 2.0 ** 23

ENGS = ("sync", "tensor", "vector", "scalar", "gpsimd")
A_TILES = 32
B_PAIRS = 8
B_NT1B = 16
DBG_PARKS = False


class Sem:
    __slots__ = ("h", "n", "name")

    def __init__(self, h, name):
        self.h = h
        self.n = 0
        self.name = name


class _Rec:

    def __init__(self, kb, eng):
        self._kb = kb
        self._eng = eng

    def wait(self, sem, val=None):
        v = sem.n if val is None else val
        self._kb.ops[self._eng].append(("__wait__", (sem.h, v), {}, ()))

    def __getattr__(self, meth):
        kb = self._kb
        eng = self._eng

        def rec(*a, inc=None, **kw):
            incs = []
            if inc is not None:
                if isinstance(inc, tuple) and len(inc) == 2 and isinstance(inc[0], Sem):
                    inc = [inc]
                for s, k in inc:
                    s.n += k
                    incs.append((s.h, k))
            kb.ops[eng].append((meth, a, kw, tuple(incs)))

        return rec


class KB:

    def __init__(self, nc):
        self.nc = nc
        self.ops = {e: [] for e in ENGS}
        self._sems = {}
        self._guards = {}
        self._ctx = []
        self.y = _Rec(self, "sync")
        self.t = _Rec(self, "tensor")
        self.v = _Rec(self, "vector")
        self.s = _Rec(self, "scalar")
        self.g = _Rec(self, "gpsimd")
        self.done = {e: self.sem("done_" + e) for e in ENGS}
        self.PW = self.sem("PW")
        self.CONST = self.sem("CONST")

    def constcol(self, name, value, dtype=F32):
        h = self.sbuf(name, [128, 1], dtype)
        self.g.memset(h[:, :], float(value), inc=(self.CONST, 1))
        return h

    def sem(self, name):
        if "_" in name:
            name = name.split("_", 1)[1]
        if name not in self._sems:
            cm = self.nc.semaphore(name)
            h = cm.__enter__()
            self._ctx.append(cm)
            self._sems[name] = Sem(h, name)
        return self._sems[name]

    def sbuf(self, name, shape, dtype):
        g = self.nc.sbuf_tensor(name, shape, dtype)
        h = g.__enter__()
        self._guards[name] = g
        return h

    def psum(self, name, shape, dtype=F32):
        g = self.nc.psum_tensor(name, shape, dtype)
        h = g.__enter__()
        self._guards[name] = g
        return h

    def free(self, *names):
        for nm in names:
            g = self._guards.pop(nm)
            g.__exit__(None, None, None)

    def free_all_bufs(self):
        for nm in reversed(list(self._guards)):
            self.free(nm)

    def stage_end(self, write_sems=()):
        for s in write_sems:
            self.y.wait(s)
        for e in ENGS:
            r = getattr(self, {"sync": "y", "tensor": "t", "vector": "v",
                               "scalar": "s", "gpsimd": "g"}[e])
            if e != "sync":
                r.drain()
            r.nop(inc=(self.done[e], 1))
        for e in ENGS:
            r = getattr(self, {"sync": "y", "tensor": "t", "vector": "v",
                               "scalar": "s", "gpsimd": "g"}[e])
            for e2 in ENGS:
                if e2 != e:
                    r.wait(self.done[e2])

    def emit(self):
        nc = self.nc
        with nc.Block() as block:
            for eng in ENGS:
                items = self.ops[eng]

                def run(engine, items=items):
                    for meth, a, kw, incs in items:
                        if meth == "__wait__":
                            engine.wait_ge(a[0], a[1])
                            continue
                        instr = getattr(engine, meth)(*a, **kw)
                        for h, k in incs:
                            instr = instr.then_inc(h, k)

                getattr(block, eng)(run)

    def close(self):
        self.free_all_bufs()
        for cm in reversed(self._ctx):
            cm.__exit__(None, None, None)



def build_identity(kb, name, dtype):
    nc = kb.nc
    colio = kb.sbuf(name + "_ci", [128, 1], F32)
    rowio = kb.sbuf(name + "_ri", [128, 128], F32)
    ident = kb.sbuf(name, [128, 128], dtype)
    s = kb.sem(name + "_s")
    kb.g.iota(colio[:, :], [[0, 1]], channel_multiplier=1, allow_small_or_imprecise_dtypes=True, inc=(s, 1))
    kb.g.iota(rowio[:, :], [[1, 128]], channel_multiplier=0, allow_small_or_imprecise_dtypes=True, inc=(s, 1))
    kb.v.wait(s)
    kb.v.tensor_scalar(ident[:, :], rowio[:, :], colio[:, :], None, ALU.is_equal,
                       inc=(s, 1))
    return ident, s


def stage_A(kb, Tn):
    x = Tn["x"]
    x_fm = Tn["x_fm"]
    xn0 = Tn["xn0"]

    epscol = kb.constcol("A_eps", 1e-5)
    identb, sid = build_identity(kb, "A_idb", BF16)
    identf, sidf = build_identity(kb, "A_idf", F32)
    kb.t.wait(sid)
    kb.t.wait(sidf)

    xt = [kb.sbuf(f"A_xt{i}", [128, 1024], F32) for i in range(2)]
    xnt = [kb.sbuf(f"A_xnt{i}", [128, 1024], BF16) for i in range(2)]
    stf = [kb.sbuf(f"A_stf{i}", [128, 1024], F32) for i in range(2)]
    stb = [kb.sbuf(f"A_stb{i}", [128, 1024], BF16) for i in range(2)]
    st = kb.sbuf("A_stat", [128, 8], F32)
    sq = [kb.sbuf(f"A_sq{i}", [128, 1024], F32) for i in range(2)]
    ptf = [kb.psum(f"A_ptf{i}", [128, 1024], F32) for i in range(2)]
    ptb = [kb.psum(f"A_ptb{i}", [128, 1024], BF16) for i in range(2)]

    ld = [kb.sem(f"A_ld{i}") for i in range(2)]
    pp = [kb.sem(f"A_pp{i}") for i in range(2)]
    tp = kb.sem("A_tp")
    sv = kb.sem("A_sv")
    rv = kb.sem("A_rv")
    cpv = kb.sem("A_cpv")
    ss = kb.sem("A_ss")

    for t in range(A_TILES):
        b = t % 2
        if t >= 2:
            kb.y.wait(tp, t - 1)
        kb.y.dma_start(out=xt[b][:, :], in_=x[t * 128:(t + 1) * 128, :],
                       inc=(ld[b], 16))
        kb.s.wait(ld[b], ld[b].n)
        if t >= 2:
            kb.s.wait(rv, t - 1)
        kb.s.activation(sq[b][:, :], xt[b][:, :], AF.Square, inc=(ss, 1))
        kb.v.wait(ld[b], ld[b].n)
        kb.v.reduce_sum(st[:, 0:1], xt[b][:, :], axis=AX.X)
        kb.v.wait(ss, ss.n)
        kb.v.reduce_sum(st[:, 1:2], sq[b][:, :], axis=AX.X, inc=(rv, 1))
        kb.v.drain()
        kb.v.tensor_scalar(st[:, 2:3], st[:, 0:1], 1.0 / D, None, ALU.mult)
        kb.v.tensor_scalar(st[:, 3:4], st[:, 1:2], 1.0 / D, None, ALU.mult)
        kb.v.drain()
        kb.v.tensor_scalar(st[:, 4:5], st[:, 2:3], st[:, 2:3], -1.0,
                           ALU.mult, ALU.mult)
        kb.v.drain()
        kb.v.tensor_tensor(st[:, 5:6], st[:, 3:4], st[:, 4:5], ALU.add,
                           inc=(sv, 1))
        kb.s.wait(sv, sv.n)
        if t == 0:
            kb.s.wait(kb.CONST)
        kb.s.activation(st[:, 6:7], st[:, 5:6], AF.Sqrt, bias=epscol[:, :],
                        inc=(ss, 1))
        kb.v.wait(ss, ss.n)
        kb.v.reciprocal(st[:, 7:8], st[:, 6:7])
        kb.v.drain()
        kb.v.tensor_scalar(xnt[b][:, :], xt[b][:, :], st[:, 2:3], st[:, 7:8],
                           ALU.subtract, ALU.mult, inc=(sv, 1))
        kb.t.wait(sv, sv.n)
        if t >= 2:
            kb.t.wait(cpv, 2 * (t - 1))
        for g in range(NG):
            kb.t.transpose(ptb[b][:, g * 128:(g + 1) * 128],
                           xnt[b][:, g * 128:(g + 1) * 128], identb[:, :])
        for g in range(NG):
            last = (g == NG - 1)
            kb.t.transpose(ptf[b][:, g * 128:(g + 1) * 128],
                           xt[b][:, g * 128:(g + 1) * 128], identf[:, :],
                           inc=(tp, 1) if last else None)
        kb.v.wait(tp, t + 1)
        if t >= 2:
            kb.v.wait(pp[b], pp[b].n)
        kb.v.tensor_copy(stb[b][:, :], ptb[b][:, :], inc=(cpv, 1))
        kb.v.tensor_copy(stf[b][:, :], ptf[b][:, :], inc=(cpv, 1))
        kb.y.wait(cpv, cpv.n)
        src_b = stb[b][:, :].rearrange("p (g j) -> p g j", g=NG)
        dst_b = xn0[:, :, t * 128:(t + 1) * 128].rearrange("g p j -> p g j")
        kb.y.dma_start(out=dst_b, in_=src_b, inc=(pp[b], 16))
        src_f = stf[b][:, :].rearrange("p (g j) -> p g j", g=NG)
        dst_f = x_fm[:, :, t * 128:(t + 1) * 128].rearrange("g p j -> p g j")
        kb.y.dma_start(out=dst_f, in_=src_f, inc=(pp[b], 16))

    kb.stage_end([pp[0], pp[1]])
    kb.free_all_bufs()



SLAB_S = 8704


def stage_B(kb, Tn):
    xn0 = Tn["xn0"]
    wq_t, wk_t, wv_t = Tn["wq_t"], Tn["wk_t"], Tn["wv_t"]
    rbt = Tn["rbt"]
    s0 = Tn["s0"]
    nc = kb.nc
    text_d = nc.dram_tensor("text_d", [16, 8576], BF16, kind="Internal").ap()
    c_d = nc.dram_tensor("c_d", [16, 1], F32, kind="Internal").ap()
    wslab = nc.dram_tensor("wslab", [16, 128 * SLAB_S + 8576], BF16,
                           kind="Internal").ap()

    identf, sidf = build_identity(kb, "B_idf", F32)

    rb_sb = kb.sbuf("B_rb", [128, 512], F32)
    T_sb = kb.sbuf("B_T", [16, 4224], F32)
    text_sb = kb.sbuf("B_text", [16, 8576], BF16)
    ptr = [kb.psum(f"B_ptr{i}", [16, 512], F32) for i in range(2)]

    ldr = kb.sem("B_ldr")
    tt = kb.sem("B_tt")
    cp = kb.sem("B_cp")
    tw = kb.sem("B_tw")
    slw = kb.sem("B_slw")

    src1 = rbt[0:3968, :].rearrange("(c p) h -> p c h", p=128)
    kb.y.dma_start(out=rb_sb[:, 0:496].rearrange("p (c h) -> p c h", h=16),
                   in_=src1, inc=(ldr, 16))
    kb.y.dma_start(out=rb_sb[0:127, 496:512], in_=rbt[3968:4095, :],
                   inc=(ldr, 16))
    kb.t.wait(sidf)
    kb.t.wait(ldr)
    for c in range(32):
        if c >= 2:
            kb.t.wait(cp, c - 1)
        kb.t.transpose(ptr[c % 2][:, 0:128], rb_sb[:, c * 16:(c + 1) * 16],
                       identf[:, :], inc=(tt, 1))
        kb.v.wait(tt, tt.n)
        ncols = 128 if c < 31 else 127
        kb.v.tensor_copy(T_sb[:, c * 128:c * 128 + ncols],
                         ptr[c % 2][0:16, 0:ncols], inc=(cp, 1))
    kb.v.drain()
    kb.v.drain()
    kb.v.tensor_scalar(text_sb[:, 0:4095], T_sb[:, 0:4095],
                       T_sb[:, 4094:4095], None, ALU.subtract)
    kb.v.memset(text_sb[:, 4095:8576], 0.0, inc=(cp, 1))
    kb.y.wait(cp, cp.n)
    kb.y.dma_start(out=text_d, in_=text_sb[:, :], inc=(tw, 16))
    kb.y.dma_start(out=c_d, in_=T_sb[:, 4094:4095], inc=(tw, 16))
    kb.y.wait(tw)
    gsrc = bass.AP(text_d.tensor, 0, [[8576, 16], [0, 128], [1, 128]])
    gdst = bass.AP(wslab.tensor, 0, [[128 * SLAB_S, 16], [SLAB_S, 128],
                                     [1, 128]])
    kb.y.dma_start(out=gdst, in_=gsrc, inc=(tw, 16))
    kb.y.wait(tw)
    for h in range(16):
        src = bass.AP(text_d.tensor, h * 8576, [[0, 128], [1, 8576]])
        dst = bass.AP(wslab.tensor, h * 128 * SLAB_S, [[SLAB_S + 1, 128],
                                                       [1, 8576]])
        kb.y.dma_start(out=dst, in_=src, inc=(slw, 16))
    kb.free("B_ptr1", "B_ptr0", "B_text", "B_T", "B_rb")

    xn0_sb = kb.sbuf("B_xn0", [128, NG * T], BF16)
    qpair = kb.sbuf("B_q", [128, T], BF16)
    kpair = kb.sbuf("B_k", [128, T], BF16)
    vaug = [kb.sbuf(f"B_va{i}", [128, 32 * 65], BF16) for i in range(2)]
    esb = [kb.sbuf(f"B_e{i}", [128, 8192], BF16) for i in range(2)]
    slab_sb = [kb.sbuf(f"B_sl{i}", [128, 8576], BF16) for i in range(2)]
    wsl = [kb.sbuf(f"B_w{i}", [128, 1024], BF16) for i in range(3)]
    ones1f = kb.sbuf("B_o1f", [1, 128], F32)
    bb = kb.sbuf("B_bb", [1, 16], F32)
    sqbuf = [kb.sbuf(f"B_sqb{i}", [128, 512], F32) for i in range(2)]
    onescol = kb.sbuf("B_onescol", [128, 1], F32)
    negB = [kb.sbuf(f"B_nb{i}", [128, 1], F32) for i in range(2)]
    onescolb = kb.sbuf("B_ocb", [128, 1], BF16)
    onesrowb = kb.sbuf("B_orb", [1, 256], BF16)
    crow = kb.sbuf("B_crow", [1, 16], F32)
    vsumc = [kb.sbuf(f"B_vs{i}", [1, 65], BF16) for i in range(2)]
    srow = kb.sbuf("B_srow", [1, 256], F32)
    rrow = kb.sbuf("B_rrow", [1, 256], F32)
    rbc = kb.sbuf("B_rbc", [64, 256], F32)
    tmp = kb.sbuf("B_tmp", [64, 256], F32)
    osb = [kb.sbuf(f"B_osb{i}", [64, 256], BF16) for i in range(2)]

    ldx = kb.sem("B_ldx")
    wl = kb.sem("B_wl")
    sll = [kb.sem(f"B_sll{i}") for i in range(2)]
    mq = kb.sem("B_mq")
    vv = kb.sem("B_vv")
    qev = kb.sem("B_qev")
    sx = kb.sem("B_sx")
    ssq = kb.sem("B_ssq")
    sav = kb.sem("B_sav")
    fav = kb.sem("B_fav")
    rcp = kb.sem("B_rcp")
    prm = kb.sem("B_prm")
    ave = kb.sem("B_ave")
    bsem = kb.sem("B_bsem")
    po = [kb.sem(f"B_po{i}") for i in range(2)]

    kb.v.memset(ones1f[:, :], 1.0)
    kb.v.memset(onescol[:, :], 1.0)
    kb.v.memset(onescolb[:, :], 1.0)
    kb.v.memset(onesrowb[:, :], 1.0)
    for i in range(2):
        kb.v.memset(vaug[i][:, :].rearrange("p (c e) -> p c e", e=65)[:, :, 64:65],
                    1.0, inc=(qev, 1))
    kb.y.wait(tw)
    kb.y.dma_start(out=crow[0:1, 0:16],
                   in_=bass.AP(c_d.tensor, 0, [[0, 1], [1, 16]]),
                   inc=(ldx, 16))
    kb.y.dma_start(out=xn0_sb[:, :].rearrange("p (g t) -> p g t", t=T),
                   in_=xn0.rearrange("g p t -> p g t"), inc=(ldx, 16))

    NT1B = B_NT1B
    for pair in range(B_PAIRS):
        kb.g.wait(mq, mq.n)
        for i, w in enumerate((wq_t, wk_t, wv_t)):
            srcw = w[:, pair * 128:(pair + 1) * 128].rearrange(
                "(g p) e -> p g e", p=128)
            kb.g.dma_start(out=wsl[i][:, :].rearrange("p (g e) -> p g e", e=128),
                           in_=srcw, inc=(wl, 16))
        for i in range(2):
            if pair == 0:
                kb.y.wait(slw)
            else:
                kb.y.wait(fav, fav.n)
            srcs = bass.AP(wslab.tensor, (2 * pair + i) * 128 * SLAB_S,
                           [[SLAB_S, 128], [1, 8576]])
            kb.y.dma_start(out=slab_sb[i][:, :], in_=srcs, inc=(sll[i], 16))

        pq = [kb.psum(f"B_pq{pair}_{i}", [128, 512], F32) for i in range(2)]
        pv = [kb.psum(f"B_pv{pair}_{i}", [128, 512], F32) for i in range(2)]
        pbx = kb.psum(f"B_pbx{pair}", [128, 512], F32)
        pn = [kb.psum(f"B_pn{pair}_{i}", [1, 512], F32) for i in range(2)]
        kb.t.wait(wl)
        if pair == 0:
            kb.t.wait(ldx)
        for which, obuf in ((0, qpair), (1, kpair)):
            for tc in range(8):
                pqb = pq[tc % 2]
                kb.t.wait(qev, max(0, qev.n - 1))
                for g in range(NG):
                    kb.t.matmul(pqb[:, :],
                                wsl[which][:, g * 128:(g + 1) * 128],
                                xn0_sb[:, g * T + tc * 512:g * T + (tc + 1) * 512],
                                start=(g == 0), stop=(g == NG - 1),
                                inc=(mq, 1) if g == NG - 1 else None)
                kb.v.wait(mq, mq.n)
                kb.v.tensor_copy(obuf[:, tc * 512:(tc + 1) * 512], pqb[:, :],
                                 inc=(qev, 1))
        for t2c in range(32):
            pvb = pv[t2c % 2]
            kb.t.wait(vv, max(0, vv.n - 1))
            for g in range(NG):
                kb.t.matmul(pvb[:, 0:128],
                            xn0_sb[:, g * T + t2c * 128:g * T + (t2c + 1) * 128],
                            wsl[2][:, g * 128:(g + 1) * 128],
                            start=(g == 0), stop=(g == NG - 1),
                            inc=(mq, 1) if g == NG - 1 else None)
            kb.v.wait(mq, mq.n)
            kb.v.tensor_copy(vaug[0][:, t2c * 65:t2c * 65 + 64], pvb[:, 0:64])
            kb.v.tensor_copy(vaug[1][:, t2c * 65:t2c * 65 + 64], pvb[:, 64:128],
                             inc=(vv, 1))
        kb.v.wait(qev, qev.n)
        kb.v.wait(vv, vv.n)
        kb.v.wait(sx, sx.n)
        kb.v.memset(bb[:, 0:4], 0.0)
        kb.v.drain()
        kb.s.wait(qev, qev.n)
        kb.s.wait(vv, vv.n)
        rnd = 0
        for src_buf, c0 in ((qpair, 0), (kpair, 2)):
            for tc in range(8):
                sqb = sqbuf[rnd % 2]
                kb.s.wait(bsem, max(0, bsem.n - 2))
                kb.s.activation(sqb[:, :], src_buf[:, tc * 512:(tc + 1) * 512],
                                AF.Square, inc=(ssq, 1))
                kb.t.wait(ssq, ssq.n)
                kb.t.wait(qev, qev.n)
                for h in range(2):
                    kb.t.matmul(pn[h][0:1, 0:512], onescol[h * 64:(h + 1) * 64, :],
                                sqb[h * 64:(h + 1) * 64, :], start=True,
                                stop=True, tile_position=(64 * h, 0),
                                inc=(bsem, 1))
                rnd += 1
                kb.v.wait(bsem, bsem.n)
                for h in range(2):
                    kb.v.reduce_max(bb[:, 8 + h:9 + h], pn[h][0:1, 0:512],
                                    axis=AX.X)
                kb.v.drain()
                for h in range(2):
                    kb.v.tensor_tensor(bb[:, c0 + h:c0 + h + 1],
                                       bb[:, c0 + h:c0 + h + 1],
                                       bb[:, 8 + h:9 + h], ALU.max)
                kb.v.drain()
                kb.v.nop(inc=(qev, 1))
        kb.v.tensor_tensor(bb[:, 10:11], bb[:, 0:1], bb[:, 2:3], ALU.mult)
        kb.v.tensor_tensor(bb[:, 11:12], bb[:, 1:2], bb[:, 3:4], ALU.mult,
                           inc=(qev, 1))
        kb.s.wait(qev, qev.n)
        kb.s.activation(bb[:, 12:14], bb[:, 10:12], AF.Sqrt,
                        scale=1.0 / 1024.0, inc=(ssq, 1))
        kb.v.wait(ssq, ssq.n)
        kb.v.tensor_scalar(bb[:, 4:6], bb[:, 12:14], -1.0, None, ALU.mult,
                           inc=(qev, 1))
        kb.t.wait(qev, qev.n)
        for i in range(2):
            kb.t.matmul(pbx[:, 128 + i * 128:129 + i * 128], ones1f[:, :],
                        bb[:, 4 + i:5 + i], start=True, stop=True,
                        inc=(bsem, 1))
        kb.v.wait(bsem, bsem.n)
        kb.v.tensor_copy(negB[0][:, :], pbx[:, 128:129])
        kb.v.tensor_copy(negB[1][:, :], pbx[:, 256:257], inc=(qev, 1))
        kb.t.wait(vv, vv.n)
        for h in range(2):
            head = 2 * pair + h
            for c in range(32):
                kb.t.matmul(pn[h][0:1, 0:65], onescolb[:, :],
                            vaug[h][:, c * 65:(c + 1) * 65],
                            start=(c == 0), stop=(c == 31),
                            inc=(bsem, 1) if c == 31 else None)
            kb.v.wait(bsem, bsem.n)
            if pair == 0 and h == 0:
                kb.v.wait(ldx, 16)
            kb.v.tensor_scalar(vsumc[h][0:1, 0:65], pn[h][0:1, 0:65],
                               crow[0:1, head:head + 1], None, ALU.mult,
                               inc=(qev, 1))
        kb.free(f"B_pn{pair}_1", f"B_pn{pair}_0", f"B_pbx{pair}",
                f"B_pv{pair}_1", f"B_pv{pair}_0", f"B_pq{pair}_1",
                f"B_pq{pair}_0")

        ps_ = [[kb.psum(f"B_ps{pair}_{h}{i}", [128, 512], F32) for i in range(2)]
               for h in range(2)]
        pm = [kb.psum(f"B_pm{pair}_{h}", [65, 512], F32) for h in range(2)]
        pb = [kb.psum(f"B_pb{pair}_{h}", [65, 512], F32) for h in range(2)]
        kb.t.wait(qev, qev.n)
        for t1b in range(NT1B):
            t10 = t1b * 256
            kept = [c for c in range(32) if 128 * c + 126 > t10]
            for h in range(2):
                kb.t.wait(ave, ave.n)
                kb.t.wait(sll[h], sll[h].n)
                kb.t.matmul(pb[h][0:65, 0:256], vsumc[h][0:1, 0:65],
                            onesrowb[:, :], start=True, stop=False)
            fav_snap = [0, 0]
            sx_hist = []

            def emit_av(cp_):
                kb.t.wait(sx, sx_hist[cp_])
                for h in range(2):
                    for half in range(2):
                        t2c = 2 * cp_ + half
                        va = vaug[h][:, t2c * 65:(t2c + 1) * 65]
                        last = (cp_ == 15 and half == 1)
                        kb.t.matmul(pm[h][0:65, 0:256], va,
                                    esb[h][:, t2c * 256:(t2c + 1) * 256],
                                    start=(cp_ == 0 and half == 0),
                                    stop=last,
                                    inc=(fav, 1) if last else None)
                        if t2c in kept:
                            Boff = 4095 - 128 * t2c + t10
                            kb.t.matmul(pb[h][0:65, 0:256], va,
                                        slab_sb[h][:, Boff:Boff + 256],
                                        start=False, stop=last,
                                        inc=(fav, 1) if last else None)
                    if cp_ == 15:
                        fav_snap[h] = fav.n

            for cpair in range(16):
                for h in range(2):
                    buf = ps_[h][cpair % 2]
                    kb.t.wait(sx, max(0, sx.n - 2))
                    for half in range(2):
                        t2c = 2 * cpair + half
                        part = (0, 0) if h == 0 else (64, 0)
                        kb.t.matmul(
                            buf[:, half * 256:(half + 1) * 256],
                            kpair[h * 64:(h + 1) * 64,
                                  t2c * 128:(t2c + 1) * 128],
                            qpair[h * 64:(h + 1) * 64, t10:t10 + 256],
                            start=True, stop=True, tile_position=part,
                            inc=(sav, 1) if half == 1 else None)
                for h in range(2):
                    kb.s.wait(sav, sav.n - (1 - h))
                    if pair == 0 and t1b == 0 and cpair == 0:
                        kb.s.wait(qev, qev.n)
                    kb.s.activation(esb[h][:, cpair * 512:(cpair + 1) * 512],
                                    ps_[h][cpair % 2][:, :], AF.Exp,
                                    scale=1.0 / 32.0, bias=negB[h][:, :],
                                    inc=(sx, 1))
                sx_hist.append(sx.n)
                if cpair >= 1:
                    emit_av(cpair - 1)
            emit_av(15)
            for h in range(2):
                head = 2 * pair + h
                kb.v.wait(fav, fav_snap[h])
                kb.v.reciprocal(rrow[:, :], pm[h][64:65, 0:256], inc=(rcp, 1))
                kb.t.wait(rcp, rcp.n)
                kb.t.matmul(pm[h][0:64, 256:512], ones1f[:, 0:64], rrow[:, :],
                            start=True, stop=True, inc=(prm, 1))
                kb.v.wait(prm, prm.n)
                kb.v.wait(po[h], po[h].n)
                kb.v.tensor_copy(rbc[:, :], pm[h][0:64, 256:512])
                kb.v.drain()
                kb.v.tensor_tensor(tmp[:, :], pm[h][0:64, 0:256], rbc[:, :],
                                   ALU.mult)
                kb.v.drain()
                kb.v.tensor_tensor(osb[h][:, :], tmp[:, :], pb[h][0:64, 0:256],
                                   ALU.add, inc=(ave, 1))
                kb.y.wait(ave, ave.n)
                g, prow = head // 2, (head % 2) * 64
                kb.y.dma_start(out=s0[g, prow:prow + 64, t10:t10 + 256],
                               in_=osb[h][:, :], inc=(po[h], 16))
        kb.free(f"B_pb{pair}_1", f"B_pb{pair}_0",
                f"B_pm{pair}_1", f"B_pm{pair}_0",
                f"B_ps{pair}_11", f"B_ps{pair}_10",
                f"B_ps{pair}_01", f"B_ps{pair}_00")

    kb.stage_end([po[0], po[1]])
    kb.free_all_bufs()



def _bitlinear_weff(kb, wrow, wq, wcols, scol, n, sem):
    kb.v.reduce_sum(scol[:, 0:1], wrow[:, :], axis=AX.X,
                    apply_absolute_value=True)
    kb.v.drain()
    kb.v.tensor_scalar(scol[:, 1:2], scol[:, 0:1], 1.0 / n, None, ALU.mult)
    kb.v.drain()
    kb.v.tensor_scalar(scol[:, 2:3], scol[:, 1:2], 1e-5, None, ALU.max)
    kb.v.drain()
    kb.v.reciprocal(scol[:, 3:4], scol[:, 2:3])
    kb.v.drain()
    kb.v.tensor_scalar(wcols[:, :], wrow[:, :], scol[:, 3:4], None, ALU.mult)
    kb.v.drain()
    kb.v.tensor_scalar(wcols[:, :], wcols[:, :], MAGIC, MAGIC, ALU.add,
                       ALU.subtract)
    kb.v.drain()
    kb.v.tensor_scalar(wcols[:, :], wcols[:, :], 1.0, -1.0, ALU.min, ALU.max)
    kb.v.drain()
    kb.v.tensor_scalar(wq[:, :], wcols[:, :], scol[:, 2:3], None, ALU.mult,
                       inc=(sem, 1))


def stage_C(kb, Tn):
    xn0, s0, x_fm = Tn["xn0"], Tn["s0"], Tn["x_fm"]
    wfc, wproj = Tn["wfc"], Tn["wproj"]
    u_t, v_t, fb_t, gate = Tn["u_t"], Tn["v_t"], Tn["fb_t"], Tn["l0_gate"]
    lng, lnb = Tn["l0_lng"], Tn["l0_lnb"]
    x1_fm, x1stats = Tn["x1_fm"], Tn["x1stats"]
    nc = kb.nc
    s0mr = nc.dram_tensor("s0mr", [2, T], F32, kind="Internal").ap()

    epscol = kb.constcol("C_eps", 1e-5)
    identb, sid = build_identity(kb, "C_idb", BF16)
    ones1f = kb.sbuf("C_o1f", [1, 128], F32)
    onesb = kb.sbuf("C_onesb", [128, 1], BF16)
    onesf = kb.sbuf("C_onesf", [128, 1], F32)
    msem = kb.sem("C_ms")
    kb.v.memset(ones1f[:, :], 1.0)
    kb.v.memset(onesb[:, :], 1.0)
    kb.v.memset(onesf[:, :], 1.0, inc=(msem, 1))

    st_sb = kb.sbuf("C_st", [1, 8192], F32)
    s0t = [kb.sbuf(f"C_s0t{i}", [128, 512], BF16) for i in range(2)]
    s0sq = kb.sbuf("C_s0sq", [128, 512], BF16)
    pst = kb.psum("C_pst", [1, 512], F32)
    psq = kb.psum("C_psq", [1, 512], F32)
    l0 = [kb.sem(f"C_l0{i}") for i in range(2)]
    mm = kb.sem("C_mm")
    ev = kb.sem("C_ev")
    sc = kb.sem("C_sc")
    pw = kb.sem("C_pw")
    kb.t.wait(msem)
    for tc in range(8):
        for g in range(NG):
            b = (tc * NG + g) % 2
            kb.y.wait(mm, max(0, mm.n - 1))
            kb.y.wait(sc, max(0, sc.n - 1))
            kb.y.dma_start(out=s0t[b][:, :],
                           in_=s0[g, :, tc * 512:(tc + 1) * 512],
                           inc=(l0[b], 16))
            kb.s.wait(l0[b], l0[b].n)
            kb.s.wait(mm, mm.n)
            kb.s.activation(s0sq[:, :], s0t[b][:, :], AF.Square, inc=(sc, 1))
            kb.t.wait(l0[b], l0[b].n)
            kb.t.matmul(pst[0:1, :], onesb[:, :], s0t[b][:, :],
                        start=(g == 0), stop=(g == NG - 1))
            kb.t.wait(sc, sc.n)
            kb.t.matmul(psq[0:1, :], onesb[:, :], s0sq[:, :],
                        start=(g == 0), stop=(g == NG - 1), inc=(mm, 1))
        kb.v.wait(mm, mm.n)
        kb.v.tensor_copy(st_sb[:, tc * 512:(tc + 1) * 512], pst[0:1, :])
        kb.v.tensor_copy(st_sb[:, 4096 + tc * 512:4096 + (tc + 1) * 512],
                         psq[0:1, :], inc=(ev, 1))
        kb.t.wait(ev, ev.n)
    kb.v.drain()
    kb.v.tensor_scalar(st_sb[:, 0:4096], st_sb[:, 0:4096], 1.0 / D, None,
                       ALU.mult)
    kb.v.tensor_scalar(st_sb[:, 4096:8192], st_sb[:, 4096:8192], 1.0 / D,
                       None, ALU.mult)
    kb.v.drain()
    musq = kb.sbuf("C_musq", [1, 4096], F32)
    kb.v.tensor_tensor(musq[:, :], st_sb[:, 0:4096], st_sb[:, 0:4096],
                       ALU.mult)
    kb.v.drain()
    kb.v.tensor_tensor(st_sb[:, 4096:8192], st_sb[:, 4096:8192], musq[:, :],
                       ALU.subtract, inc=(ev, 1))
    kb.s.wait(ev, ev.n)
    kb.s.wait(kb.CONST)
    kb.s.activation(musq[:, :], st_sb[:, 4096:8192], AF.Sqrt,
                    bias=epscol[0:1, :], inc=(sc, 1))
    kb.v.wait(sc, sc.n)
    kb.v.reciprocal(st_sb[:, 4096:8192], musq[:, :], inc=(ev, 1))
    kb.y.wait(ev, ev.n)
    kb.y.dma_start(out=bass.AP(s0mr.tensor, 0, [[0, 1], [T, 2], [1, T]]),
                   in_=st_sb[:, :].rearrange("o (a t) -> o a t", a=2),
                   inc=(pw, 16))
    kb.stage_end([pw])
    kb.free("C_musq", "C_psq", "C_pst", "C_s0sq", "C_s0t1", "C_s0t0", "C_st")

    wfcT = kb.sbuf("C_wfcT", [128, NG * 4096], BF16)
    wprojT = kb.sbuf("C_wprojT", [128, 32 * 1024], BF16)
    wrow = [kb.sbuf(f"C_wrow{i}", [128, 4096], F32) for i in range(2)]
    wcols = kb.sbuf("C_wcols", [128, 4096], F32)
    wqb = kb.sbuf("C_wqb", [128, 4096], BF16)
    scol = kb.sbuf("C_scol", [128, 4], F32)
    ptw = kb.psum("C_ptw", [128, 1024], BF16)
    lw = [kb.sem(f"C_lw{i}") for i in range(2)]
    tb = kb.sem("C_tb")
    tt2 = kb.sem("C_tt2")
    tb2 = kb.sem("C_tb2")

    kb.t.wait(sid)
    for jg in range(32):
        b = jg % 2
        kb.y.wait(ev, ev.n)
        kb.y.dma_start(out=wrow[b][:, 0:1024],
                       in_=wfc[jg * 128:(jg + 1) * 128, :], inc=(lw[b], 16))
        kb.v.wait(lw[b], lw[b].n)
        kb.v.wait(tb, tb.n)
        _bitlinear_weff(kb, wrow[b][:, 0:1024], wqb[:, 0:1024],
                        wcols[:, 0:1024], scol, 1024, ev)
        kb.t.wait(ev, ev.n)
        kb.t.wait(tb, tb.n)
        for g in range(NG):
            kb.t.transpose(ptw[:, g * 128:(g + 1) * 128],
                           wqb[:, g * 128:(g + 1) * 128], identb[:, :],
                           inc=(tb, 1) if g == NG - 1 else None)
        kb.v.wait(tb, tb.n)
        for g in range(NG):
            kb.v.tensor_copy(wfcT[:, g * 4096 + jg * 128:g * 4096 + (jg + 1) * 128],
                             ptw[:, g * 128:(g + 1) * 128],
                             inc=(tb, 1) if g == NG - 1 else None)
    for dg in range(NG):
        b = dg % 2
        kb.y.wait(ev, ev.n)
        kb.y.dma_start(out=wrow[b][:, :],
                       in_=wproj[dg * 128:(dg + 1) * 128, :], inc=(lw[b], 16))
        kb.v.wait(lw[b], lw[b].n)
        kb.v.wait(tt2, tt2.n)
        _bitlinear_weff(kb, wrow[b][:, :], wqb[:, :], wcols[:, :], scol,
                        4096, ev)
        kb.t.wait(ev, ev.n)
        for jb in range(4):
            kb.t.wait(tb2, tb2.n)
            for jj in range(8):
                jc = jb * 8 + jj
                kb.t.transpose(ptw[:, jj * 128:(jj + 1) * 128],
                               wqb[:, jc * 128:(jc + 1) * 128], identb[:, :],
                               inc=(tt2, 1) if jj == 7 else None)
            kb.v.wait(tt2, tt2.n)
            for jj in range(8):
                jc = jb * 8 + jj
                kb.v.tensor_copy(
                    wprojT[:, jc * 1024 + dg * 128:jc * 1024 + (dg + 1) * 128],
                    ptw[:, jj * 128:(jj + 1) * 128],
                    inc=(tb2, 1) if jj == 7 else None)
        kb.v.nop(inc=(tb, 1))
    kb.stage_end()
    kb.free("C_ptw", "C_scol", "C_wqb", "C_wcols", "C_wrow1", "C_wrow0")

    usl = kb.sbuf("C_usl", [16, 4096], BF16)
    vsl = kb.sbuf("C_vsl", [128, 128], BF16)
    fbsl = kb.sbuf("C_fb", [16, 16], BF16)
    gcol = kb.sbuf("C_gcol", [16, 1], F32)
    b_sb = kb.sbuf("C_bsb", [16, 4096], BF16)
    xnt = kb.sbuf("C_xnt", [128, 4096], BF16)
    gf = kb.sbuf("C_gf", [1, 1], F32)
    a_sb = kb.sbuf("C_asb", [16, 4096], BF16)
    pfa = kb.psum("C_pfa", [16, 512], F32)
    pga = kb.psum("C_pga", [16, 512], F32)
    lf = kb.sem("C_lf")
    lx = kb.sem("C_lx")

    kb.g.dma_start(out=usl[:, :], in_=u_t, inc=(lf, 16))
    kb.g.dma_start(out=vsl[:, :].rearrange("p (g r) -> p g r", r=16),
                   in_=v_t.rearrange("(g p) r -> p g r", p=128), inc=(lf, 16))
    kb.g.dma_start(out=fbsl[:, :], in_=fb_t, inc=(lf, 16))
    kb.y.dma_start(out=gf[:, :], in_=gate, inc=(lx, 16))
    kb.t.wait(lx, lx.n)
    kb.t.wait(msem)
    kb.t.matmul(pga[0:16, 0:1], ones1f[:, 0:16], gf[:, :], start=True,
                stop=True, inc=(mm, 1))
    kb.v.wait(mm, mm.n)
    kb.v.tensor_copy(gcol[:, :], pga[0:16, 0:1], inc=(ev, 1))
    for tc in range(8):
        kb.y.wait(mm, mm.n)
        kb.y.dma_start(out=xnt[:, :].rearrange("p (g t) -> p g t", t=512),
                       in_=xn0[:, :, tc * 512:(tc + 1) * 512].rearrange(
                           "g p t -> p g t"), inc=(lx, 16))
        kb.t.wait(lx, lx.n)
        kb.t.wait(lf, lf.n)
        kb.t.wait(ev, ev.n)
        for g in range(NG):
            kb.t.matmul(pfa[0:16, :], vsl[:, g * 16:(g + 1) * 16],
                        xnt[:, g * 512:(g + 1) * 512],
                        start=(g == 0), stop=(g == NG - 1),
                        inc=(mm, 1) if g == NG - 1 else None)
        kb.v.wait(mm, mm.n)
        kb.v.tensor_copy(a_sb[:, tc * 512:(tc + 1) * 512], pfa[0:16, :],
                         inc=(ev, 1))
    for tc in range(8):
        kb.t.wait(ev, ev.n)
        kb.t.matmul(pga[0:16, :], fbsl[:, :], a_sb[:, tc * 512:(tc + 1) * 512],
                    start=True, stop=True, inc=(mm, 1))
        kb.v.wait(mm, mm.n)
        kb.v.tensor_scalar(b_sb[:, tc * 512:(tc + 1) * 512], pga[0:16, :],
                           gcol[:, :], None, ALU.mult, inc=(ev, 1))
    kb.stage_end()
    kb.free("C_pga", "C_pfa", "C_asb")

    rr = kb.sbuf("C_rr", [128, 32 * 512], BF16)
    rr1 = kb.sbuf("C_rr1", [128, 512], BF16)
    g0t = kb.sbuf("C_g0t", [128, 512], BF16)
    s0c = kb.sbuf("C_s0c", [128, 512], BF16)
    xtc = kb.sbuf("C_xtc", [128, 512], F32)
    mrow = kb.sbuf("C_mrow", [1, 1024], F32)
    mub = kb.sbuf("C_mub", [128, 512], F32)
    rsb = kb.sbuf("C_rsb", [128, 512], F32)
    gcols = kb.sbuf("C_gcols", [128, 8], F32)
    bcols = kb.sbuf("C_bcols", [128, 8], F32)
    t1 = kb.sbuf("C_t1", [128, 512], F32)
    t2 = kb.sbuf("C_t2", [128, 512], F32)
    xn1t = kb.sbuf("C_xn1t", [128, 512], BF16)
    x1t = kb.sbuf("C_x1t", [128, 512], F32)
    x1sq = kb.sbuf("C_x1sq", [128, 512], F32)
    ph = [kb.psum(f"C_ph{i}", [128, 512], F32) for i in range(2)]
    po = [kb.psum(f"C_po{i}", [128, 512], F32) for i in range(2)]
    pb2 = kb.psum("C_pb2", [128, 1024], F32)
    pst2 = kb.psum("C_pst2", [1, 1024], F32)
    lm = kb.sem("C_lm")
    lsx = [kb.sem(f"C_lsx{i}") for i in range(3)]
    rvv = kb.sem("C_rvv")
    scc = kb.sem("C_scc")
    pxw = kb.sem("C_pxw")

    kb.g.dma_start(out=gcols[:, :],
                   in_=bass.AP(lng.tensor, 0, [[1, 128], [128, 8]]),
                   allow_slow_non_contiguous=True, inc=(lm, 16))
    kb.g.dma_start(out=bcols[:, :],
                   in_=bass.AP(lnb.tensor, 0, [[1, 128], [128, 8]]),
                   allow_slow_non_contiguous=True, inc=(lm, 16))

    for tc in range(8):
        kb.y.wait(mm, mm.n)
        kb.y.dma_start(out=xnt[:, :].rearrange("p (g t) -> p g t", t=512),
                       in_=xn0[:, :, tc * 512:(tc + 1) * 512].rearrange(
                           "g p t -> p g t"), inc=(lx, 16))
        kb.y.wait(pw, pw.n)
        kb.y.wait(pxw, pxw.n)
        kb.y.dma_start(out=mrow[:, :].rearrange("o (a t) -> o a t", a=2),
                       in_=bass.AP(s0mr.tensor, tc * 512,
                                   [[0, 1], [T, 2], [1, 512]]),
                       inc=(lsx[2], 16))
        kb.t.wait(lsx[2], lsx[2].n)
        kb.t.wait(rvv, rvv.n)
        kb.t.matmul(pb2[:, 0:512], ones1f[:, :], mrow[:, 0:512], start=True,
                    stop=True)
        kb.t.matmul(pb2[:, 512:1024], ones1f[:, :], mrow[:, 512:1024],
                    start=True, stop=True, inc=(mm, 1))
        kb.v.wait(mm, mm.n)
        kb.v.tensor_copy(mub[:, :], pb2[:, 0:512])
        kb.v.tensor_copy(rsb[:, :], pb2[:, 512:1024], inc=(rvv, 1))
        kb.t.wait(lx, lx.n)
        for jsl in range(32):
            pp = ph[jsl % 2]
            kb.t.wait(rvv, max(0, rvv.n - 1))
            for g in range(NG):
                kb.t.matmul(pp[:, :],
                            wfcT[:, g * 4096 + jsl * 128:g * 4096 + (jsl + 1) * 128],
                            xnt[:, g * 512:(g + 1) * 512],
                            start=(g == 0), stop=False)
            kb.t.matmul(pp[:, :], usl[:, jsl * 128:(jsl + 1) * 128],
                        b_sb[:, tc * 512:(tc + 1) * 512],
                        start=False, stop=True, inc=(mm, 1))
            kb.v.wait(mm, mm.n)
            kb.v.wait(scc, scc.n)
            kb.v.tensor_scalar(rr1[:, :], pp[:, :], 0.0, None, ALU.max,
                               inc=(rvv, 1))
            kb.s.wait(rvv, rvv.n)
            kb.s.activation(rr[:, jsl * 512:(jsl + 1) * 512], rr1[:, :],
                            AF.Square, inc=(scc, 1))
        kb.y.wait(scc, scc.n)
        kb.y.wait(pxw, pxw.n)
        for ds in range(NG):
            kb.y.dma_start(out=s0c[:, :],
                           in_=s0[ds, :, tc * 512:(tc + 1) * 512],
                           inc=(lsx[0], 16))
            kb.y.dma_start(out=xtc[:, :],
                           in_=x_fm[ds, :, tc * 512:(tc + 1) * 512],
                           inc=(lsx[1], 16))
            pq = po[ds % 2]
            kb.t.wait(scc, scc.n)
            kb.t.wait(rvv, rvv.n)
            for jc in range(32):
                kb.t.matmul(pq[:, :],
                            wprojT[:, jc * 1024 + ds * 128:jc * 1024 + (ds + 1) * 128],
                            rr[:, jc * 512:(jc + 1) * 512],
                            start=(jc == 0), stop=(jc == 31),
                            inc=(mm, 1) if jc == 31 else None)
            kb.s.wait(mm, mm.n)
            kb.s.activation(g0t[:, :], pq[:, :], AF.Silu, inc=(scc, 1))
            kb.v.wait(scc, scc.n)
            kb.v.wait(lsx[0], lsx[0].n)
            kb.v.wait(lsx[1], lsx[1].n)
            if tc == 0 and ds == 0:
                kb.v.wait(lm, lm.n)
            kb.v.wait(mm, mm.n)
            kb.v.wait(pxw, pxw.n)
            kb.v.tensor_tensor(t1[:, :], s0c[:, :], mub[:, :], ALU.subtract)
            kb.v.drain()
            kb.v.tensor_tensor(t2[:, :], t1[:, :], rsb[:, :], ALU.mult)
            kb.v.drain()
            kb.v.tensor_scalar(t1[:, :], t2[:, :], gcols[:, ds:ds + 1],
                               bcols[:, ds:ds + 1], ALU.mult, ALU.add)
            kb.v.drain()
            kb.v.tensor_tensor(t2[:, :], t1[:, :], g0t[:, :], ALU.mult)
            kb.v.drain()
            kb.v.tensor_tensor(x1t[:, :], t2[:, :], xtc[:, :], ALU.add,
                               inc=(rvv, 1))
            kb.y.wait(rvv, rvv.n)
            kb.y.dma_start(out=x1_fm[ds, :, tc * 512:(tc + 1) * 512],
                           in_=x1t[:, :], inc=(pxw, 16))
            kb.s.wait(rvv, rvv.n)
            kb.s.activation(x1sq[:, :], x1t[:, :], AF.Square, inc=(scc, 1))
            kb.t.wait(rvv, rvv.n)
            kb.t.matmul(pst2[0:1, 0:512], onesf[:, :], x1t[:, :],
                        start=(ds == 0), stop=(ds == NG - 1))
            kb.t.wait(scc, scc.n)
            kb.t.matmul(pst2[0:1, 512:1024], onesf[:, :], x1sq[:, :],
                        start=(ds == 0), stop=(ds == NG - 1), inc=(mm, 1))
        kb.v.wait(mm, mm.n)
        kb.v.tensor_copy(mrow[:, :], pst2[0:1, :], inc=(rvv, 1))
        kb.t.wait(rvv, rvv.n)
        kb.v.drain()
        kb.v.tensor_scalar(mrow[:, :], mrow[:, :], 1.0 / D, None, ALU.mult)
        kb.v.drain()
        kb.v.tensor_tensor(x1sq[0:1, 0:512], mrow[:, 0:512], mrow[:, 0:512],
                           ALU.mult)
        kb.v.drain()
        kb.v.tensor_tensor(mrow[:, 512:1024], mrow[:, 512:1024],
                           x1sq[0:1, 0:512], ALU.subtract, inc=(rvv, 1))
        kb.s.wait(rvv, rvv.n)
        kb.s.activation(x1sq[0:1, 0:512], mrow[:, 512:1024], AF.Sqrt,
                        bias=epscol[0:1, :], inc=(scc, 1))
        kb.v.wait(scc, scc.n)
        kb.v.reciprocal(mrow[:, 512:1024], x1sq[0:1, 0:512], inc=(rvv, 1))
        kb.t.wait(rvv, rvv.n)
        kb.t.matmul(pb2[:, 0:512], ones1f[:, :], mrow[0:1, 0:512], start=True,
                    stop=True)
        kb.t.matmul(pb2[:, 512:1024], ones1f[:, :], mrow[0:1, 512:1024],
                    start=True, stop=True, inc=(mm, 1))
        kb.v.wait(mm, mm.n)
        kb.v.tensor_copy(mub[:, :], pb2[:, 0:512])
        kb.v.tensor_copy(rsb[:, :], pb2[:, 512:1024], inc=(rvv, 1))
        kb.y.wait(pxw, pxw.n)
        for g in range(NG):
            kb.y.wait(rvv, rvv.n)
            kb.y.dma_start(out=xtc[:, :],
                           in_=x1_fm[g, :, tc * 512:(tc + 1) * 512],
                           inc=(lsx[1], 16))
            kb.v.wait(lsx[1], lsx[1].n)
            kb.v.wait(pxw, pxw.n)
            kb.v.tensor_tensor(t1[:, :], xtc[:, :], mub[:, :], ALU.subtract)
            kb.v.drain()
            kb.v.tensor_tensor(xn1t[:, :], t1[:, :], rsb[:, :], ALU.mult,
                               inc=(rvv, 1))
            kb.y.wait(rvv, rvv.n)
            kb.y.dma_start(out=Tn["xn1"][g, :, tc * 512:(tc + 1) * 512],
                           in_=xn1t[:, :], inc=(pxw, 16))

    kb.stage_end([pxw, pw])
    kb.free_all_bufs()



def stage_D(kb, Tn):
    x1_fm, x1stats, xn1 = Tn["x1_fm"], Tn["x1stats"], Tn["xn1"]
    epscol = kb.constcol("D_eps", 1e-5)
    ones1f = kb.sbuf("D_o1f", [1, 128], F32)
    strow = kb.sbuf("D_st", [1, 8192], F32)
    musq = kb.sbuf("D_musq", [1, 4096], F32)
    mub = kb.sbuf("D_mub", [128, 512], F32)
    rsb = kb.sbuf("D_rsb", [128, 512], F32)
    x1t = [kb.sbuf(f"D_x1t{i}", [128, 512], F32) for i in range(2)]
    xnt = [kb.sbuf(f"D_xnt{i}", [128, 512], BF16) for i in range(2)]
    t1 = kb.sbuf("D_t1", [128, 512], F32)
    pb2 = kb.psum("D_pb2", [128, 1024], F32)
    ls = kb.sem("D_ls")
    lx = [kb.sem(f"D_lx{i}") for i in range(2)]
    mm = kb.sem("D_mm")
    ev = kb.sem("D_ev")
    sc = kb.sem("D_sc")
    pw = kb.sem("D_pw")

    kb.v.memset(ones1f[:, :], 1.0, inc=(kb.CONST, 1))
    kb.y.dma_start(out=strow[:, :].rearrange("o (a t) -> o a t", a=2),
                   in_=bass.AP(x1stats.tensor, 0, [[0, 1], [T, 2], [1, T]]),
                   inc=(ls, 16))
    kb.v.wait(ls, ls.n)
    kb.v.tensor_scalar(strow[:, 0:4096], strow[:, 0:4096], 1.0 / D, None,
                       ALU.mult)
    kb.v.tensor_scalar(strow[:, 4096:8192], strow[:, 4096:8192], 1.0 / D,
                       None, ALU.mult)
    kb.v.drain()
    kb.v.tensor_tensor(musq[:, :], strow[:, 0:4096], strow[:, 0:4096],
                       ALU.mult)
    kb.v.drain()
    kb.v.tensor_tensor(strow[:, 4096:8192], strow[:, 4096:8192], musq[:, :],
                       ALU.subtract, inc=(ev, 1))
    kb.s.wait(ev, ev.n)
    kb.s.wait(kb.CONST)
    kb.s.activation(musq[:, :], strow[:, 4096:8192], AF.Sqrt,
                    bias=epscol[0:1, :], inc=(sc, 1))
    kb.v.wait(sc, sc.n)
    kb.v.reciprocal(strow[:, 4096:8192], musq[:, :], inc=(ev, 1))
    for tc in range(8):
        kb.t.wait(ev, ev.n)
        kb.t.matmul(pb2[:, 0:512], ones1f[:, :],
                    strow[0:1, tc * 512:(tc + 1) * 512], start=True, stop=True)
        kb.t.matmul(pb2[:, 512:1024], ones1f[:, :],
                    strow[0:1, 4096 + tc * 512:4096 + (tc + 1) * 512],
                    start=True, stop=True, inc=(mm, 1))
        kb.v.wait(mm, mm.n)
        kb.v.tensor_copy(mub[:, :], pb2[:, 0:512])
        kb.v.tensor_copy(rsb[:, :], pb2[:, 512:1024], inc=(ev, 1))
        for g in range(NG):
            b = g % 2
            kb.y.wait(ev, ev.n)
            kb.y.dma_start(out=x1t[b][:, :],
                           in_=x1_fm[g, :, tc * 512:(tc + 1) * 512],
                           inc=(lx[b], 16))
            kb.v.wait(lx[b], lx[b].n)
            kb.v.wait(pw, pw.n)
            kb.v.tensor_tensor(t1[:, :], x1t[b][:, :], mub[:, :], ALU.subtract)
            kb.v.drain()
            kb.v.tensor_tensor(xnt[b][:, :], t1[:, :], rsb[:, :], ALU.mult,
                               inc=(ev, 1))
            kb.y.wait(ev, ev.n)
            kb.y.dma_start(out=xn1[g, :, tc * 512:(tc + 1) * 512],
                           in_=xnt[b][:, :], inc=(pw, 16))
    kb.stage_end([pw])
    kb.free_all_bufs()



def stage_E(kb, Tn):
    xn1 = Tn["xn1"]
    win, bin_, plog = Tn["l1_win"], Tn["l1_bin"], Tn["l1_plog"]
    wmid, bmid = Tn["l1_wmid"], Tn["l1_bmid"]
    w1, w1b, w2, w2b = Tn["l1_w1"], Tn["l1_w1b"], Tn["l1_w2"], Tn["l1_w2b"]
    lng, lnb = Tn["l1_lng"], Tn["l1_lnb"]
    s1 = Tn["s1"]
    nc = kb.nc
    hpark = nc.dram_tensor("hpark", [4, NG, 128, T], BF16,
                           kind="ExternalOutput" if DBG_PARKS else "Internal").ap()

    PI = float(np.pi)
    epscol = kb.constcol("E_eps", 1e-5)
    pi2col = kb.constcol("E_pi2", PI / 2.0)
    identf, sidf = build_identity(kb, "E_idf", F32)
    ones1f = kb.sbuf("E_o1f", [1, 128], F32)
    onesrow = kb.sbuf("E_orow", [1, 4096], BF16)
    onesf = kb.sbuf("E_onesf", [128, 1], F32)
    msem = kb.sem("E_ms")
    kb.v.memset(ones1f[:, :], 1.0)
    kb.v.memset(onesrow[:, :], 1.0)
    kb.v.memset(onesf[:, :], 1.0, inc=(msem, 1))

    plog_sb = kb.sbuf("E_plog", [3, 1024], F32)
    pp = kb.sbuf("E_pp", [128, 24], F32)
    nu8 = kb.sbuf("E_nu8", [128, 8], F32)
    rho8 = kb.sbuf("E_rho8", [128, 8], F32)
    th8 = kb.sbuf("E_th8", [128, 8], F32)
    ga8 = kb.sbuf("E_ga8", [128, 8], F32)
    cth8 = kb.sbuf("E_cth8", [128, 8], F32)
    sth8 = kb.sbuf("E_sth8", [128, 8], F32)
    tsc = kb.sbuf("E_tsc", [128, 24], F32)
    ptp = kb.psum("E_ptp", [128, 512], F32)
    lp = kb.sem("E_lp")
    tt0 = kb.sem("E_tt0")
    ev = kb.sem("E_ev")
    sc = kb.sem("E_sc")

    kb.y.dma_start(out=plog_sb[:, :], in_=plog, inc=(lp, 16))
    kb.t.wait(sidf)
    kb.t.wait(lp)
    for g in range(NG):
        kb.t.transpose(ptp[:, g * 4:g * 4 + 3], plog_sb[:, g * 128:(g + 1) * 128],
                       identf[0:3, 0:3], inc=(tt0, 1) if g == NG - 1 else None)
    kb.v.wait(tt0, tt0.n)
    kb.v.tensor_copy(pp[:, :].rearrange("p (g e) -> p g e", e=3),
                     ptp[:, 0:32].rearrange("p (g e) -> p g e", e=4)[:, :, 0:3],
                     inc=(ev, 1))
    kb.v.drain()
    kb.v.tensor_copy(nu8[:, :], pp[:, :].rearrange("p (g e) -> p e g", e=3)[:, 0, :])
    kb.v.tensor_copy(th8[:, :], pp[:, :].rearrange("p (g e) -> p e g", e=3)[:, 1, :])
    kb.v.tensor_copy(ga8[:, :], pp[:, :].rearrange("p (g e) -> p e g", e=3)[:, 2, :],
                     inc=(ev, 1))
    kb.s.wait(ev, ev.n)
    kb.s.activation(nu8[:, :], nu8[:, :], AF.Exp)
    kb.s.drain()
    kb.s.activation(rho8[:, :], nu8[:, :], AF.Exp, scale=-1.0)
    kb.s.activation(th8[:, :], th8[:, :], AF.Exp)
    kb.s.drain()
    kb.s.activation(ga8[:, :], ga8[:, :], AF.Exp, inc=(sc, 1))
    kb.v.wait(sc, sc.n)
    kb.v.tensor_scalar(tsc[:, 0:8], th8[:, :], 1.0 / (2 * PI), MAGIC, ALU.mult,
                       ALU.add)
    kb.v.drain()
    kb.v.tensor_scalar(tsc[:, 0:8], tsc[:, 0:8], MAGIC, -2.0 * PI,
                       ALU.subtract, ALU.mult)
    kb.v.drain()
    kb.v.tensor_tensor(th8[:, :], th8[:, :], tsc[:, 0:8], ALU.add,
                       inc=(ev, 1))
    kb.s.wait(ev, ev.n)
    kb.s.wait(kb.CONST)
    kb.s.activation(tsc[:, 8:16], th8[:, :], AF.Sin, scale=0.5)
    kb.s.drain()
    kb.s.activation(tsc[:, 16:24], th8[:, :], AF.Sin, scale=-0.5,
                    bias=pi2col[:, :])
    kb.s.drain()
    kb.s.activation(tsc[:, 0:8], tsc[:, 8:16], AF.Square, inc=(sc, 1))
    kb.v.wait(sc, sc.n)
    kb.v.scalar_tensor_tensor(sth8[:, :], tsc[:, 8:16], 2.0, tsc[:, 16:24],
                              ALU.mult, ALU.mult)
    kb.v.tensor_scalar(cth8[:, :], tsc[:, 0:8], -2.0, 1.0, ALU.mult, ALU.add,
                       inc=(ev, 1))
    kb.free("E_ptp")

    xn1_sb = kb.sbuf("E_xn1", [128, NG * T], BF16)
    lxx = kb.sem("E_lxx")
    kb.y.dma_start(out=xn1_sb[:, :].rearrange("p (g t) -> p g t", t=T),
                   in_=xn1.rearrange("g p t -> p g t"), inc=(lxx, 16))

    ur = kb.sbuf("E_ur", [128, T], F32)
    ui = kb.sbuf("E_ui", [128, T], F32)
    gr = kb.sbuf("E_gr", [128, T], F32)
    gi = kb.sbuf("E_gi", [128, T], F32)
    cs = kb.sbuf("E_cs", [128, 520], F32)
    sn = kb.sbuf("E_sn", [128, 520], F32)
    rho_rep = kb.sbuf("E_rhor", [128, 512], F32)
    sc1 = kb.sbuf("E_sc1", [128, 520], F32)
    sc2 = kb.sbuf("E_sc2", [128, 520], F32)
    sc3 = kb.sbuf("E_sc3", [128, 520], F32)
    sc4 = kb.sbuf("E_sc4", [128, 520], F32)
    gcr = kb.sbuf("E_gcr", [128, 1], F32)
    gci = kb.sbuf("E_gci", [128, 1], F32)
    hb = kb.sbuf("E_hb", [128, 2048], BF16)
    wine = [kb.sbuf(f"E_wine{i}", [128, 1024], BF16) for i in range(2)]
    brow = kb.sbuf("E_brow", [1, 256], BF16)
    pu = [kb.psum(f"E_pu{i}", [128, 512], F32) for i in range(2)]
    lw = kb.sem("E_lw")
    warm = kb.sem("E_warm")
    mm = kb.sem("E_mm")
    pwh = kb.sem("E_pwh")
    gsm = kb.sem("E_gsm")
    psm = kb.sem("E_psm")

    for pg in range(NG):
        kb.g.wait(mm, mm.n)
        for e in range(2):
            srcw = win[e, :, pg * 128:(pg + 1) * 128].rearrange(
                "(g p) h -> p g h", p=128)
            kb.g.dma_start(out=wine[e][:, :].rearrange("p (g h) -> p g h", h=128),
                           in_=srcw, inc=(lw, 16))
            kb.g.dma_start(out=brow[:, e * 128:(e + 1) * 128],
                           in_=bin_[e:e + 1, pg * 128:(pg + 1) * 128],
                           inc=(lw, 16))
        RC = 512
        kb.v.wait(ev, ev.n)
        kb.v.wait(psm, psm.n)
        kb.v.memset(cs[:, 0:1], 1.0)
        kb.v.memset(sn[:, 0:1], 0.0)
        kb.v.tensor_copy(cs[:, 1:2], cth8[:, pg:pg + 1])
        kb.v.tensor_copy(sn[:, 1:2], sth8[:, pg:pg + 1])
        kb.v.drain()
        n = 2
        while n < RC + 1:
            m = min(n, RC + 1 - n)
            kb.v.tensor_scalar(sc1[:, 0:1], cs[:, n - 1:n], cth8[:, pg:pg + 1],
                               None, ALU.mult)
            kb.v.tensor_scalar(sc3[:, 0:1], sn[:, n - 1:n], sth8[:, pg:pg + 1],
                               None, ALU.mult)
            kb.v.tensor_scalar(sc2[:, 0:1], sn[:, n - 1:n], cth8[:, pg:pg + 1],
                               None, ALU.mult)
            kb.v.tensor_scalar(sc4[:, 0:1], cs[:, n - 1:n], sth8[:, pg:pg + 1],
                               None, ALU.mult)
            kb.v.drain()
            kb.v.tensor_tensor(sc1[:, 0:1], sc1[:, 0:1], sc3[:, 0:1],
                               ALU.subtract)
            kb.v.tensor_tensor(sc2[:, 0:1], sc2[:, 0:1], sc4[:, 0:1], ALU.add)
            kb.v.drain()
            kb.v.tensor_scalar(cs[:, n:n + m], cs[:, 0:m], sc1[:, 0:1], None,
                               ALU.mult)
            kb.v.tensor_scalar(sn[:, n:n + m], sn[:, 0:m], sc1[:, 0:1], None,
                               ALU.mult)
            kb.v.tensor_scalar(gr[:, 0:m], sn[:, 0:m], sc2[:, 0:1], None,
                               ALU.mult)
            kb.v.tensor_scalar(gi[:, 0:m], cs[:, 0:m], sc2[:, 0:1], None,
                               ALU.mult)
            kb.v.drain()
            kb.v.tensor_tensor(cs[:, n:n + m], cs[:, n:n + m], gr[:, 0:m],
                               ALU.subtract)
            kb.v.tensor_tensor(sn[:, n:n + m], sn[:, n:n + m], gi[:, 0:m],
                               ALU.add)
            kb.v.drain()
            n *= 2
        kb.v.tensor_tensor(sc1[:, 0:RC + 1], cs[:, 0:RC + 1], cs[:, 0:RC + 1],
                           ALU.mult)
        kb.v.tensor_tensor(sc2[:, 0:RC + 1], sn[:, 0:RC + 1], sn[:, 0:RC + 1],
                           ALU.mult)
        kb.v.drain()
        kb.v.tensor_tensor(sc1[:, 0:RC + 1], sc1[:, 0:RC + 1],
                           sc2[:, 0:RC + 1], ALU.add, inc=(ev, 1))
        kb.s.wait(ev, ev.n)
        kb.s.activation(sc2[:, 0:RC + 1], sc1[:, 0:RC + 1], AF.Sqrt,
                        inc=(sc, 1))
        kb.v.wait(sc, sc.n)
        kb.v.reciprocal(sc1[:, 0:RC + 1], sc2[:, 0:RC + 1])
        kb.v.drain()
        kb.v.tensor_tensor(cs[:, 0:RC + 1], cs[:, 0:RC + 1], sc1[:, 0:RC + 1],
                           ALU.mult)
        kb.v.tensor_tensor(sn[:, 0:RC + 1], sn[:, 0:RC + 1], sc1[:, 0:RC + 1],
                           ALU.mult)
        kb.v.drain()
        kb.v.memset(rho_rep[:, 0:RC], 0.0)
        kb.v.drain()
        kb.v.tensor_scalar(rho_rep[:, 0:RC], rho_rep[:, 0:RC],
                           rho8[:, pg:pg + 1], None, ALU.add)
        kb.v.drain()
        kb.t.wait(lw)
        if pg == 0:
            kb.t.wait(lxx)
            kb.t.wait(msem)
            kb.t.wait(ev, ev.n)
        for e, ubuf in ((0, ur), (1, ui)):
            for tc in range(8):
                pb = pu[tc % 2]
                kb.t.wait(ev, max(0, ev.n - 1))
                for g in range(NG):
                    kb.t.matmul(pb[:, :], wine[e][:, g * 128:(g + 1) * 128],
                                xn1_sb[:, g * T + tc * 512:g * T + (tc + 1) * 512],
                                start=(g == 0), stop=False)
                kb.t.matmul(pb[:, :], brow[:, e * 128:(e + 1) * 128],
                            onesrow[:, tc * 512:(tc + 1) * 512],
                            start=False, stop=True, inc=(mm, 1))
                kb.v.wait(mm, mm.n)
                kb.v.tensor_copy(ubuf[:, tc * 512:(tc + 1) * 512], pb[:, :],
                                 inc=(ev, 1))
        for ch in range(T // RC):
            s_ = slice(ch * RC, (ch + 1) * RC)
            kb.v.tensor_tensor(sc1[:, 0:RC], ur[:, s_], cs[:, 0:RC], ALU.mult)
            kb.v.tensor_tensor(sc2[:, 0:RC], ui[:, s_], sn[:, 0:RC], ALU.mult)
            kb.v.tensor_tensor(sc3[:, 0:RC], ui[:, s_], cs[:, 0:RC], ALU.mult)
            kb.v.tensor_tensor(sc4[:, 0:RC], ur[:, s_], sn[:, 0:RC], ALU.mult)
            kb.v.drain()
            kb.v.tensor_tensor(sc1[:, 0:RC], sc1[:, 0:RC], sc2[:, 0:RC],
                               ALU.add)
            kb.v.tensor_tensor(sc3[:, 0:RC], sc3[:, 0:RC], sc4[:, 0:RC],
                               ALU.subtract)
            kb.v.drain()
            kb.v.tensor_scalar(ur[:, s_], sc1[:, 0:RC], ga8[:, pg:pg + 1],
                               None, ALU.mult)
            kb.v.tensor_scalar(ui[:, s_], sc3[:, 0:RC], ga8[:, pg:pg + 1],
                               None, ALU.mult)
            kb.v.drain()
        kb.v.memset(gcr[:, :], 0.0)
        kb.v.memset(gci[:, :], 0.0)
        kb.v.drain()
        for ch in range(T // RC):
            s_ = slice(ch * RC, (ch + 1) * RC)
            kb.v.tensor_tensor_scan(gr[:, s_], rho_rep[:, 0:RC], ur[:, s_],
                                    gcr[:, :], ALU.mult, ALU.add)
            kb.v.tensor_tensor_scan(gi[:, s_], rho_rep[:, 0:RC], ui[:, s_],
                                    gci[:, :], ALU.mult, ALU.add)
            kb.v.drain()
            if ch < T // RC - 1:
                e_ = slice((ch + 1) * RC - 1, (ch + 1) * RC)
                kb.v.tensor_tensor(sc1[:, 0:1], gr[:, e_], cs[:, RC:RC + 1],
                                   ALU.mult)
                kb.v.tensor_tensor(sc2[:, 0:1], gi[:, e_], sn[:, RC:RC + 1],
                                   ALU.mult)
                kb.v.tensor_tensor(sc3[:, 0:1], gr[:, e_], sn[:, RC:RC + 1],
                                   ALU.mult)
                kb.v.tensor_tensor(sc4[:, 0:1], gi[:, e_], cs[:, RC:RC + 1],
                                   ALU.mult)
                kb.v.drain()
                kb.v.tensor_tensor(gcr[:, :], sc1[:, 0:1], sc2[:, 0:1],
                                   ALU.subtract)
                kb.v.tensor_tensor(gci[:, :], sc3[:, 0:1], sc4[:, 0:1],
                                   ALU.add)
                kb.v.drain()
        for tc in range(8):
            kb.v.wait(pwh, pwh.n)
            for half in range(1):
                s_ = slice(tc * 512, (tc + 1) * 512)
                o_ = slice(0, 512)
                kb.v.tensor_tensor(sc1[:, o_], gr[:, s_], cs[:, 0:RC], ALU.mult)
                kb.v.tensor_tensor(sc2[:, o_], gi[:, s_], sn[:, 0:RC], ALU.mult)
                kb.v.tensor_tensor(sc3[:, o_], gr[:, s_], sn[:, 0:RC], ALU.mult)
                kb.v.tensor_tensor(sc4[:, o_], gi[:, s_], cs[:, 0:RC], ALU.mult)
            kb.v.drain()
            kb.v.tensor_tensor(sc1[:, 0:512], sc1[:, 0:512], sc2[:, 0:512],
                               ALU.subtract)
            kb.v.tensor_tensor(sc3[:, 0:512], sc3[:, 0:512], sc4[:, 0:512],
                               ALU.add)
            kb.v.drain()
            kb.v.tensor_copy(hb[:, 0:512], sc1[:, 0:512])
            kb.v.tensor_copy(hb[:, 512:1024], sc3[:, 0:512])
            kb.v.drain()
            kb.v.tensor_tensor(hb[:, 1024:1536], sc1[:, 0:512], hb[:, 0:512],
                               ALU.subtract)
            kb.v.tensor_tensor(hb[:, 1536:2048], sc3[:, 0:512], hb[:, 512:1024],
                               ALU.subtract, inc=(ev, 1))
            kb.y.wait(ev, ev.n)
            kb.y.dma_start(out=hpark[:, pg, :, tc * 512:(tc + 1) * 512]
                           .rearrange("e p t -> p e t"),
                           in_=hb[:, :].rearrange("p (e t) -> p e t", t=512),
                           inc=(pwh, 16))
    kb.stage_end([pwh])
    kb.free("E_pu1", "E_pu0", "E_brow", "E_wine1", "E_wine0", "E_hb",
            "E_hb" if False else "E_gci", "E_gcr",
            "E_sc4", "E_sc3", "E_sc2", "E_sc1", "E_rhor", "E_sn", "E_cs",
            "E_gi", "E_gr", "E_ui", "E_ur", "E_xn1", "E_tsc", "E_sth8",
            "E_cth8", "E_ga8", "E_th8", "E_rho8", "E_nu8", "E_pp", "E_plog")

    wm0 = kb.sbuf("E_wm0", [128, 8192], BF16)
    wm1 = kb.sbuf("E_wm1", [128, 8192], BF16)
    wi2 = kb.sbuf("E_wi2", [128, 8192], BF16)
    w1s = kb.sbuf("E_w1s", [128, 8192], BF16)
    w2s = kb.sbuf("E_w2s", [128, 8192], BF16)
    brows = kb.sbuf("E_brows", [1, 4 * 1024], F32)
    bcomb = kb.sbuf("E_bcomb", [1, 1024], BF16)
    w1br = kb.sbuf("E_w1br", [1, 1024], BF16)
    w2br = kb.sbuf("E_w2br", [1, 1024], BF16)
    gcols = kb.sbuf("E_gcols", [128, 8], F32)
    bcols = kb.sbuf("E_bcols", [128, 8], F32)
    hrt = kb.sbuf("E_hrt", [128, 4096], BF16)
    hit = kb.sbuf("E_hit", [128, 4096], BF16)
    hrtl = kb.sbuf("E_hrtl", [128, 4096], BF16)
    hitl = kb.sbuf("E_hitl", [128, 4096], BF16)
    xntc = kb.sbuf("E_xntc", [128, 4096], BF16)
    outb = kb.sbuf("E_outb", [128, 4096], BF16)
    outlo = kb.sbuf("E_outlo", [128, 4096], BF16)
    y1s = kb.sbuf("E_y1s", [128, 4096], BF16)
    ysb = kb.sbuf("E_ysb", [128, 4096], F32)
    ysq = kb.sbuf("E_ysq", [128, 512], F32)
    mrow = kb.sbuf("E_mrow", [1, 1024], F32)
    mub = kb.sbuf("E_mub", [128, 512], F32)
    rsb = kb.sbuf("E_rsb", [128, 512], F32)
    s1t = kb.sbuf("E_s1t", [128, 512], BF16)
    t1 = kb.sbuf("E_t1", [128, 512], F32)
    po2 = [kb.psum(f"E_po{i}", [128, 512], F32) for i in range(2)]
    pst2 = kb.psum("E_pst2", [1, 1024], F32)
    pb2 = kb.psum("E_pb2", [128, 1024], F32)
    lw2 = kb.sem("E_lw2")
    lh = kb.sem("E_lh")
    rvv = kb.sem("E_rvv")
    scc = kb.sem("E_scc")
    psw = kb.sem("E_psw")

    for name, srcw in (("E_wm0", wmid[0]), ("E_wm1", wmid[1]),
                       ("E_wi2", win[2]), ("E_w1s", w1), ("E_w2s", w2)):
        dst = {"E_wm0": wm0, "E_wm1": wm1, "E_wi2": wi2, "E_w1s": w1s,
               "E_w2s": w2s}[name]
        kb.g.dma_start(out=dst[:, :].rearrange("p (g h) -> p g h", h=1024),
                       in_=srcw.rearrange("(g p) h -> p g h", p=128),
                       inc=(lw2, 16))
    kb.y.dma_start(out=brows[:, 0:1024], in_=bmid[0:1, :], inc=(lh, 16))
    kb.y.dma_start(out=brows[:, 1024:2048], in_=bmid[1:2, :], inc=(lh, 16))
    kb.y.dma_start(out=brows[:, 2048:3072], in_=bin_[2:3, :], inc=(lh, 16))
    kb.g.dma_start(out=w1br[:, :], in_=w1b, inc=(lw2, 16))
    kb.g.dma_start(out=w2br[:, :], in_=w2b, inc=(lw2, 16))
    kb.g.dma_start(out=gcols[:, :],
                   in_=bass.AP(lng.tensor, 0, [[1, 128], [128, 8]]),
                   allow_slow_non_contiguous=True, inc=(lw2, 16))
    kb.g.dma_start(out=bcols[:, :],
                   in_=bass.AP(lnb.tensor, 0, [[1, 128], [128, 8]]),
                   allow_slow_non_contiguous=True, inc=(lw2, 16))
    kb.v.wait(lw2, lw2.n)
    kb.v.tensor_scalar(wm1[:, :], wm1[:, :], -1.0, None, ALU.mult, inc=(ev, 1))
    kb.v.wait(lh, lh.n)
    kb.v.tensor_tensor(brows[:, 3072:4096], brows[:, 0:1024],
                       brows[:, 1024:2048], ALU.subtract)
    kb.v.drain()
    kb.v.tensor_tensor(brows[:, 3072:4096], brows[:, 3072:4096],
                       brows[:, 2048:3072], ALU.add)
    kb.v.drain()
    kb.v.tensor_copy(bcomb[:, :], brows[:, 3072:4096], inc=(ev, 1))

    for tc in range(8):
        ts_ = slice(tc * 512, (tc + 1) * 512)
        kb.y.wait(mm, mm.n)
        kb.y.dma_start(out=hrt[:, :].rearrange("p (g t) -> p g t", t=512),
                       in_=hpark[0, :, :, ts_].rearrange("g p t -> p g t"),
                       inc=(lh, 16))
        kb.y.dma_start(out=hit[:, :].rearrange("p (g t) -> p g t", t=512),
                       in_=hpark[1, :, :, ts_].rearrange("g p t -> p g t"),
                       inc=(lh, 16))
        kb.y.dma_start(out=hrtl[:, :].rearrange("p (g t) -> p g t", t=512),
                       in_=hpark[2, :, :, ts_].rearrange("g p t -> p g t"),
                       inc=(lh, 16))
        kb.y.dma_start(out=hitl[:, :].rearrange("p (g t) -> p g t", t=512),
                       in_=hpark[3, :, :, ts_].rearrange("g p t -> p g t"),
                       inc=(lh, 16))
        kb.y.dma_start(out=xntc[:, :].rearrange("p (g t) -> p g t", t=512),
                       in_=xn1[:, :, ts_].rearrange("g p t -> p g t"),
                       inc=(lh, 16))
        kb.t.wait(lh, lh.n)
        kb.t.wait(ev, ev.n)
        for hm in range(NG):
            pb = po2[hm % 2]
            kb.t.wait(rvv, max(0, rvv.n - 1))
            for g in range(NG):
                kb.t.matmul(pb[:, :], wm0[:, g * 1024 + hm * 128:g * 1024 + (hm + 1) * 128],
                            hrt[:, g * 512:(g + 1) * 512],
                            start=(g == 0), stop=False)
            for g in range(NG):
                kb.t.matmul(pb[:, :], wm0[:, g * 1024 + hm * 128:g * 1024 + (hm + 1) * 128],
                            hrtl[:, g * 512:(g + 1) * 512],
                            start=False, stop=False)
            for g in range(NG):
                kb.t.matmul(pb[:, :], wm1[:, g * 1024 + hm * 128:g * 1024 + (hm + 1) * 128],
                            hit[:, g * 512:(g + 1) * 512],
                            start=False, stop=False)
            for g in range(NG):
                kb.t.matmul(pb[:, :], wm1[:, g * 1024 + hm * 128:g * 1024 + (hm + 1) * 128],
                            hitl[:, g * 512:(g + 1) * 512],
                            start=False, stop=False)
            for g in range(NG):
                kb.t.matmul(pb[:, :], wi2[:, g * 1024 + hm * 128:g * 1024 + (hm + 1) * 128],
                            xntc[:, g * 512:(g + 1) * 512],
                            start=False, stop=False)
            kb.t.matmul(pb[:, :], bcomb[:, hm * 128:(hm + 1) * 128],
                        onesrow[:, 0:512], start=False, stop=True, inc=(mm, 1))
            kb.v.wait(mm, mm.n)
            kb.v.tensor_copy(outb[:, hm * 512:(hm + 1) * 512], pb[:, :])
            kb.v.drain()
            kb.v.tensor_tensor(outlo[:, hm * 512:(hm + 1) * 512], pb[:, :],
                               outb[:, hm * 512:(hm + 1) * 512], ALU.subtract,
                               inc=(rvv, 1))
        for jsl in range(NG):
            pb = po2[jsl % 2]
            kb.t.wait(rvv, rvv.n)
            kb.t.wait(scc, max(0, scc.n - 1))
            for g in range(NG):
                kb.t.matmul(pb[:, :], w1s[:, g * 1024 + jsl * 128:g * 1024 + (jsl + 1) * 128],
                            outb[:, g * 512:(g + 1) * 512],
                            start=(g == 0), stop=False)
            kb.t.matmul(pb[:, :], w1br[:, jsl * 128:(jsl + 1) * 128],
                        onesrow[:, 0:512], start=False, stop=True, inc=(mm, 1))
            kb.s.wait(mm, mm.n)
            kb.s.activation(y1s[:, jsl * 512:(jsl + 1) * 512], pb[:, :],
                            AF.Gelu, inc=(scc, 1))
        for ds in range(NG):
            pb = po2[ds % 2]
            kb.t.wait(scc, scc.n)
            kb.t.wait(rvv, max(0, rvv.n - 1))
            for g in range(NG):
                kb.t.matmul(pb[:, :], w2s[:, g * 1024 + ds * 128:g * 1024 + (ds + 1) * 128],
                            y1s[:, g * 512:(g + 1) * 512],
                            start=(g == 0), stop=False)
            kb.t.matmul(pb[:, :], w2br[:, ds * 128:(ds + 1) * 128],
                        onesrow[:, 0:512], start=False, stop=True, inc=(mm, 1))
            kb.v.wait(mm, mm.n)
            kb.v.tensor_tensor(ysb[:, ds * 512:(ds + 1) * 512], pb[:, :],
                               outb[:, ds * 512:(ds + 1) * 512], ALU.add)
            kb.v.drain()
            kb.v.tensor_tensor(ysb[:, ds * 512:(ds + 1) * 512],
                               ysb[:, ds * 512:(ds + 1) * 512],
                               outlo[:, ds * 512:(ds + 1) * 512], ALU.add,
                               inc=(rvv, 1))
            kb.s.wait(rvv, rvv.n)
            kb.s.activation(ysq[:, :], ysb[:, ds * 512:(ds + 1) * 512],
                            AF.Square, inc=(scc, 1))
            kb.t.wait(rvv, rvv.n)
            kb.t.matmul(pst2[0:1, 0:512], onesf[:, :],
                        ysb[:, ds * 512:(ds + 1) * 512],
                        start=(ds == 0), stop=(ds == NG - 1))
            kb.t.wait(scc, scc.n)
            kb.t.matmul(pst2[0:1, 512:1024], onesf[:, :], ysq[:, :],
                        start=(ds == 0), stop=(ds == NG - 1), inc=(mm, 1))
        kb.v.wait(mm, mm.n)
        kb.v.tensor_scalar(mrow[:, :], pst2[0:1, :], 1.0 / D, None, ALU.mult,
                           inc=(rvv, 1))
        kb.v.drain()
        kb.v.tensor_tensor(ysq[0:1, 0:512], mrow[:, 0:512], mrow[:, 0:512],
                           ALU.mult)
        kb.v.drain()
        kb.v.tensor_tensor(mrow[:, 512:1024], mrow[:, 512:1024],
                           ysq[0:1, 0:512], ALU.subtract, inc=(rvv, 1))
        kb.s.wait(rvv, rvv.n)
        kb.s.activation(ysq[0:1, 0:512], mrow[:, 512:1024], AF.Sqrt,
                        bias=epscol[0:1, :], inc=(scc, 1))
        kb.v.wait(scc, scc.n)
        kb.v.reciprocal(mrow[:, 512:1024], ysq[0:1, 0:512], inc=(rvv, 1))
        kb.t.wait(rvv, rvv.n)
        kb.t.matmul(pb2[:, 0:512], ones1f[:, :], mrow[:, 0:512], start=True,
                    stop=True)
        kb.t.matmul(pb2[:, 512:1024], ones1f[:, :], mrow[:, 512:1024],
                    start=True, stop=True, inc=(mm, 1))
        kb.v.wait(mm, mm.n)
        kb.v.tensor_copy(mub[:, :], pb2[:, 0:512])
        kb.v.tensor_copy(rsb[:, :], pb2[:, 512:1024])
        kb.v.drain()
        for ds in range(NG):
            kb.v.wait(psw, psw.n)
            kb.v.tensor_tensor(t1[:, :], ysb[:, ds * 512:(ds + 1) * 512],
                               mub[:, :], ALU.subtract)
            kb.v.drain()
            kb.v.tensor_tensor(t1[:, :], t1[:, :], rsb[:, :], ALU.mult)
            kb.v.drain()
            kb.v.tensor_scalar(s1t[:, :], t1[:, :], gcols[:, ds:ds + 1],
                               bcols[:, ds:ds + 1], ALU.mult, ALU.add,
                               inc=(rvv, 1))
            kb.y.wait(rvv, rvv.n)
            kb.y.dma_start(out=s1[ds, :, ts_], in_=s1t[:, :], inc=(psw, 16))
        kb.t.wait(rvv, rvv.n)

    kb.stage_end([psw])
    kb.free_all_bufs()



GELU_C = float(np.sqrt(2.0 / np.pi))
GELU_S = float(np.sqrt(0.044715))


def stage_F(kb, Tn):
    xn1, x1_fm, s1 = Tn["xn1"], Tn["x1_fm"], Tn["s1"]
    wfcs, bfcs = Tn["m_wfcs"], Tn["m_bfcs"]
    wprojs, bprojs = Tn["m_wprojs"], Tn["m_bprojs"]
    wr = Tn["m_wr"]
    wfc, bfc = Tn["m_wfc"], Tn["m_bfc"]
    wproj, bproj = Tn["m_wproj"], Tn["m_bproj"]
    out = Tn["out"]
    nc = kb.nc
    gate_d = nc.dram_tensor("gate_d", [8, T], F32, kind="Internal").ap()

    identf, sidf = build_identity(kb, "F_idf", F32)
    ones1f = kb.sbuf("F_o1f", [1, 128], F32)
    ones1b = kb.sbuf("F_o1b", [1, 128], BF16)
    onesrow = kb.sbuf("F_orow", [1, 512], BF16)
    msem = kb.sem("F_ms")
    kb.v.memset(ones1f[:, :], 1.0)
    kb.v.memset(ones1b[:, :], 1.0)
    kb.v.memset(onesrow[:, :], 1.0, inc=(msem, 1))

    wr_sb = kb.sbuf("F_wr", [128, 64], BF16)
    gate_fm = kb.sbuf("F_gfm", [8, T], F32)
    xg = [kb.sbuf(f"F_xg{i}", [128, 1024], BF16) for i in range(2)]
    gt = kb.sbuf("F_gt", [128, 24], F32)
    gtm = kb.sbuf("F_gtm", [128, 8], F32)
    pl = kb.psum("F_pl", [128, 512], F32)
    pgt = kb.psum("F_pgt", [8, 512], F32)
    lg = [kb.sem(f"F_lg{i}") for i in range(2)]
    mm = kb.sem("F_mm")
    ev = kb.sem("F_ev")
    sc = kb.sem("F_sc")

    kb.g.dma_start(out=wr_sb[:, :].rearrange("p (g e) -> p g e", e=8),
                   in_=wr.rearrange("(g p) e -> p g e", p=128), inc=(lg[0], 16))
    for tt in range(32):
        b = tt % 2
        kb.y.wait(mm, mm.n)
        kb.y.dma_start(out=xg[b][:, :].rearrange("p (g t) -> p g t", t=128),
                       in_=xn1[:, :, tt * 128:(tt + 1) * 128].rearrange(
                           "g p t -> p g t"), inc=(lg[b], 16))
        kb.t.wait(lg[b], lg[b].n)
        if tt == 0:
            kb.t.wait(lg[0], lg[0].n)
        kb.t.wait(ev, ev.n)
        for g in range(NG):
            kb.t.matmul(pl[0:128, 0:8], xg[b][:, g * 128:(g + 1) * 128],
                        wr_sb[:, g * 8:(g + 1) * 8],
                        start=(g == 0), stop=(g == NG - 1), inc=(mm, 1))
        kb.v.wait(mm, mm.n)
        kb.v.tensor_copy(gt[:, 0:8], pl[0:128, 0:8])
        kb.v.drain()
        kb.v.reduce_max(gt[:, 8:9], gt[:, 0:8], axis=AX.X)
        kb.v.drain()
        kb.v.tensor_scalar(gt[:, 9:10], gt[:, 8:9], -1.0, None, ALU.mult)
        kb.v.tensor_scalar(gt[:, 16:24], gt[:, 0:8], gt[:, 8:9], None,
                           ALU.is_ge)
        kb.v.drain()
        kb.v.scalar_tensor_tensor(gt[:, 16:24], gt[:, 16:24], -1e30,
                                  gt[:, 0:8], ALU.mult, ALU.add)
        kb.v.drain()
        kb.v.reduce_max(gt[:, 10:11], gt[:, 16:24], axis=AX.X, inc=(ev, 1))
        kb.s.wait(ev, ev.n)
        kb.s.activation(gt[:, 16:24], gt[:, 0:8], AF.Exp, bias=gt[:, 9:10],
                        inc=(sc, 1))
        kb.v.wait(sc, sc.n)
        kb.v.tensor_scalar(gt[:, 0:8], gt[:, 0:8], gt[:, 10:11], None,
                           ALU.is_ge)
        kb.v.drain()
        kb.v.tensor_tensor(gtm[:, :], gt[:, 16:24], gt[:, 0:8], ALU.mult)
        kb.v.drain()
        kb.v.reduce_sum(gt[:, 11:12], gtm[:, :], axis=AX.X)
        kb.v.drain()
        kb.v.reciprocal(gt[:, 12:13], gt[:, 11:12])
        kb.v.drain()
        kb.v.tensor_scalar(gtm[:, :], gtm[:, :], gt[:, 12:13], None, ALU.mult,
                           inc=(ev, 1))
        kb.t.wait(ev, ev.n)
        kb.t.transpose(pgt[0:8, 0:128], gtm[:, :], identf[:, :], inc=(mm, 1))
        kb.v.wait(mm, mm.n)
        kb.v.tensor_copy(gate_fm[:, tt * 128:(tt + 1) * 128], pgt[0:8, 0:128],
                         inc=(ev, 1))
        kb.t.wait(ev, ev.n)
    kb.y.wait(ev, ev.n)
    gw = kb.sem("F_gw")
    kb.y.dma_start(out=gate_d, in_=gate_fm[:, :], inc=(gw, 16))
    kb.stage_end([gw])
    kb.free("F_pgt", "F_pl", "F_gtm", "F_gt", "F_xg1", "F_xg0", "F_gfm")

    TH = 2048
    routed = kb.sbuf("F_rout", [128, NG * TH], F32)
    xnth = kb.sbuf("F_xnth", [128, NG * TH], BF16)
    wfcq = [kb.sbuf(f"F_wfcq{i}", [128, 8192], BF16) for i in range(2)]
    wprojq = [kb.sbuf(f"F_wprojq{i}", [128, 8192], BF16) for i in range(2)]
    hq = [kb.sbuf(f"F_hq{i}", [128, 8 * 512], BF16) for i in range(2)]
    gb = kb.sbuf("F_gb", [128, TH], BF16)
    grow = kb.sbuf("F_grow", [1, TH], BF16)
    bfcr = [kb.sbuf(f"F_bfcr{i}", [1, 1024], BF16) for i in range(2)]
    bprojr0 = kb.sbuf("F_bprojr", [1, 1024], BF16)
    bprojr = [bprojr0, bprojr0]
    asb = [kb.sbuf(f"F_asb{i}", [128, 512], F32) for i in range(2)]
    bsb = [kb.sbuf(f"F_bsb{i}", [128, 512], F32) for i in range(2)]
    usb = [kb.sbuf(f"F_usb{i}", [128, 512], F32) for i in range(2)]
    g1t = kb.sbuf("F_g1t", [128, 512], BF16)
    s1t = kb.sbuf("F_s1t", [128, 512], BF16)
    x1t = asb[0]
    x2t = bsb[0]
    ost = usb[0]
    ph = [kb.psum(f"F_ph{i}", [128, 512], F32) for i in range(4)]
    pp = [kb.psum(f"F_pp{i}", [128, 512], F32) for i in range(2)]
    pgb = kb.psum("F_pgb", [128, 512], F32)
    ptr = pgb
    lw = [kb.sem(f"F_lw{i}") for i in range(2)]
    vch = kb.sem("F_vch")
    lx = kb.sem("F_lx")
    lxs = kb.sem("F_lxs")
    rvv = kb.sem("F_rvv")
    scc = kb.sem("F_scc")
    pow_ = kb.sem("F_pow")

    iters = [(th, e, jq) for th in range(2) for e in range(9)
             for jq in range(4)]
    itr = kb.sem("F_itr")

    def issue_load(k, part):
        th, e, jq = iters[k]
        b = k % 2
        if part == 0:
            kb.g.wait(itr, max(0, k - 1))
            wsrc = wfcs if e == 0 else wfc[e - 1]
            kb.g.dma_start(
                out=wfcq[b][:, :].rearrange("p (g j) -> p g j", j=1024),
                in_=wsrc[:, jq * 1024:(jq + 1) * 1024].rearrange(
                    "(g p) j -> p g j", p=128), inc=(lw[b], 16))
        elif part == 1:
            psrc = wprojs if e == 0 else wproj[e - 1]
            kb.g.dma_start(
                out=wprojq[b][:, :].rearrange("p (c d) -> p c d", d=1024),
                in_=psrc[jq * 1024:(jq + 1) * 1024, :].rearrange(
                    "(c p) d -> p c d", p=128), inc=(lw[b], 16))
        else:
            bsrc = bfcs if e == 0 else bfc[e - 1:e]
            kb.g.dma_start(out=bfcr[b][:, :],
                           in_=bsrc[0:1, jq * 1024:(jq + 1) * 1024],
                           inc=(lw[b], 16))
            if jq == 0:
                bp = bprojs if e == 0 else bproj[e - 1:e]
                kb.g.dma_start(out=bprojr0[:, :], in_=bp[0:1, :],
                               inc=(lw[b], 16))

    for part in range(3):
        issue_load(0, part)
    lw_hist = [lw[0].n]
    for k, (th, e, jq) in enumerate(iters):
        b = k % 2
        t0 = th * TH
        if jq == 0:
            if e == 0:
                kb.y.wait(mm, mm.n)
                kb.y.dma_start(
                    out=xnth[:, :].rearrange("p (g t) -> p g t", t=TH),
                    in_=xn1[:, :, t0:t0 + TH].rearrange("g p t -> p g t"),
                    inc=(lx, 16))
                kb.v.wait(scc, scc.n)
                kb.v.memset(gb[:, :], 0.5, inc=(ev, 1))
            else:
                kb.g.wait(mm, mm.n)
                kb.g.dma_start(out=grow[:, :],
                               in_=gate_d[e - 1:e, t0:t0 + TH], inc=(ev, 16))
                for q in range(4):
                    kb.t.wait(ev, ev.n)
                    kb.t.wait(rvv, rvv.n)
                    kb.t.matmul(pgb[:, :], ones1b[:, :],
                                grow[:, q * 512:(q + 1) * 512], start=True,
                                stop=True, inc=(mm, 1))
                    kb.v.wait(mm, mm.n)
                    kb.v.wait(scc, scc.n)
                    kb.v.tensor_scalar(gb[:, q * 512:(q + 1) * 512], pgb[:, :],
                                       0.5, None, ALU.mult, inc=(rvv, 1))
        fc_scc = {}

        def fc_block(tc4):
            c0 = tc4 * 512
            hb_ = hq[tc4 % 2]
            kb.t.wait(lw[b], lw_hist[k])
            if e == 0 and jq == 0 and tc4 == 0:
                kb.t.wait(lx, lx.n)
                kb.t.wait(msem)
                kb.v.wait(pow_, pow_.n)
            for jsl in range(NG):
                i2 = (tc4 * NG + jsl) % 2
                pb = ph[(tc4 * NG + jsl) % 4]
                kb.t.wait(scc, max(0, scc.n - 3))
                for g in range(NG):
                    kb.t.matmul(
                        pb[:, :],
                        wfcq[b][:, g * 1024 + jsl * 128:g * 1024 + (jsl + 1) * 128],
                        xnth[:, g * TH + c0:g * TH + c0 + 512],
                        start=(g == 0), stop=False)
                kb.t.matmul(pb[:, :], bfcr[b][:, jsl * 128:(jsl + 1) * 128],
                            onesrow[:, 0:512], start=False, stop=True,
                            inc=(mm, 1))
                kb.s.wait(mm, mm.n)
                kb.s.wait(vch, max(0, vch.n - 2))
                kb.s.activation(asb[i2][:, :], pb[:, :], AF.Square,
                                scale=GELU_S, inc=(sc, 1))
                kb.v.wait(sc, sc.n)
                kb.v.scalar_tensor_tensor(bsb[i2][:, :], asb[i2][:, :], 1.0,
                                          pb[:, :], ALU.add, ALU.mult,
                                          inc=(vch, 1))
                kb.s.wait(vch, vch.n)
                kb.s.activation(asb[i2][:, :], bsb[i2][:, :], AF.Tanh,
                                scale=GELU_C, inc=(sc, 1))
                kb.v.wait(sc, sc.n)
                kb.v.wait(scc, max(0, scc.n - 1))
                kb.v.scalar_tensor_tensor(usb[i2][:, :], asb[i2][:, :], 1.0,
                                          pb[:, :], ALU.add, ALU.mult,
                                          inc=(vch, 1))
                kb.g.wait(vch, vch.n)
                kb.g.wait(mm, mm.n)
                kb.g.tensor_tensor(hb_[:, jsl * 512:(jsl + 1) * 512],
                                   usb[i2][:, :], gb[:, c0:c0 + 512], ALU.mult,
                                   inc=(scc, 1))
            fc_scc[tc4] = scc.n

        def proj_block(tc4):
            c0 = tc4 * 512
            hb_ = hq[tc4 % 2]
            for ds in range(NG):
                pb = pp[ds % 2]
                kb.t.wait(scc, fc_scc[tc4])
                kb.t.wait(rvv, max(0, rvv.n - 1))
                for jc in range(NG):
                    last = (jq != 0 and jc == NG - 1)
                    kb.t.matmul(
                        pb[:, :],
                        wprojq[b][:, jc * 1024 + ds * 128:jc * 1024 + (ds + 1) * 128],
                        hb_[:, jc * 512:(jc + 1) * 512],
                        start=(jc == 0), stop=last,
                        inc=(mm, 1) if last else None)
                if jq == 0:
                    rrhs = onesrow[:, 0:512] if e == 0 else \
                        grow[:, c0:c0 + 512]
                    kb.t.matmul(pb[:, :],
                                bprojr[e % 2][:, ds * 128:(ds + 1) * 128],
                                rrhs, start=False, stop=True, inc=(mm, 1))
                if tc4 == 3 and ds == NG - 1:
                    kb.t.nop(inc=(itr, 1))
                rsl = routed[:, ds * TH + c0:ds * TH + c0 + 512]
                kb.v.wait(mm, mm.n)
                if e == 0 and jq == 0:
                    kb.v.tensor_copy(rsl, pb[:, :], inc=(rvv, 1))
                else:
                    kb.v.tensor_tensor(rsl, rsl, pb[:, :], ALU.add,
                                       inc=(rvv, 1))

        fc_block(0)
        if k + 1 < len(iters):
            issue_load(k + 1, 0)
        fc_block(1)
        proj_block(0)
        if k + 1 < len(iters):
            issue_load(k + 1, 1)
        fc_block(2)
        proj_block(1)
        if k + 1 < len(iters):
            issue_load(k + 1, 2)
            lw_hist.append(lw[(k + 1) % 2].n)
        fc_block(3)
        proj_block(2)
        proj_block(3)
        if jq == 3 and e == 8:
            kb.t.wait(sidf)
            for ds in range(NG):
                for tc4 in range(4):
                    c0 = tc4 * 512
                    kb.y.wait(rvv, rvv.n)
                    kb.y.wait(scc, scc.n)
                    kb.y.wait(vch, vch.n)
                    kb.y.dma_start(out=x1t[:, :],
                                   in_=x1_fm[ds, :, t0 + c0:t0 + c0 + 512],
                                   inc=(lxs, 16))
                    kb.y.dma_start(out=s1t[:, :],
                                   in_=s1[ds, :, t0 + c0:t0 + c0 + 512],
                                   inc=(lxs, 16))
                    kb.s.wait(rvv, rvv.n)
                    kb.s.wait(scc, scc.n)
                    kb.s.activation(g1t[:, :],
                                    routed[:, ds * TH + c0:ds * TH + c0 + 512],
                                    AF.Silu, inc=(sc, 1))
                    kb.v.wait(sc, sc.n)
                    kb.v.wait(lxs, lxs.n)
                    kb.v.tensor_tensor(x2t[:, :], s1t[:, :], g1t[:, :],
                                       ALU.mult)
                    kb.v.drain()
                    kb.v.tensor_tensor(x2t[:, :], x2t[:, :], x1t[:, :],
                                       ALU.add, inc=(rvv, 1))
                    for q in range(4):
                        kb.t.wait(rvv, rvv.n)
                        kb.t.wait(ev, max(0, ev.n))
                        kb.t.transpose(ptr[:, 0:128],
                                       x2t[:, q * 128:(q + 1) * 128],
                                       identf[:, :], inc=(mm, 1))
                        kb.v.wait(mm, mm.n)
                        kb.v.tensor_copy(ost[:, q * 128:(q + 1) * 128],
                                         ptr[:, 0:128], inc=(ev, 1))
                    kb.y.wait(ev, ev.n)
                    src_o = ost[:, :].rearrange("p (q j) -> p q j", j=128)
                    dst_o = bass.AP(out.tensor, (t0 + c0) * D + ds * 128,
                                    [[D, 128], [128 * D, 4], [1, 128]])
                    kb.y.dma_start(out=dst_o, in_=src_o, inc=(pow_, 16))

    kb.stage_end([pow_])
    kb.free_all_bufs()



def core_input_arrays(inputs, core):
    A = {k: np.asarray(v) for k, v in inputs.items()}
    d = {}
    d["x"] = np.ascontiguousarray(A["L0_attn_x"][core])
    d["wq_t"] = np.ascontiguousarray(A["L0_attn_Wq"].T)
    d["wk_t"] = np.ascontiguousarray(A["L0_attn_Wk"].T)
    d["wv_t"] = np.ascontiguousarray(A["L0_attn_Wv"].T)
    d["rbt"] = np.ascontiguousarray(A["L0_attn_rel_bias_table"])
    d["l0_lng"] = A["L0_attn_ln_gamma"].reshape(1, D)
    d["l0_lnb"] = A["L0_attn_ln_beta"].reshape(1, D)
    d["fb_t"] = np.ascontiguousarray(A["L0_ffn_fast_b"].T)
    d["wfc"] = np.ascontiguousarray(A["L0_ffn_w_fc"])
    d["wproj"] = np.ascontiguousarray(A["L0_ffn_w_proj"])
    d["u_t"] = np.ascontiguousarray(A["L0_ffn_u"].T)
    d["v_t"] = np.ascontiguousarray(A["L0_ffn_v"].T)
    d["l0_gate"] = A["L0_ffn_gate"].reshape(1, 1)
    d["l1_win"] = np.ascontiguousarray(A["L1_attn_W_in"])
    d["l1_bin"] = np.ascontiguousarray(A["L1_attn_b_in"])
    d["l1_plog"] = np.ascontiguousarray(A["L1_attn_params_log"])
    d["l1_wmid"] = np.ascontiguousarray(A["L1_attn_W_mid"])
    d["l1_bmid"] = np.ascontiguousarray(A["L1_attn_b_mid"])
    d["l1_w1"] = np.ascontiguousarray(A["L1_attn_w1_w"])
    d["l1_w1b"] = A["L1_attn_w1_b"].reshape(1, D)
    d["l1_w2"] = np.ascontiguousarray(A["L1_attn_w2_w"])
    d["l1_w2b"] = A["L1_attn_w2_b"].reshape(1, D)
    d["l1_lng"] = A["L1_attn_ln_g"].reshape(1, D)
    d["l1_lnb"] = A["L1_attn_ln_b"].reshape(1, D)
    d["m_wfcs"] = np.ascontiguousarray(A["L1_ffn_Wfc_s"])
    d["m_bfcs"] = A["L1_ffn_bfc_s"].reshape(1, 4096)
    d["m_wprojs"] = np.ascontiguousarray(A["L1_ffn_Wproj_s"])
    d["m_bprojs"] = A["L1_ffn_bproj_s"].reshape(1, D)
    d["m_wr"] = np.ascontiguousarray(A["L1_ffn_Wr"])
    d["m_wfc"] = np.ascontiguousarray(A["L1_ffn_Wfc"])
    d["m_bfc"] = np.ascontiguousarray(A["L1_ffn_bfc"])
    d["m_wproj"] = np.ascontiguousarray(A["L1_ffn_Wproj"])
    d["m_bproj"] = np.ascontiguousarray(A["L1_ffn_bproj"])
    return d


PARK_SPECS = {
    "x_fm": ([NG, 128, T], F32),
    "xn0": ([NG, 128, T], BF16),
    "s0": ([NG, 128, T], BF16),
    "s0stats": ([2, T], F32),
    "x1_fm": ([NG, 128, T], F32),
    "x1stats": ([2, T], F32),
    "xn1": ([NG, 128, T], BF16),
    "s1": ([NG, 128, T], BF16),
}

INPUT_SPECS = {
    "x": ([T, D], F32),
    "wq_t": ([D, D], F32), "wk_t": ([D, D], F32), "wv_t": ([D, D], F32),
    "rbt": ([4095, 16], F32),
    "l0_lng": ([1, D], F32), "l0_lnb": ([1, D], F32),
    "fb_t": ([16, 16], F32), "wfc": ([4096, D], F32), "wproj": ([D, 4096], F32),
    "u_t": ([16, 4096], F32), "v_t": ([D, 16], F32), "l0_gate": ([1, 1], F32),
    "l1_win": ([3, D, D], F32), "l1_bin": ([3, D], F32), "l1_plog": ([3, D], F32),
    "l1_wmid": ([2, D, D], F32), "l1_bmid": ([2, D], F32),
    "l1_w1": ([D, D], F32), "l1_w1b": ([1, D], F32),
    "l1_w2": ([D, D], F32), "l1_w2b": ([1, D], F32),
    "l1_lng": ([1, D], F32), "l1_lnb": ([1, D], F32),
    "m_wfcs": ([D, 4096], F32), "m_bfcs": ([1, 4096], F32),
    "m_wprojs": ([4096, D], F32), "m_bprojs": ([1, D], F32),
    "m_wr": ([D, 8], F32),
    "m_wfc": ([8, D, 4096], F32), "m_bfc": ([8, 4096], F32),
    "m_wproj": ([8, 4096, D], F32), "m_bproj": ([8, D], F32),
}


def declare(nc, names, kinds):
    out = {}
    specs = dict(INPUT_SPECS)
    specs.update(PARK_SPECS)
    specs["out"] = ([T, D], F32)
    for nm in names:
        shape, dt = specs[nm]
        kind = {"in": "ExternalInput", "out": "ExternalOutput",
                "internal": "Internal"}[kinds[nm]]
        out[nm] = nc.dram_tensor(nm, shape, dt, kind=kind).ap()
    return out


STAGE_IO = [
    (stage_A, ["x"], ["x_fm", "xn0"]),
    (stage_B, ["xn0", "wq_t", "wk_t", "wv_t", "rbt"], ["s0"]),
    (stage_C, ["xn0", "s0", "x_fm", "wfc", "wproj", "u_t", "v_t", "fb_t",
               "l0_gate", "l0_lng", "l0_lnb"], ["x1_fm", "x1stats", "xn1"]),
    (stage_E, ["xn1", "l1_win", "l1_bin", "l1_plog", "l1_wmid", "l1_bmid",
               "l1_w1", "l1_w1b", "l1_w2", "l1_w2b", "l1_lng", "l1_lnb"],
     ["s1"]),
    (stage_F, ["xn1", "x1_fm", "s1", "m_wfcs", "m_bfcs", "m_wprojs",
               "m_bprojs", "m_wr", "m_wfc", "m_bfc", "m_wproj", "m_bproj"],
     ["out"]),
]


FUSED = True


def kernel(**inputs):
    per_core = [core_input_arrays(inputs, c) for c in range(NCORES)]
    if not FUSED:
        acc = [dict() for _ in range(NCORES)]
        for fn, ins, outs in STAGE_IO:
            nc = bass.Bass("TRN2", target_bir_lowering=False)
            kb = KB(nc)
            kinds = {n: "in" for n in ins}
            for n in outs:
                kinds[n] = "out"
            Tn = declare(nc, ins + outs, kinds)
            fn(kb, Tn)
            kb.emit()
            kb.close()
            maps = [{n: (acc[c][n] if n in acc[c] else per_core[c][n])
                     for n in ins} for c in range(NCORES)]
            res = run_bass_kernel_spmd(nc, maps, core_ids=list(range(NCORES)))
            for c in range(NCORES):
                for n in outs:
                    acc[c][n] = np.asarray(res.results[c][n])
        return np.stack([acc[c]["out"] for c in range(NCORES)])

    in_names = list(INPUT_SPECS)
    park_names = list(PARK_SPECS)
    nc = bass.Bass("TRN2", target_bir_lowering=False)
    kb = KB(nc)
    kinds = {n: "in" for n in in_names}
    kinds.update({n: "internal" for n in park_names})
    kinds["out"] = "out"
    Tn = declare(nc, in_names + park_names + ["out"], kinds)
    for fn, ins, outs in STAGE_IO:
        fn(kb, Tn)
    kb.emit()
    kb.close()
    maps = [{n: per_core[c][n] for n in in_names} for c in range(NCORES)]
    res = run_bass_kernel_spmd(nc, maps, core_ids=list(range(NCORES)))
    return np.stack([np.asarray(res.results[c]["out"])
                     for c in range(NCORES)])


# version 17
# speedup vs baseline: 1.0212x; 1.0168x over previous
import numpy as np
import concourse.bass as bass
import concourse.mybir as mybir
from concourse.bass_utils import run_bass_kernel_spmd

F32 = mybir.dt.float32
BF16 = mybir.dt.bfloat16
AF = mybir.ActivationFunctionType
ALU = mybir.AluOpType
AX = mybir.AxisListType

NCORES = 8
T = 4096
D = 1024
NG = D // 128
MAGIC = 1.5 * 2.0 ** 23

ENGS = ("sync", "tensor", "vector", "scalar", "gpsimd")
A_TILES = 32
B_PAIRS = 8
B_NT1B = 16
DBG_PARKS = False


class Sem:
    __slots__ = ("h", "n", "name")

    def __init__(self, h, name):
        self.h = h
        self.n = 0
        self.name = name


class _Rec:

    def __init__(self, kb, eng):
        self._kb = kb
        self._eng = eng

    def wait(self, sem, val=None):
        v = sem.n if val is None else val
        self._kb.ops[self._eng].append(("__wait__", (sem.h, v), {}, ()))

    def __getattr__(self, meth):
        kb = self._kb
        eng = self._eng

        def rec(*a, inc=None, **kw):
            incs = []
            if inc is not None:
                if isinstance(inc, tuple) and len(inc) == 2 and isinstance(inc[0], Sem):
                    inc = [inc]
                for s, k in inc:
                    s.n += k
                    incs.append((s.h, k))
            kb.ops[eng].append((meth, a, kw, tuple(incs)))

        return rec


class KB:

    def __init__(self, nc):
        self.nc = nc
        self.ops = {e: [] for e in ENGS}
        self._sems = {}
        self._guards = {}
        self._ctx = []
        self.y = _Rec(self, "sync")
        self.t = _Rec(self, "tensor")
        self.v = _Rec(self, "vector")
        self.s = _Rec(self, "scalar")
        self.g = _Rec(self, "gpsimd")
        self.done = {e: self.sem("done_" + e) for e in ENGS}
        self.PW = self.sem("PW")
        self.CONST = self.sem("CONST")

    def constcol(self, name, value, dtype=F32):
        h = self.sbuf(name, [128, 1], dtype)
        self.g.memset(h[:, :], float(value), inc=(self.CONST, 1))
        return h

    def sem(self, name):
        if "_" in name:
            name = name.split("_", 1)[1]
        if name not in self._sems:
            cm = self.nc.semaphore(name)
            h = cm.__enter__()
            self._ctx.append(cm)
            self._sems[name] = Sem(h, name)
        return self._sems[name]

    def sbuf(self, name, shape, dtype):
        g = self.nc.sbuf_tensor(name, shape, dtype)
        h = g.__enter__()
        self._guards[name] = g
        return h

    def psum(self, name, shape, dtype=F32):
        g = self.nc.psum_tensor(name, shape, dtype)
        h = g.__enter__()
        self._guards[name] = g
        return h

    def free(self, *names):
        for nm in names:
            g = self._guards.pop(nm)
            g.__exit__(None, None, None)

    def free_all_bufs(self):
        for nm in reversed(list(self._guards)):
            self.free(nm)

    def stage_end(self, write_sems=()):
        for s in write_sems:
            self.y.wait(s)
        for e in ENGS:
            r = getattr(self, {"sync": "y", "tensor": "t", "vector": "v",
                               "scalar": "s", "gpsimd": "g"}[e])
            if e != "sync":
                r.drain()
            r.nop(inc=(self.done[e], 1))
        for e in ENGS:
            r = getattr(self, {"sync": "y", "tensor": "t", "vector": "v",
                               "scalar": "s", "gpsimd": "g"}[e])
            for e2 in ENGS:
                if e2 != e:
                    r.wait(self.done[e2])

    def emit(self):
        nc = self.nc
        with nc.Block() as block:
            for eng in ENGS:
                items = self.ops[eng]

                def run(engine, items=items):
                    for meth, a, kw, incs in items:
                        if meth == "__wait__":
                            engine.wait_ge(a[0], a[1])
                            continue
                        instr = getattr(engine, meth)(*a, **kw)
                        for h, k in incs:
                            instr = instr.then_inc(h, k)

                getattr(block, eng)(run)

    def close(self):
        self.free_all_bufs()
        for cm in reversed(self._ctx):
            cm.__exit__(None, None, None)



def build_identity(kb, name, dtype):
    nc = kb.nc
    colio = kb.sbuf(name + "_ci", [128, 1], F32)
    rowio = kb.sbuf(name + "_ri", [128, 128], F32)
    ident = kb.sbuf(name, [128, 128], dtype)
    s = kb.sem(name + "_s")
    kb.g.iota(colio[:, :], [[0, 1]], channel_multiplier=1, allow_small_or_imprecise_dtypes=True, inc=(s, 1))
    kb.g.iota(rowio[:, :], [[1, 128]], channel_multiplier=0, allow_small_or_imprecise_dtypes=True, inc=(s, 1))
    kb.v.wait(s)
    kb.v.tensor_scalar(ident[:, :], rowio[:, :], colio[:, :], None, ALU.is_equal,
                       inc=(s, 1))
    return ident, s


def stage_A(kb, Tn):
    x = Tn["x"]
    x_fm = Tn["x_fm"]
    xn0 = Tn["xn0"]

    epscol = kb.constcol("A_eps", 1e-5)
    identb, sid = build_identity(kb, "A_idb", BF16)
    identf, sidf = build_identity(kb, "A_idf", F32)
    kb.t.wait(sid)
    kb.t.wait(sidf)

    xt = [kb.sbuf(f"A_xt{i}", [128, 1024], F32) for i in range(2)]
    xnt = [kb.sbuf(f"A_xnt{i}", [128, 1024], BF16) for i in range(2)]
    stf = [kb.sbuf(f"A_stf{i}", [128, 1024], F32) for i in range(2)]
    stb = [kb.sbuf(f"A_stb{i}", [128, 1024], BF16) for i in range(2)]
    st = kb.sbuf("A_stat", [128, 8], F32)
    sq = [kb.sbuf(f"A_sq{i}", [128, 1024], F32) for i in range(2)]
    ptf = [kb.psum(f"A_ptf{i}", [128, 1024], F32) for i in range(2)]
    ptb = [kb.psum(f"A_ptb{i}", [128, 1024], BF16) for i in range(2)]

    ld = [kb.sem(f"A_ld{i}") for i in range(2)]
    pp = [kb.sem(f"A_pp{i}") for i in range(2)]
    tp = kb.sem("A_tp")
    sv = kb.sem("A_sv")
    rv = kb.sem("A_rv")
    cpv = kb.sem("A_cpv")
    ss = kb.sem("A_ss")

    for t in range(A_TILES):
        b = t % 2
        if t >= 2:
            kb.y.wait(tp, t - 1)
        kb.y.dma_start(out=xt[b][:, :], in_=x[t * 128:(t + 1) * 128, :],
                       inc=(ld[b], 16))
        kb.s.wait(ld[b], ld[b].n)
        if t >= 2:
            kb.s.wait(rv, t - 1)
        kb.s.activation(sq[b][:, :], xt[b][:, :], AF.Square, inc=(ss, 1))
        kb.v.wait(ld[b], ld[b].n)
        kb.v.reduce_sum(st[:, 0:1], xt[b][:, :], axis=AX.X)
        kb.v.wait(ss, ss.n)
        kb.v.reduce_sum(st[:, 1:2], sq[b][:, :], axis=AX.X, inc=(rv, 1))
        kb.v.drain()
        kb.v.tensor_scalar(st[:, 2:3], st[:, 0:1], 1.0 / D, None, ALU.mult)
        kb.v.tensor_scalar(st[:, 3:4], st[:, 1:2], 1.0 / D, None, ALU.mult)
        kb.v.drain()
        kb.v.tensor_scalar(st[:, 4:5], st[:, 2:3], st[:, 2:3], -1.0,
                           ALU.mult, ALU.mult)
        kb.v.drain()
        kb.v.tensor_tensor(st[:, 5:6], st[:, 3:4], st[:, 4:5], ALU.add,
                           inc=(sv, 1))
        kb.s.wait(sv, sv.n)
        if t == 0:
            kb.s.wait(kb.CONST)
        kb.s.activation(st[:, 6:7], st[:, 5:6], AF.Sqrt, bias=epscol[:, :],
                        inc=(ss, 1))
        kb.v.wait(ss, ss.n)
        kb.v.reciprocal(st[:, 7:8], st[:, 6:7])
        kb.v.drain()
        kb.v.tensor_scalar(xnt[b][:, :], xt[b][:, :], st[:, 2:3], st[:, 7:8],
                           ALU.subtract, ALU.mult, inc=(sv, 1))
        kb.t.wait(sv, sv.n)
        if t >= 2:
            kb.t.wait(cpv, 2 * (t - 1))
        for g in range(NG):
            kb.t.transpose(ptb[b][:, g * 128:(g + 1) * 128],
                           xnt[b][:, g * 128:(g + 1) * 128], identb[:, :])
        for g in range(NG):
            last = (g == NG - 1)
            kb.t.transpose(ptf[b][:, g * 128:(g + 1) * 128],
                           xt[b][:, g * 128:(g + 1) * 128], identf[:, :],
                           inc=(tp, 1) if last else None)
        kb.v.wait(tp, t + 1)
        if t >= 2:
            kb.v.wait(pp[b], pp[b].n)
        kb.v.tensor_copy(stb[b][:, :], ptb[b][:, :], inc=(cpv, 1))
        kb.v.tensor_copy(stf[b][:, :], ptf[b][:, :], inc=(cpv, 1))
        kb.y.wait(cpv, cpv.n)
        src_b = stb[b][:, :].rearrange("p (g j) -> p g j", g=NG)
        dst_b = xn0[:, :, t * 128:(t + 1) * 128].rearrange("g p j -> p g j")
        kb.y.dma_start(out=dst_b, in_=src_b, inc=(pp[b], 16))
        src_f = stf[b][:, :].rearrange("p (g j) -> p g j", g=NG)
        dst_f = x_fm[:, :, t * 128:(t + 1) * 128].rearrange("g p j -> p g j")
        kb.y.dma_start(out=dst_f, in_=src_f, inc=(pp[b], 16))

    kb.stage_end([pp[0], pp[1]])
    kb.free_all_bufs()



SLAB_S = 8704


def stage_B(kb, Tn):
    xn0 = Tn["xn0"]
    wq_t, wk_t, wv_t = Tn["wq_t"], Tn["wk_t"], Tn["wv_t"]
    rbt = Tn["rbt"]
    s0 = Tn["s0"]
    nc = kb.nc
    text_d = nc.dram_tensor("text_d", [16, 8576], BF16, kind="Internal").ap()
    c_d = nc.dram_tensor("c_d", [16, 1], F32, kind="Internal").ap()
    wslab = nc.dram_tensor("wslab", [16, 128 * SLAB_S + 8576], BF16,
                           kind="Internal").ap()

    identf, sidf = build_identity(kb, "B_idf", F32)

    rb_sb = kb.sbuf("B_rb", [128, 512], F32)
    T_sb = kb.sbuf("B_T", [16, 4224], F32)
    text_sb = kb.sbuf("B_text", [16, 8576], BF16)
    ptr = [kb.psum(f"B_ptr{i}", [16, 512], F32) for i in range(2)]

    ldr = kb.sem("B_ldr")
    tt = kb.sem("B_tt")
    cp = kb.sem("B_cp")
    tw = kb.sem("B_tw")
    slw = kb.sem("B_slw")

    src1 = rbt[0:3968, :].rearrange("(c p) h -> p c h", p=128)
    kb.y.dma_start(out=rb_sb[:, 0:496].rearrange("p (c h) -> p c h", h=16),
                   in_=src1, inc=(ldr, 16))
    kb.y.dma_start(out=rb_sb[0:127, 496:512], in_=rbt[3968:4095, :],
                   inc=(ldr, 16))
    kb.t.wait(sidf)
    kb.t.wait(ldr)
    for c in range(32):
        if c >= 2:
            kb.t.wait(cp, c - 1)
        kb.t.transpose(ptr[c % 2][:, 0:128], rb_sb[:, c * 16:(c + 1) * 16],
                       identf[:, :], inc=(tt, 1))
        kb.v.wait(tt, tt.n)
        ncols = 128 if c < 31 else 127
        kb.v.tensor_copy(T_sb[:, c * 128:c * 128 + ncols],
                         ptr[c % 2][0:16, 0:ncols], inc=(cp, 1))
    kb.v.drain()
    kb.v.drain()
    kb.v.tensor_scalar(text_sb[:, 0:4095], T_sb[:, 0:4095],
                       T_sb[:, 4094:4095], None, ALU.subtract)
    kb.v.memset(text_sb[:, 4095:8576], 0.0, inc=(cp, 1))
    kb.y.wait(cp, cp.n)
    kb.y.dma_start(out=text_d, in_=text_sb[:, :], inc=(tw, 16))
    kb.y.dma_start(out=c_d, in_=T_sb[:, 4094:4095], inc=(tw, 16))
    kb.y.wait(tw)
    gsrc = bass.AP(text_d.tensor, 0, [[8576, 16], [0, 128], [1, 128]])
    gdst = bass.AP(wslab.tensor, 0, [[128 * SLAB_S, 16], [SLAB_S, 128],
                                     [1, 128]])
    kb.y.dma_start(out=gdst, in_=gsrc, inc=(tw, 16))
    kb.y.wait(tw)
    for h in range(16):
        src = bass.AP(text_d.tensor, h * 8576, [[0, 128], [1, 8576]])
        dst = bass.AP(wslab.tensor, h * 128 * SLAB_S, [[SLAB_S + 1, 128],
                                                       [1, 8576]])
        kb.y.dma_start(out=dst, in_=src, inc=(slw, 16))
    kb.free("B_ptr1", "B_ptr0", "B_text", "B_T", "B_rb")

    xn0_sb = kb.sbuf("B_xn0", [128, NG * T], BF16)
    qpair = kb.sbuf("B_q", [128, T], BF16)
    kpair = kb.sbuf("B_k", [128, T], BF16)
    vaug = [kb.sbuf(f"B_va{i}", [128, 32 * 65], BF16) for i in range(2)]
    esb = [kb.sbuf(f"B_e{i}", [128, 8192], BF16) for i in range(2)]
    slab_sb = [kb.sbuf(f"B_sl{i}", [128, 8576], BF16) for i in range(2)]
    wsl = [kb.sbuf(f"B_w{i}", [128, 1024], BF16) for i in range(3)]
    ones1f = kb.sbuf("B_o1f", [1, 128], F32)
    bb = kb.sbuf("B_bb", [1, 16], F32)
    sqbuf = [kb.sbuf(f"B_sqb{i}", [128, 512], F32) for i in range(2)]
    onescol = kb.sbuf("B_onescol", [128, 1], F32)
    negB = [kb.sbuf(f"B_nb{i}", [128, 1], F32) for i in range(2)]
    onescolb = kb.sbuf("B_ocb", [128, 1], BF16)
    onesrowb = kb.sbuf("B_orb", [1, 256], BF16)
    crow = kb.sbuf("B_crow", [1, 16], F32)
    vsumc = [kb.sbuf(f"B_vs{i}", [1, 65], BF16) for i in range(2)]
    srow = kb.sbuf("B_srow", [1, 256], F32)
    rrow = kb.sbuf("B_rrow", [1, 256], F32)
    rbc = kb.sbuf("B_rbc", [64, 256], F32)
    tmp = kb.sbuf("B_tmp", [64, 256], F32)
    osb = [kb.sbuf(f"B_osb{i}", [64, 256], BF16) for i in range(2)]

    ldx = kb.sem("B_ldx")
    wl = kb.sem("B_wl")
    sll = [kb.sem(f"B_sll{i}") for i in range(2)]
    mq = kb.sem("B_mq")
    vv = kb.sem("B_vv")
    qev = kb.sem("B_qev")
    sx = kb.sem("B_sx")
    ssq = kb.sem("B_ssq")
    sav = kb.sem("B_sav")
    fav = kb.sem("B_fav")
    rcp = kb.sem("B_rcp")
    prm = kb.sem("B_prm")
    ave = kb.sem("B_ave")
    bsem = kb.sem("B_bsem")
    po = [kb.sem(f"B_po{i}") for i in range(2)]

    kb.v.memset(ones1f[:, :], 1.0)
    kb.v.memset(onescol[:, :], 1.0)
    kb.v.memset(onescolb[:, :], 1.0)
    kb.v.memset(onesrowb[:, :], 1.0)
    for i in range(2):
        kb.v.memset(vaug[i][:, :].rearrange("p (c e) -> p c e", e=65)[:, :, 64:65],
                    1.0, inc=(qev, 1))
    kb.y.wait(tw)
    kb.y.dma_start(out=crow[0:1, 0:16],
                   in_=bass.AP(c_d.tensor, 0, [[0, 1], [1, 16]]),
                   inc=(ldx, 16))
    kb.y.dma_start(out=xn0_sb[:, :].rearrange("p (g t) -> p g t", t=T),
                   in_=xn0.rearrange("g p t -> p g t"), inc=(ldx, 16))

    NT1B = B_NT1B
    for pair in range(B_PAIRS):
        kb.g.wait(mq, mq.n)
        for i, w in enumerate((wq_t, wk_t, wv_t)):
            srcw = w[:, pair * 128:(pair + 1) * 128].rearrange(
                "(g p) e -> p g e", p=128)
            kb.g.dma_start(out=wsl[i][:, :].rearrange("p (g e) -> p g e", e=128),
                           in_=srcw, inc=(wl, 16))
        for i in range(2):
            if pair == 0:
                kb.y.wait(slw)
            else:
                kb.y.wait(fav, fav.n)
            srcs = bass.AP(wslab.tensor, (2 * pair + i) * 128 * SLAB_S,
                           [[SLAB_S, 128], [1, 8576]])
            kb.y.dma_start(out=slab_sb[i][:, :], in_=srcs, inc=(sll[i], 16))

        pq = [kb.psum(f"B_pq{pair}_{i}", [128, 512], F32) for i in range(2)]
        pv = [kb.psum(f"B_pv{pair}_{i}", [128, 512], F32) for i in range(2)]
        pbx = kb.psum(f"B_pbx{pair}", [128, 512], F32)
        pn = [kb.psum(f"B_pn{pair}_{i}", [1, 512], F32) for i in range(2)]
        kb.t.wait(wl)
        if pair == 0:
            kb.t.wait(ldx)
        for which, obuf in ((0, qpair), (1, kpair)):
            for tc in range(8):
                pqb = pq[tc % 2]
                kb.t.wait(qev, max(0, qev.n - 1))
                for g in range(NG):
                    kb.t.matmul(pqb[:, :],
                                wsl[which][:, g * 128:(g + 1) * 128],
                                xn0_sb[:, g * T + tc * 512:g * T + (tc + 1) * 512],
                                start=(g == 0), stop=(g == NG - 1),
                                inc=(mq, 1) if g == NG - 1 else None)
                kb.v.wait(mq, mq.n)
                kb.v.tensor_copy(obuf[:, tc * 512:(tc + 1) * 512], pqb[:, :],
                                 inc=(qev, 1))
        for t2c in range(32):
            pvb = pv[t2c % 2]
            kb.t.wait(vv, max(0, vv.n - 1))
            for g in range(NG):
                kb.t.matmul(pvb[:, 0:128],
                            xn0_sb[:, g * T + t2c * 128:g * T + (t2c + 1) * 128],
                            wsl[2][:, g * 128:(g + 1) * 128],
                            start=(g == 0), stop=(g == NG - 1),
                            inc=(mq, 1) if g == NG - 1 else None)
            kb.v.wait(mq, mq.n)
            kb.v.tensor_copy(vaug[0][:, t2c * 65:t2c * 65 + 64], pvb[:, 0:64])
            kb.v.tensor_copy(vaug[1][:, t2c * 65:t2c * 65 + 64], pvb[:, 64:128],
                             inc=(vv, 1))
        kb.v.wait(qev, qev.n)
        kb.v.wait(vv, vv.n)
        kb.v.wait(sx, sx.n)
        kb.v.memset(bb[:, 0:4], 0.0)
        kb.v.drain()
        kb.s.wait(qev, qev.n)
        kb.s.wait(vv, vv.n)
        rnd = 0
        for src_buf, c0 in ((qpair, 0), (kpair, 2)):
            for tc in range(8):
                sqb = sqbuf[rnd % 2]
                kb.s.wait(bsem, max(0, bsem.n - 2))
                kb.s.activation(sqb[:, :], src_buf[:, tc * 512:(tc + 1) * 512],
                                AF.Square, inc=(ssq, 1))
                kb.t.wait(ssq, ssq.n)
                kb.t.wait(qev, qev.n)
                for h in range(2):
                    kb.t.matmul(pn[h][0:1, 0:512], onescol[h * 64:(h + 1) * 64, :],
                                sqb[h * 64:(h + 1) * 64, :], start=True,
                                stop=True, tile_position=(64 * h, 0),
                                inc=(bsem, 1))
                rnd += 1
                kb.v.wait(bsem, bsem.n)
                for h in range(2):
                    kb.v.reduce_max(bb[:, 8 + h:9 + h], pn[h][0:1, 0:512],
                                    axis=AX.X)
                kb.v.drain()
                for h in range(2):
                    kb.v.tensor_tensor(bb[:, c0 + h:c0 + h + 1],
                                       bb[:, c0 + h:c0 + h + 1],
                                       bb[:, 8 + h:9 + h], ALU.max)
                kb.v.drain()
                kb.v.nop(inc=(qev, 1))
        kb.v.tensor_tensor(bb[:, 10:11], bb[:, 0:1], bb[:, 2:3], ALU.mult)
        kb.v.tensor_tensor(bb[:, 11:12], bb[:, 1:2], bb[:, 3:4], ALU.mult,
                           inc=(qev, 1))
        kb.s.wait(qev, qev.n)
        kb.s.activation(bb[:, 12:14], bb[:, 10:12], AF.Sqrt,
                        scale=1.0 / 1024.0, inc=(ssq, 1))
        kb.v.wait(ssq, ssq.n)
        kb.v.tensor_scalar(bb[:, 4:6], bb[:, 12:14], -1.0, None, ALU.mult,
                           inc=(qev, 1))
        kb.t.wait(qev, qev.n)
        for i in range(2):
            kb.t.matmul(pbx[:, 128 + i * 128:129 + i * 128], ones1f[:, :],
                        bb[:, 4 + i:5 + i], start=True, stop=True,
                        inc=(bsem, 1))
        kb.v.wait(bsem, bsem.n)
        kb.v.tensor_copy(negB[0][:, :], pbx[:, 128:129])
        kb.v.tensor_copy(negB[1][:, :], pbx[:, 256:257], inc=(qev, 1))
        kb.t.wait(vv, vv.n)
        for h in range(2):
            head = 2 * pair + h
            for c in range(32):
                kb.t.matmul(pn[h][0:1, 0:65], onescolb[:, :],
                            vaug[h][:, c * 65:(c + 1) * 65],
                            start=(c == 0), stop=(c == 31),
                            inc=(bsem, 1) if c == 31 else None)
            kb.v.wait(bsem, bsem.n)
            if pair == 0 and h == 0:
                kb.v.wait(ldx, 16)
            kb.v.tensor_scalar(vsumc[h][0:1, 0:65], pn[h][0:1, 0:65],
                               crow[0:1, head:head + 1], None, ALU.mult,
                               inc=(qev, 1))
        kb.free(f"B_pn{pair}_1", f"B_pn{pair}_0", f"B_pbx{pair}",
                f"B_pv{pair}_1", f"B_pv{pair}_0", f"B_pq{pair}_1",
                f"B_pq{pair}_0")

        ps_ = [[kb.psum(f"B_ps{pair}_{h}{i}", [128, 512], F32) for i in range(2)]
               for h in range(2)]
        pm = [kb.psum(f"B_pm{pair}_{h}", [65, 512], F32) for h in range(2)]
        pb = [kb.psum(f"B_pb{pair}_{h}", [65, 512], F32) for h in range(2)]
        kb.t.wait(qev, qev.n)
        for t1b in range(NT1B):
            t10 = t1b * 256
            kept = [c for c in range(32) if 128 * c + 126 > t10]
            for h in range(2):
                kb.t.wait(ave, ave.n)
                kb.t.wait(sll[h], sll[h].n)
                kb.t.matmul(pb[h][0:65, 0:256], vsumc[h][0:1, 0:65],
                            onesrowb[:, :], start=True, stop=False)
            fav_snap = [0, 0]
            sx_hist = []

            def emit_av(cp_):
                kb.t.wait(sx, sx_hist[cp_])
                for h in range(2):
                    for half in range(2):
                        t2c = 2 * cp_ + half
                        va = vaug[h][:, t2c * 65:(t2c + 1) * 65]
                        last = (cp_ == 15 and half == 1)
                        kb.t.matmul(pm[h][0:65, 0:256], va,
                                    esb[h][:, t2c * 256:(t2c + 1) * 256],
                                    start=(cp_ == 0 and half == 0),
                                    stop=last,
                                    inc=(fav, 1) if last else None)
                        if t2c in kept:
                            Boff = 4095 - 128 * t2c + t10
                            kb.t.matmul(pb[h][0:65, 0:256], va,
                                        slab_sb[h][:, Boff:Boff + 256],
                                        start=False, stop=last,
                                        inc=(fav, 1) if last else None)
                    if cp_ == 15:
                        fav_snap[h] = fav.n

            for cpair in range(16):
                for h in range(2):
                    buf = ps_[h][cpair % 2]
                    kb.t.wait(sx, max(0, sx.n - 2))
                    for half in range(2):
                        t2c = 2 * cpair + half
                        part = (0, 0) if h == 0 else (64, 0)
                        kb.t.matmul(
                            buf[:, half * 256:(half + 1) * 256],
                            kpair[h * 64:(h + 1) * 64,
                                  t2c * 128:(t2c + 1) * 128],
                            qpair[h * 64:(h + 1) * 64, t10:t10 + 256],
                            start=True, stop=True, tile_position=part,
                            inc=(sav, 1) if half == 1 else None)
                for h in range(2):
                    kb.s.wait(sav, sav.n - (1 - h))
                    if pair == 0 and t1b == 0 and cpair == 0:
                        kb.s.wait(qev, qev.n)
                    kb.s.activation(esb[h][:, cpair * 512:(cpair + 1) * 512],
                                    ps_[h][cpair % 2][:, :], AF.Exp,
                                    scale=1.0 / 32.0, bias=negB[h][:, :],
                                    inc=(sx, 1))
                sx_hist.append(sx.n)
                if cpair >= 1:
                    emit_av(cpair - 1)
            emit_av(15)
            for h in range(2):
                head = 2 * pair + h
                kb.v.wait(fav, fav_snap[h])
                kb.v.reciprocal(rrow[:, :], pm[h][64:65, 0:256], inc=(rcp, 1))
                kb.t.wait(rcp, rcp.n)
                kb.t.matmul(pm[h][0:64, 256:512], ones1f[:, 0:64], rrow[:, :],
                            start=True, stop=True, inc=(prm, 1))
                kb.v.wait(prm, prm.n)
                kb.v.wait(po[h], po[h].n)
                kb.v.tensor_copy(rbc[:, :], pm[h][0:64, 256:512])
                kb.v.drain()
                kb.v.tensor_tensor(tmp[:, :], pm[h][0:64, 0:256], rbc[:, :],
                                   ALU.mult)
                kb.v.drain()
                kb.v.tensor_tensor(osb[h][:, :], tmp[:, :], pb[h][0:64, 0:256],
                                   ALU.add, inc=(ave, 1))
                kb.y.wait(ave, ave.n)
                g, prow = head // 2, (head % 2) * 64
                kb.y.dma_start(out=s0[g, prow:prow + 64, t10:t10 + 256],
                               in_=osb[h][:, :], inc=(po[h], 16))
        kb.free(f"B_pb{pair}_1", f"B_pb{pair}_0",
                f"B_pm{pair}_1", f"B_pm{pair}_0",
                f"B_ps{pair}_11", f"B_ps{pair}_10",
                f"B_ps{pair}_01", f"B_ps{pair}_00")

    kb.stage_end([po[0], po[1]])
    kb.free_all_bufs()



def _bitlinear_weff(kb, wrow, wq, wcols, scol, n, sem):
    kb.v.reduce_sum(scol[:, 0:1], wrow[:, :], axis=AX.X,
                    apply_absolute_value=True)
    kb.v.drain()
    kb.v.tensor_scalar(scol[:, 1:2], scol[:, 0:1], 1.0 / n, None, ALU.mult)
    kb.v.drain()
    kb.v.tensor_scalar(scol[:, 2:3], scol[:, 1:2], 1e-5, None, ALU.max)
    kb.v.drain()
    kb.v.reciprocal(scol[:, 3:4], scol[:, 2:3])
    kb.v.drain()
    kb.v.tensor_scalar(wcols[:, :], wrow[:, :], scol[:, 3:4], None, ALU.mult)
    kb.v.drain()
    kb.v.tensor_scalar(wcols[:, :], wcols[:, :], MAGIC, MAGIC, ALU.add,
                       ALU.subtract)
    kb.v.drain()
    kb.v.tensor_scalar(wcols[:, :], wcols[:, :], 1.0, -1.0, ALU.min, ALU.max)
    kb.v.drain()
    kb.v.tensor_scalar(wq[:, :], wcols[:, :], scol[:, 2:3], None, ALU.mult,
                       inc=(sem, 1))


def stage_C(kb, Tn):
    xn0, s0, x_fm = Tn["xn0"], Tn["s0"], Tn["x_fm"]
    wfc, wproj = Tn["wfc"], Tn["wproj"]
    u_t, v_t, fb_t, gate = Tn["u_t"], Tn["v_t"], Tn["fb_t"], Tn["l0_gate"]
    lng, lnb = Tn["l0_lng"], Tn["l0_lnb"]
    x1_fm, x1stats = Tn["x1_fm"], Tn["x1stats"]
    nc = kb.nc
    s0mr = nc.dram_tensor("s0mr", [2, T], F32, kind="Internal").ap()

    epscol = kb.constcol("C_eps", 1e-5)
    identb, sid = build_identity(kb, "C_idb", BF16)
    ones1f = kb.sbuf("C_o1f", [1, 128], F32)
    onesb = kb.sbuf("C_onesb", [128, 1], BF16)
    onesf = kb.sbuf("C_onesf", [128, 1], F32)
    msem = kb.sem("C_ms")
    kb.v.memset(ones1f[:, :], 1.0)
    kb.v.memset(onesb[:, :], 1.0)
    kb.v.memset(onesf[:, :], 1.0, inc=(msem, 1))

    st_sb = kb.sbuf("C_st", [1, 8192], F32)
    s0t = [kb.sbuf(f"C_s0t{i}", [128, 512], BF16) for i in range(2)]
    s0sq = kb.sbuf("C_s0sq", [128, 512], BF16)
    pst = kb.psum("C_pst", [1, 512], F32)
    psq = kb.psum("C_psq", [1, 512], F32)
    l0 = [kb.sem(f"C_l0{i}") for i in range(2)]
    mm = kb.sem("C_mm")
    ev = kb.sem("C_ev")
    sc = kb.sem("C_sc")
    pw = kb.sem("C_pw")
    kb.t.wait(msem)
    for tc in range(8):
        for g in range(NG):
            b = (tc * NG + g) % 2
            kb.y.wait(mm, max(0, mm.n - 1))
            kb.y.wait(sc, max(0, sc.n - 1))
            kb.y.dma_start(out=s0t[b][:, :],
                           in_=s0[g, :, tc * 512:(tc + 1) * 512],
                           inc=(l0[b], 16))
            kb.s.wait(l0[b], l0[b].n)
            kb.s.wait(mm, mm.n)
            kb.s.activation(s0sq[:, :], s0t[b][:, :], AF.Square, inc=(sc, 1))
            kb.t.wait(l0[b], l0[b].n)
            kb.t.matmul(pst[0:1, :], onesb[:, :], s0t[b][:, :],
                        start=(g == 0), stop=(g == NG - 1))
            kb.t.wait(sc, sc.n)
            kb.t.matmul(psq[0:1, :], onesb[:, :], s0sq[:, :],
                        start=(g == 0), stop=(g == NG - 1), inc=(mm, 1))
        kb.v.wait(mm, mm.n)
        kb.v.tensor_copy(st_sb[:, tc * 512:(tc + 1) * 512], pst[0:1, :])
        kb.v.tensor_copy(st_sb[:, 4096 + tc * 512:4096 + (tc + 1) * 512],
                         psq[0:1, :], inc=(ev, 1))
        kb.t.wait(ev, ev.n)
    kb.v.drain()
    kb.v.tensor_scalar(st_sb[:, 0:4096], st_sb[:, 0:4096], 1.0 / D, None,
                       ALU.mult)
    kb.v.tensor_scalar(st_sb[:, 4096:8192], st_sb[:, 4096:8192], 1.0 / D,
                       None, ALU.mult)
    kb.v.drain()
    musq = kb.sbuf("C_musq", [1, 4096], F32)
    kb.v.tensor_tensor(musq[:, :], st_sb[:, 0:4096], st_sb[:, 0:4096],
                       ALU.mult)
    kb.v.drain()
    kb.v.tensor_tensor(st_sb[:, 4096:8192], st_sb[:, 4096:8192], musq[:, :],
                       ALU.subtract, inc=(ev, 1))
    kb.s.wait(ev, ev.n)
    kb.s.wait(kb.CONST)
    kb.s.activation(musq[:, :], st_sb[:, 4096:8192], AF.Sqrt,
                    bias=epscol[0:1, :], inc=(sc, 1))
    kb.v.wait(sc, sc.n)
    kb.v.reciprocal(st_sb[:, 4096:8192], musq[:, :], inc=(ev, 1))
    kb.y.wait(ev, ev.n)
    kb.y.dma_start(out=bass.AP(s0mr.tensor, 0, [[0, 1], [T, 2], [1, T]]),
                   in_=st_sb[:, :].rearrange("o (a t) -> o a t", a=2),
                   inc=(pw, 16))
    kb.stage_end([pw])
    kb.free("C_musq", "C_psq", "C_pst", "C_s0sq", "C_s0t1", "C_s0t0", "C_st")

    wfcT = kb.sbuf("C_wfcT", [128, NG * 4096], BF16)
    wprojT = kb.sbuf("C_wprojT", [128, 32 * 1024], BF16)
    wrow = [kb.sbuf(f"C_wrow{i}", [128, 4096], F32) for i in range(2)]
    wcols = kb.sbuf("C_wcols", [128, 4096], F32)
    wqb = kb.sbuf("C_wqb", [128, 4096], BF16)
    scol = kb.sbuf("C_scol", [128, 4], F32)
    ptw = kb.psum("C_ptw", [128, 1024], BF16)
    lw = [kb.sem(f"C_lw{i}") for i in range(2)]
    tb = kb.sem("C_tb")
    tt2 = kb.sem("C_tt2")
    tb2 = kb.sem("C_tb2")

    kb.t.wait(sid)
    for jg in range(32):
        b = jg % 2
        kb.y.wait(ev, ev.n)
        kb.y.dma_start(out=wrow[b][:, 0:1024],
                       in_=wfc[jg * 128:(jg + 1) * 128, :], inc=(lw[b], 16))
        kb.v.wait(lw[b], lw[b].n)
        kb.v.wait(tb, tb.n)
        _bitlinear_weff(kb, wrow[b][:, 0:1024], wqb[:, 0:1024],
                        wcols[:, 0:1024], scol, 1024, ev)
        kb.t.wait(ev, ev.n)
        kb.t.wait(tb, tb.n)
        for g in range(NG):
            kb.t.transpose(ptw[:, g * 128:(g + 1) * 128],
                           wqb[:, g * 128:(g + 1) * 128], identb[:, :],
                           inc=(tb, 1) if g == NG - 1 else None)
        kb.v.wait(tb, tb.n)
        for g in range(NG):
            kb.v.tensor_copy(wfcT[:, g * 4096 + jg * 128:g * 4096 + (jg + 1) * 128],
                             ptw[:, g * 128:(g + 1) * 128],
                             inc=(tb, 1) if g == NG - 1 else None)
    for dg in range(NG):
        b = dg % 2
        kb.y.wait(ev, ev.n)
        kb.y.dma_start(out=wrow[b][:, :],
                       in_=wproj[dg * 128:(dg + 1) * 128, :], inc=(lw[b], 16))
        kb.v.wait(lw[b], lw[b].n)
        kb.v.wait(tt2, tt2.n)
        _bitlinear_weff(kb, wrow[b][:, :], wqb[:, :], wcols[:, :], scol,
                        4096, ev)
        kb.t.wait(ev, ev.n)
        for jb in range(4):
            kb.t.wait(tb2, tb2.n)
            for jj in range(8):
                jc = jb * 8 + jj
                kb.t.transpose(ptw[:, jj * 128:(jj + 1) * 128],
                               wqb[:, jc * 128:(jc + 1) * 128], identb[:, :],
                               inc=(tt2, 1) if jj == 7 else None)
            kb.v.wait(tt2, tt2.n)
            for jj in range(8):
                jc = jb * 8 + jj
                kb.v.tensor_copy(
                    wprojT[:, jc * 1024 + dg * 128:jc * 1024 + (dg + 1) * 128],
                    ptw[:, jj * 128:(jj + 1) * 128],
                    inc=(tb2, 1) if jj == 7 else None)
        kb.v.nop(inc=(tb, 1))
    kb.stage_end()
    kb.free("C_ptw", "C_scol", "C_wqb", "C_wcols", "C_wrow1", "C_wrow0")

    usl = kb.sbuf("C_usl", [16, 4096], BF16)
    vsl = kb.sbuf("C_vsl", [128, 128], BF16)
    fbsl = kb.sbuf("C_fb", [16, 16], BF16)
    gcol = kb.sbuf("C_gcol", [16, 1], F32)
    b_sb = kb.sbuf("C_bsb", [16, 4096], BF16)
    xnt = kb.sbuf("C_xnt", [128, 4096], BF16)
    gf = kb.sbuf("C_gf", [1, 1], F32)
    a_sb = kb.sbuf("C_asb", [16, 4096], BF16)
    pfa = kb.psum("C_pfa", [16, 512], F32)
    pga = kb.psum("C_pga", [16, 512], F32)
    lf = kb.sem("C_lf")
    lx = kb.sem("C_lx")

    kb.g.dma_start(out=usl[:, :], in_=u_t, inc=(lf, 16))
    kb.g.dma_start(out=vsl[:, :].rearrange("p (g r) -> p g r", r=16),
                   in_=v_t.rearrange("(g p) r -> p g r", p=128), inc=(lf, 16))
    kb.g.dma_start(out=fbsl[:, :], in_=fb_t, inc=(lf, 16))
    kb.y.dma_start(out=gf[:, :], in_=gate, inc=(lx, 16))
    kb.t.wait(lx, lx.n)
    kb.t.wait(msem)
    kb.t.matmul(pga[0:16, 0:1], ones1f[:, 0:16], gf[:, :], start=True,
                stop=True, inc=(mm, 1))
    kb.v.wait(mm, mm.n)
    kb.v.tensor_copy(gcol[:, :], pga[0:16, 0:1], inc=(ev, 1))
    for tc in range(8):
        kb.y.wait(mm, mm.n)
        kb.y.dma_start(out=xnt[:, :].rearrange("p (g t) -> p g t", t=512),
                       in_=xn0[:, :, tc * 512:(tc + 1) * 512].rearrange(
                           "g p t -> p g t"), inc=(lx, 16))
        kb.t.wait(lx, lx.n)
        kb.t.wait(lf, lf.n)
        kb.t.wait(ev, ev.n)
        for g in range(NG):
            kb.t.matmul(pfa[0:16, :], vsl[:, g * 16:(g + 1) * 16],
                        xnt[:, g * 512:(g + 1) * 512],
                        start=(g == 0), stop=(g == NG - 1),
                        inc=(mm, 1) if g == NG - 1 else None)
        kb.v.wait(mm, mm.n)
        kb.v.tensor_copy(a_sb[:, tc * 512:(tc + 1) * 512], pfa[0:16, :],
                         inc=(ev, 1))
    for tc in range(8):
        kb.t.wait(ev, ev.n)
        kb.t.matmul(pga[0:16, :], fbsl[:, :], a_sb[:, tc * 512:(tc + 1) * 512],
                    start=True, stop=True, inc=(mm, 1))
        kb.v.wait(mm, mm.n)
        kb.v.tensor_scalar(b_sb[:, tc * 512:(tc + 1) * 512], pga[0:16, :],
                           gcol[:, :], None, ALU.mult, inc=(ev, 1))
    kb.stage_end()
    kb.free("C_pga", "C_pfa", "C_asb")

    rr = kb.sbuf("C_rr", [128, 32 * 512], BF16)
    rr1 = kb.sbuf("C_rr1", [128, 512], BF16)
    g0t = kb.sbuf("C_g0t", [128, 512], BF16)
    s0c = kb.sbuf("C_s0c", [128, 512], BF16)
    xtc = kb.sbuf("C_xtc", [128, 512], F32)
    mrow = kb.sbuf("C_mrow", [1, 1024], F32)
    mub = kb.sbuf("C_mub", [128, 512], F32)
    rsb = kb.sbuf("C_rsb", [128, 512], F32)
    gcols = kb.sbuf("C_gcols", [128, 8], F32)
    bcols = kb.sbuf("C_bcols", [128, 8], F32)
    t1 = kb.sbuf("C_t1", [128, 512], F32)
    t2 = kb.sbuf("C_t2", [128, 512], F32)
    xn1t = kb.sbuf("C_xn1t", [128, 512], BF16)
    x1t = kb.sbuf("C_x1t", [128, 512], F32)
    x1sq = kb.sbuf("C_x1sq", [128, 512], F32)
    ph = [kb.psum(f"C_ph{i}", [128, 512], F32) for i in range(2)]
    po = [kb.psum(f"C_po{i}", [128, 512], F32) for i in range(2)]
    pb2 = kb.psum("C_pb2", [128, 1024], F32)
    pst2 = kb.psum("C_pst2", [1, 1024], F32)
    lm = kb.sem("C_lm")
    lsx = [kb.sem(f"C_lsx{i}") for i in range(3)]
    rvv = kb.sem("C_rvv")
    scc = kb.sem("C_scc")
    pxw = kb.sem("C_pxw")

    kb.g.dma_start(out=gcols[:, :],
                   in_=bass.AP(lng.tensor, 0, [[1, 128], [128, 8]]),
                   allow_slow_non_contiguous=True, inc=(lm, 16))
    kb.g.dma_start(out=bcols[:, :],
                   in_=bass.AP(lnb.tensor, 0, [[1, 128], [128, 8]]),
                   allow_slow_non_contiguous=True, inc=(lm, 16))

    for tc in range(8):
        kb.y.wait(mm, mm.n)
        kb.y.dma_start(out=xnt[:, :].rearrange("p (g t) -> p g t", t=512),
                       in_=xn0[:, :, tc * 512:(tc + 1) * 512].rearrange(
                           "g p t -> p g t"), inc=(lx, 16))
        kb.y.wait(pw, pw.n)
        kb.y.wait(pxw, pxw.n)
        kb.y.dma_start(out=mrow[:, :].rearrange("o (a t) -> o a t", a=2),
                       in_=bass.AP(s0mr.tensor, tc * 512,
                                   [[0, 1], [T, 2], [1, 512]]),
                       inc=(lsx[2], 16))
        kb.t.wait(lsx[2], lsx[2].n)
        kb.t.wait(rvv, rvv.n)
        kb.t.matmul(pb2[:, 0:512], ones1f[:, :], mrow[:, 0:512], start=True,
                    stop=True)
        kb.t.matmul(pb2[:, 512:1024], ones1f[:, :], mrow[:, 512:1024],
                    start=True, stop=True, inc=(mm, 1))
        kb.v.wait(mm, mm.n)
        kb.v.tensor_copy(mub[:, :], pb2[:, 0:512])
        kb.v.tensor_copy(rsb[:, :], pb2[:, 512:1024], inc=(rvv, 1))
        kb.t.wait(lx, lx.n)
        for jsl in range(32):
            pp = ph[jsl % 2]
            kb.t.wait(rvv, max(0, rvv.n - 1))
            for g in range(NG):
                kb.t.matmul(pp[:, :],
                            wfcT[:, g * 4096 + jsl * 128:g * 4096 + (jsl + 1) * 128],
                            xnt[:, g * 512:(g + 1) * 512],
                            start=(g == 0), stop=False)
            kb.t.matmul(pp[:, :], usl[:, jsl * 128:(jsl + 1) * 128],
                        b_sb[:, tc * 512:(tc + 1) * 512],
                        start=False, stop=True, inc=(mm, 1))
            kb.v.wait(mm, mm.n)
            kb.v.wait(scc, scc.n)
            kb.v.tensor_scalar(rr1[:, :], pp[:, :], 0.0, None, ALU.max,
                               inc=(rvv, 1))
            kb.s.wait(rvv, rvv.n)
            kb.s.activation(rr[:, jsl * 512:(jsl + 1) * 512], rr1[:, :],
                            AF.Square, inc=(scc, 1))
        kb.y.wait(scc, scc.n)
        kb.y.wait(pxw, pxw.n)
        for ds in range(NG):
            kb.y.dma_start(out=s0c[:, :],
                           in_=s0[ds, :, tc * 512:(tc + 1) * 512],
                           inc=(lsx[0], 16))
            kb.y.dma_start(out=xtc[:, :],
                           in_=x_fm[ds, :, tc * 512:(tc + 1) * 512],
                           inc=(lsx[1], 16))
            pq = po[ds % 2]
            kb.t.wait(scc, scc.n)
            kb.t.wait(rvv, rvv.n)
            for jc in range(32):
                kb.t.matmul(pq[:, :],
                            wprojT[:, jc * 1024 + ds * 128:jc * 1024 + (ds + 1) * 128],
                            rr[:, jc * 512:(jc + 1) * 512],
                            start=(jc == 0), stop=(jc == 31),
                            inc=(mm, 1) if jc == 31 else None)
            kb.s.wait(mm, mm.n)
            kb.s.activation(g0t[:, :], pq[:, :], AF.Silu, inc=(scc, 1))
            kb.v.wait(scc, scc.n)
            kb.v.wait(lsx[0], lsx[0].n)
            kb.v.wait(lsx[1], lsx[1].n)
            if tc == 0 and ds == 0:
                kb.v.wait(lm, lm.n)
            kb.v.wait(mm, mm.n)
            kb.v.wait(pxw, pxw.n)
            kb.v.tensor_tensor(t1[:, :], s0c[:, :], mub[:, :], ALU.subtract)
            kb.v.drain()
            kb.v.tensor_tensor(t2[:, :], t1[:, :], rsb[:, :], ALU.mult)
            kb.v.drain()
            kb.v.tensor_scalar(t1[:, :], t2[:, :], gcols[:, ds:ds + 1],
                               bcols[:, ds:ds + 1], ALU.mult, ALU.add)
            kb.v.drain()
            kb.v.tensor_tensor(t2[:, :], t1[:, :], g0t[:, :], ALU.mult)
            kb.v.drain()
            kb.v.tensor_tensor(x1t[:, :], t2[:, :], xtc[:, :], ALU.add,
                               inc=(rvv, 1))
            kb.y.wait(rvv, rvv.n)
            kb.y.dma_start(out=x1_fm[ds, :, tc * 512:(tc + 1) * 512],
                           in_=x1t[:, :], inc=(pxw, 16))
            kb.s.wait(rvv, rvv.n)
            kb.s.activation(x1sq[:, :], x1t[:, :], AF.Square, inc=(scc, 1))
            kb.t.wait(rvv, rvv.n)
            kb.t.matmul(pst2[0:1, 0:512], onesf[:, :], x1t[:, :],
                        start=(ds == 0), stop=(ds == NG - 1))
            kb.t.wait(scc, scc.n)
            kb.t.matmul(pst2[0:1, 512:1024], onesf[:, :], x1sq[:, :],
                        start=(ds == 0), stop=(ds == NG - 1), inc=(mm, 1))
        kb.v.wait(mm, mm.n)
        kb.v.tensor_copy(mrow[:, :], pst2[0:1, :], inc=(rvv, 1))
        kb.t.wait(rvv, rvv.n)
        kb.v.drain()
        kb.v.tensor_scalar(mrow[:, :], mrow[:, :], 1.0 / D, None, ALU.mult)
        kb.v.drain()
        kb.v.tensor_tensor(x1sq[0:1, 0:512], mrow[:, 0:512], mrow[:, 0:512],
                           ALU.mult)
        kb.v.drain()
        kb.v.tensor_tensor(mrow[:, 512:1024], mrow[:, 512:1024],
                           x1sq[0:1, 0:512], ALU.subtract, inc=(rvv, 1))
        kb.s.wait(rvv, rvv.n)
        kb.s.activation(x1sq[0:1, 0:512], mrow[:, 512:1024], AF.Sqrt,
                        bias=epscol[0:1, :], inc=(scc, 1))
        kb.v.wait(scc, scc.n)
        kb.v.reciprocal(mrow[:, 512:1024], x1sq[0:1, 0:512], inc=(rvv, 1))
        kb.t.wait(rvv, rvv.n)
        kb.t.matmul(pb2[:, 0:512], ones1f[:, :], mrow[0:1, 0:512], start=True,
                    stop=True)
        kb.t.matmul(pb2[:, 512:1024], ones1f[:, :], mrow[0:1, 512:1024],
                    start=True, stop=True, inc=(mm, 1))
        kb.v.wait(mm, mm.n)
        kb.v.tensor_copy(mub[:, :], pb2[:, 0:512])
        kb.v.tensor_copy(rsb[:, :], pb2[:, 512:1024], inc=(rvv, 1))
        kb.y.wait(pxw, pxw.n)
        for g in range(NG):
            kb.y.wait(rvv, rvv.n)
            kb.y.dma_start(out=xtc[:, :],
                           in_=x1_fm[g, :, tc * 512:(tc + 1) * 512],
                           inc=(lsx[1], 16))
            kb.v.wait(lsx[1], lsx[1].n)
            kb.v.wait(pxw, pxw.n)
            kb.v.tensor_tensor(t1[:, :], xtc[:, :], mub[:, :], ALU.subtract)
            kb.v.drain()
            kb.v.tensor_tensor(xn1t[:, :], t1[:, :], rsb[:, :], ALU.mult,
                               inc=(rvv, 1))
            kb.y.wait(rvv, rvv.n)
            kb.y.dma_start(out=Tn["xn1"][g, :, tc * 512:(tc + 1) * 512],
                           in_=xn1t[:, :], inc=(pxw, 16))

    kb.stage_end([pxw, pw])
    kb.free_all_bufs()



def stage_D(kb, Tn):
    x1_fm, x1stats, xn1 = Tn["x1_fm"], Tn["x1stats"], Tn["xn1"]
    epscol = kb.constcol("D_eps", 1e-5)
    ones1f = kb.sbuf("D_o1f", [1, 128], F32)
    strow = kb.sbuf("D_st", [1, 8192], F32)
    musq = kb.sbuf("D_musq", [1, 4096], F32)
    mub = kb.sbuf("D_mub", [128, 512], F32)
    rsb = kb.sbuf("D_rsb", [128, 512], F32)
    x1t = [kb.sbuf(f"D_x1t{i}", [128, 512], F32) for i in range(2)]
    xnt = [kb.sbuf(f"D_xnt{i}", [128, 512], BF16) for i in range(2)]
    t1 = kb.sbuf("D_t1", [128, 512], F32)
    pb2 = kb.psum("D_pb2", [128, 1024], F32)
    ls = kb.sem("D_ls")
    lx = [kb.sem(f"D_lx{i}") for i in range(2)]
    mm = kb.sem("D_mm")
    ev = kb.sem("D_ev")
    sc = kb.sem("D_sc")
    pw = kb.sem("D_pw")

    kb.v.memset(ones1f[:, :], 1.0, inc=(kb.CONST, 1))
    kb.y.dma_start(out=strow[:, :].rearrange("o (a t) -> o a t", a=2),
                   in_=bass.AP(x1stats.tensor, 0, [[0, 1], [T, 2], [1, T]]),
                   inc=(ls, 16))
    kb.v.wait(ls, ls.n)
    kb.v.tensor_scalar(strow[:, 0:4096], strow[:, 0:4096], 1.0 / D, None,
                       ALU.mult)
    kb.v.tensor_scalar(strow[:, 4096:8192], strow[:, 4096:8192], 1.0 / D,
                       None, ALU.mult)
    kb.v.drain()
    kb.v.tensor_tensor(musq[:, :], strow[:, 0:4096], strow[:, 0:4096],
                       ALU.mult)
    kb.v.drain()
    kb.v.tensor_tensor(strow[:, 4096:8192], strow[:, 4096:8192], musq[:, :],
                       ALU.subtract, inc=(ev, 1))
    kb.s.wait(ev, ev.n)
    kb.s.wait(kb.CONST)
    kb.s.activation(musq[:, :], strow[:, 4096:8192], AF.Sqrt,
                    bias=epscol[0:1, :], inc=(sc, 1))
    kb.v.wait(sc, sc.n)
    kb.v.reciprocal(strow[:, 4096:8192], musq[:, :], inc=(ev, 1))
    for tc in range(8):
        kb.t.wait(ev, ev.n)
        kb.t.matmul(pb2[:, 0:512], ones1f[:, :],
                    strow[0:1, tc * 512:(tc + 1) * 512], start=True, stop=True)
        kb.t.matmul(pb2[:, 512:1024], ones1f[:, :],
                    strow[0:1, 4096 + tc * 512:4096 + (tc + 1) * 512],
                    start=True, stop=True, inc=(mm, 1))
        kb.v.wait(mm, mm.n)
        kb.v.tensor_copy(mub[:, :], pb2[:, 0:512])
        kb.v.tensor_copy(rsb[:, :], pb2[:, 512:1024], inc=(ev, 1))
        for g in range(NG):
            b = g % 2
            kb.y.wait(ev, ev.n)
            kb.y.dma_start(out=x1t[b][:, :],
                           in_=x1_fm[g, :, tc * 512:(tc + 1) * 512],
                           inc=(lx[b], 16))
            kb.v.wait(lx[b], lx[b].n)
            kb.v.wait(pw, pw.n)
            kb.v.tensor_tensor(t1[:, :], x1t[b][:, :], mub[:, :], ALU.subtract)
            kb.v.drain()
            kb.v.tensor_tensor(xnt[b][:, :], t1[:, :], rsb[:, :], ALU.mult,
                               inc=(ev, 1))
            kb.y.wait(ev, ev.n)
            kb.y.dma_start(out=xn1[g, :, tc * 512:(tc + 1) * 512],
                           in_=xnt[b][:, :], inc=(pw, 16))
    kb.stage_end([pw])
    kb.free_all_bufs()



def stage_E(kb, Tn):
    xn1 = Tn["xn1"]
    win, bin_, plog = Tn["l1_win"], Tn["l1_bin"], Tn["l1_plog"]
    wmid, bmid = Tn["l1_wmid"], Tn["l1_bmid"]
    w1, w1b, w2, w2b = Tn["l1_w1"], Tn["l1_w1b"], Tn["l1_w2"], Tn["l1_w2b"]
    lng, lnb = Tn["l1_lng"], Tn["l1_lnb"]
    s1 = Tn["s1"]
    nc = kb.nc
    hpark = nc.dram_tensor("hpark", [4, NG, 128, T], BF16,
                           kind="ExternalOutput" if DBG_PARKS else "Internal").ap()

    PI = float(np.pi)
    epscol = kb.constcol("E_eps", 1e-5)
    pi2col = kb.constcol("E_pi2", PI / 2.0)
    identf, sidf = build_identity(kb, "E_idf", F32)
    ones1f = kb.sbuf("E_o1f", [1, 128], F32)
    onesrow = kb.sbuf("E_orow", [1, 512], BF16)
    onesf = kb.sbuf("E_onesf", [128, 1], F32)
    msem = kb.sem("E_ms")
    kb.v.memset(ones1f[:, :], 1.0)
    kb.v.memset(onesrow[:, :], 1.0)
    kb.v.memset(onesf[:, :], 1.0, inc=(msem, 1))

    plog_sb = kb.sbuf("E_plog", [3, 1024], F32)
    pp = kb.sbuf("E_pp", [128, 24], F32)
    nu8 = kb.sbuf("E_nu8", [128, 8], F32)
    rho8 = kb.sbuf("E_rho8", [128, 8], F32)
    th8 = kb.sbuf("E_th8", [128, 8], F32)
    ga8 = kb.sbuf("E_ga8", [128, 8], F32)
    cth8 = kb.sbuf("E_cth8", [128, 8], F32)
    sth8 = kb.sbuf("E_sth8", [128, 8], F32)
    tsc = kb.sbuf("E_tsc", [128, 24], F32)
    ptp = kb.psum("E_ptp", [128, 512], F32)
    lp = kb.sem("E_lp")
    tt0 = kb.sem("E_tt0")
    ev = kb.sem("E_ev")
    sc = kb.sem("E_sc")

    kb.y.dma_start(out=plog_sb[:, :], in_=plog, inc=(lp, 16))
    kb.t.wait(sidf)
    kb.t.wait(lp)
    for g in range(NG):
        kb.t.transpose(ptp[:, g * 4:g * 4 + 3], plog_sb[:, g * 128:(g + 1) * 128],
                       identf[0:3, 0:3], inc=(tt0, 1) if g == NG - 1 else None)
    kb.v.wait(tt0, tt0.n)
    kb.v.tensor_copy(pp[:, :].rearrange("p (g e) -> p g e", e=3),
                     ptp[:, 0:32].rearrange("p (g e) -> p g e", e=4)[:, :, 0:3],
                     inc=(ev, 1))
    kb.v.drain()
    kb.v.tensor_copy(nu8[:, :], pp[:, :].rearrange("p (g e) -> p e g", e=3)[:, 0, :])
    kb.v.tensor_copy(th8[:, :], pp[:, :].rearrange("p (g e) -> p e g", e=3)[:, 1, :])
    kb.v.tensor_copy(ga8[:, :], pp[:, :].rearrange("p (g e) -> p e g", e=3)[:, 2, :],
                     inc=(ev, 1))
    kb.s.wait(ev, ev.n)
    kb.s.activation(nu8[:, :], nu8[:, :], AF.Exp)
    kb.s.drain()
    kb.s.activation(rho8[:, :], nu8[:, :], AF.Exp, scale=-1.0)
    kb.s.activation(th8[:, :], th8[:, :], AF.Exp)
    kb.s.drain()
    kb.s.activation(ga8[:, :], ga8[:, :], AF.Exp, inc=(sc, 1))
    kb.v.wait(sc, sc.n)
    kb.v.tensor_scalar(tsc[:, 0:8], th8[:, :], 1.0 / (2 * PI), MAGIC, ALU.mult,
                       ALU.add)
    kb.v.drain()
    kb.v.tensor_scalar(tsc[:, 0:8], tsc[:, 0:8], MAGIC, -2.0 * PI,
                       ALU.subtract, ALU.mult)
    kb.v.drain()
    kb.v.tensor_tensor(th8[:, :], th8[:, :], tsc[:, 0:8], ALU.add,
                       inc=(ev, 1))
    kb.s.wait(ev, ev.n)
    kb.s.wait(kb.CONST)
    kb.s.activation(tsc[:, 8:16], th8[:, :], AF.Sin, scale=0.5)
    kb.s.drain()
    kb.s.activation(tsc[:, 16:24], th8[:, :], AF.Sin, scale=-0.5,
                    bias=pi2col[:, :])
    kb.s.drain()
    kb.s.activation(tsc[:, 0:8], tsc[:, 8:16], AF.Square, inc=(sc, 1))
    kb.v.wait(sc, sc.n)
    kb.v.scalar_tensor_tensor(sth8[:, :], tsc[:, 8:16], 2.0, tsc[:, 16:24],
                              ALU.mult, ALU.mult)
    kb.v.tensor_scalar(cth8[:, :], tsc[:, 0:8], -2.0, 1.0, ALU.mult, ALU.add,
                       inc=(ev, 1))
    kb.free("E_ptp")

    xn1_sb = kb.sbuf("E_xn1", [128, NG * T], BF16)
    lxx = kb.sem("E_lxx")
    kb.y.dma_start(out=xn1_sb[:, :].rearrange("p (g t) -> p g t", t=T),
                   in_=xn1.rearrange("g p t -> p g t"), inc=(lxx, 16))

    ur = kb.sbuf("E_ur", [128, T], F32)
    ui = kb.sbuf("E_ui", [128, T], F32)
    gr = kb.sbuf("E_gr", [128, T], F32)
    gi = kb.sbuf("E_gi", [128, T], F32)
    cs = kb.sbuf("E_cs", [128, 520], F32)
    sn = kb.sbuf("E_sn", [128, 520], F32)
    rho_rep = kb.sbuf("E_rhor", [128, 512], F32)
    sc1 = kb.sbuf("E_sc1", [128, 520], F32)
    sc2 = kb.sbuf("E_sc2", [128, 520], F32)
    sc3 = kb.sbuf("E_sc3", [128, 520], F32)
    sc4 = kb.sbuf("E_sc4", [128, 520], F32)
    gcr = kb.sbuf("E_gcr", [128, 1], F32)
    gci = kb.sbuf("E_gci", [128, 1], F32)
    hb = kb.sbuf("E_hb", [128, 2048], BF16)
    wine = [kb.sbuf(f"E_wine{i}", [128, 1024], BF16) for i in range(2)]
    brow = kb.sbuf("E_brow", [1, 256], BF16)
    pu = [kb.psum(f"E_pu{i}", [128, 512], F32) for i in range(2)]
    lw = kb.sem("E_lw")
    warm = kb.sem("E_warm")
    mm = kb.sem("E_mm")
    pwh = kb.sem("E_pwh")
    gsm = kb.sem("E_gsm")
    psm = kb.sem("E_psm")

    for pg in range(NG):
        kb.g.wait(mm, mm.n)
        for e in range(2):
            srcw = win[e, :, pg * 128:(pg + 1) * 128].rearrange(
                "(g p) h -> p g h", p=128)
            kb.g.dma_start(out=wine[e][:, :].rearrange("p (g h) -> p g h", h=128),
                           in_=srcw, inc=(lw, 16))
            kb.g.dma_start(out=brow[:, e * 128:(e + 1) * 128],
                           in_=bin_[e:e + 1, pg * 128:(pg + 1) * 128],
                           inc=(lw, 16))
        RC = 512
        kb.v.wait(ev, ev.n)
        kb.v.wait(psm, psm.n)
        kb.v.memset(cs[:, 0:1], 1.0)
        kb.v.memset(sn[:, 0:1], 0.0)
        kb.v.tensor_copy(cs[:, 1:2], cth8[:, pg:pg + 1])
        kb.v.tensor_copy(sn[:, 1:2], sth8[:, pg:pg + 1])
        kb.v.drain()
        n = 2
        while n < RC + 1:
            m = min(n, RC + 1 - n)
            kb.v.tensor_scalar(sc1[:, 0:1], cs[:, n - 1:n], cth8[:, pg:pg + 1],
                               None, ALU.mult)
            kb.v.tensor_scalar(sc3[:, 0:1], sn[:, n - 1:n], sth8[:, pg:pg + 1],
                               None, ALU.mult)
            kb.v.tensor_scalar(sc2[:, 0:1], sn[:, n - 1:n], cth8[:, pg:pg + 1],
                               None, ALU.mult)
            kb.v.tensor_scalar(sc4[:, 0:1], cs[:, n - 1:n], sth8[:, pg:pg + 1],
                               None, ALU.mult)
            kb.v.drain()
            kb.v.tensor_tensor(sc1[:, 0:1], sc1[:, 0:1], sc3[:, 0:1],
                               ALU.subtract)
            kb.v.tensor_tensor(sc2[:, 0:1], sc2[:, 0:1], sc4[:, 0:1], ALU.add)
            kb.v.drain()
            kb.v.tensor_scalar(cs[:, n:n + m], cs[:, 0:m], sc1[:, 0:1], None,
                               ALU.mult)
            kb.v.tensor_scalar(sn[:, n:n + m], sn[:, 0:m], sc1[:, 0:1], None,
                               ALU.mult)
            kb.v.tensor_scalar(gr[:, 0:m], sn[:, 0:m], sc2[:, 0:1], None,
                               ALU.mult)
            kb.v.tensor_scalar(gi[:, 0:m], cs[:, 0:m], sc2[:, 0:1], None,
                               ALU.mult)
            kb.v.drain()
            kb.v.tensor_tensor(cs[:, n:n + m], cs[:, n:n + m], gr[:, 0:m],
                               ALU.subtract)
            kb.v.tensor_tensor(sn[:, n:n + m], sn[:, n:n + m], gi[:, 0:m],
                               ALU.add)
            kb.v.drain()
            n *= 2
        kb.v.tensor_tensor(sc1[:, 0:RC + 1], cs[:, 0:RC + 1], cs[:, 0:RC + 1],
                           ALU.mult)
        kb.v.tensor_tensor(sc2[:, 0:RC + 1], sn[:, 0:RC + 1], sn[:, 0:RC + 1],
                           ALU.mult)
        kb.v.drain()
        kb.v.tensor_tensor(sc1[:, 0:RC + 1], sc1[:, 0:RC + 1],
                           sc2[:, 0:RC + 1], ALU.add, inc=(ev, 1))
        kb.s.wait(ev, ev.n)
        kb.s.activation(sc2[:, 0:RC + 1], sc1[:, 0:RC + 1], AF.Sqrt,
                        inc=(sc, 1))
        kb.v.wait(sc, sc.n)
        kb.v.reciprocal(sc1[:, 0:RC + 1], sc2[:, 0:RC + 1])
        kb.v.drain()
        kb.v.tensor_tensor(cs[:, 0:RC + 1], cs[:, 0:RC + 1], sc1[:, 0:RC + 1],
                           ALU.mult)
        kb.v.tensor_tensor(sn[:, 0:RC + 1], sn[:, 0:RC + 1], sc1[:, 0:RC + 1],
                           ALU.mult)
        kb.v.drain()
        kb.v.memset(rho_rep[:, 0:RC], 0.0)
        kb.v.drain()
        kb.v.tensor_scalar(rho_rep[:, 0:RC], rho_rep[:, 0:RC],
                           rho8[:, pg:pg + 1], None, ALU.add)
        kb.v.drain()
        kb.t.wait(lw)
        if pg == 0:
            kb.t.wait(lxx)
            kb.t.wait(msem)
            kb.t.wait(ev, ev.n)
        for e, ubuf in ((0, ur), (1, ui)):
            for tc in range(8):
                pb = pu[tc % 2]
                kb.t.wait(ev, max(0, ev.n - 1))
                for g in range(NG):
                    kb.t.matmul(pb[:, :], wine[e][:, g * 128:(g + 1) * 128],
                                xn1_sb[:, g * T + tc * 512:g * T + (tc + 1) * 512],
                                start=(g == 0), stop=False)
                kb.t.matmul(pb[:, :], brow[:, e * 128:(e + 1) * 128],
                            onesrow[:, 0:512],
                            start=False, stop=True, inc=(mm, 1))
                kb.v.wait(mm, mm.n)
                kb.v.tensor_copy(ubuf[:, tc * 512:(tc + 1) * 512], pb[:, :],
                                 inc=(ev, 1))
        for ch in range(T // RC):
            s_ = slice(ch * RC, (ch + 1) * RC)
            kb.v.tensor_tensor(sc1[:, 0:RC], ur[:, s_], cs[:, 0:RC], ALU.mult)
            kb.v.tensor_tensor(sc2[:, 0:RC], ui[:, s_], sn[:, 0:RC], ALU.mult)
            kb.v.tensor_tensor(sc3[:, 0:RC], ui[:, s_], cs[:, 0:RC], ALU.mult)
            kb.v.tensor_tensor(sc4[:, 0:RC], ur[:, s_], sn[:, 0:RC], ALU.mult)
            kb.v.drain()
            kb.v.tensor_tensor(sc1[:, 0:RC], sc1[:, 0:RC], sc2[:, 0:RC],
                               ALU.add)
            kb.v.tensor_tensor(sc3[:, 0:RC], sc3[:, 0:RC], sc4[:, 0:RC],
                               ALU.subtract)
            kb.v.drain()
            kb.v.tensor_scalar(ur[:, s_], sc1[:, 0:RC], ga8[:, pg:pg + 1],
                               None, ALU.mult)
            kb.v.tensor_scalar(ui[:, s_], sc3[:, 0:RC], ga8[:, pg:pg + 1],
                               None, ALU.mult)
            kb.v.drain()
        kb.v.memset(gcr[:, :], 0.0)
        kb.v.memset(gci[:, :], 0.0)
        kb.v.drain()
        for ch in range(T // RC):
            s_ = slice(ch * RC, (ch + 1) * RC)
            kb.v.tensor_tensor_scan(gr[:, s_], rho_rep[:, 0:RC], ur[:, s_],
                                    gcr[:, :], ALU.mult, ALU.add)
            kb.v.tensor_tensor_scan(gi[:, s_], rho_rep[:, 0:RC], ui[:, s_],
                                    gci[:, :], ALU.mult, ALU.add)
            kb.v.drain()
            if ch < T // RC - 1:
                e_ = slice((ch + 1) * RC - 1, (ch + 1) * RC)
                kb.v.tensor_tensor(sc1[:, 0:1], gr[:, e_], cs[:, RC:RC + 1],
                                   ALU.mult)
                kb.v.tensor_tensor(sc2[:, 0:1], gi[:, e_], sn[:, RC:RC + 1],
                                   ALU.mult)
                kb.v.tensor_tensor(sc3[:, 0:1], gr[:, e_], sn[:, RC:RC + 1],
                                   ALU.mult)
                kb.v.tensor_tensor(sc4[:, 0:1], gi[:, e_], cs[:, RC:RC + 1],
                                   ALU.mult)
                kb.v.drain()
                kb.v.tensor_tensor(gcr[:, :], sc1[:, 0:1], sc2[:, 0:1],
                                   ALU.subtract)
                kb.v.tensor_tensor(gci[:, :], sc3[:, 0:1], sc4[:, 0:1],
                                   ALU.add)
                kb.v.drain()
        for tc in range(8):
            kb.v.wait(pwh, pwh.n)
            for half in range(1):
                s_ = slice(tc * 512, (tc + 1) * 512)
                o_ = slice(0, 512)
                kb.v.tensor_tensor(sc1[:, o_], gr[:, s_], cs[:, 0:RC], ALU.mult)
                kb.v.tensor_tensor(sc2[:, o_], gi[:, s_], sn[:, 0:RC], ALU.mult)
                kb.v.tensor_tensor(sc3[:, o_], gr[:, s_], sn[:, 0:RC], ALU.mult)
                kb.v.tensor_tensor(sc4[:, o_], gi[:, s_], cs[:, 0:RC], ALU.mult)
            kb.v.drain()
            kb.v.tensor_tensor(sc1[:, 0:512], sc1[:, 0:512], sc2[:, 0:512],
                               ALU.subtract)
            kb.v.tensor_tensor(sc3[:, 0:512], sc3[:, 0:512], sc4[:, 0:512],
                               ALU.add)
            kb.v.drain()
            kb.v.tensor_copy(hb[:, 0:512], sc1[:, 0:512])
            kb.v.tensor_copy(hb[:, 512:1024], sc3[:, 0:512])
            kb.v.drain()
            kb.v.tensor_tensor(hb[:, 1024:1536], sc1[:, 0:512], hb[:, 0:512],
                               ALU.subtract)
            kb.v.tensor_tensor(hb[:, 1536:2048], sc3[:, 0:512], hb[:, 512:1024],
                               ALU.subtract, inc=(ev, 1))
            kb.y.wait(ev, ev.n)
            kb.y.dma_start(out=hpark[:, pg, :, tc * 512:(tc + 1) * 512]
                           .rearrange("e p t -> p e t"),
                           in_=hb[:, :].rearrange("p (e t) -> p e t", t=512),
                           inc=(pwh, 16))
    kb.stage_end([pwh])
    kb.free("E_pu1", "E_pu0", "E_brow", "E_wine1", "E_wine0", "E_hb",
            "E_hb" if False else "E_gci", "E_gcr",
            "E_sc4", "E_sc3", "E_sc2", "E_sc1", "E_rhor", "E_sn", "E_cs",
            "E_gi", "E_gr", "E_ui", "E_ur", "E_xn1", "E_tsc", "E_sth8",
            "E_cth8", "E_ga8", "E_th8", "E_rho8", "E_nu8", "E_pp", "E_plog")

    wm0 = kb.sbuf("E_wm0", [128, 8192], BF16)
    wm1 = kb.sbuf("E_wm1", [128, 8192], BF16)
    wi2 = kb.sbuf("E_wi2", [128, 8192], BF16)
    w1s = kb.sbuf("E_w1s", [128, 8192], BF16)
    w2s = kb.sbuf("E_w2s", [128, 8192], BF16)
    bcomb = kb.sbuf("E_bcomb", [1, 1024], BF16)
    w1br = kb.sbuf("E_w1br", [1, 1024], BF16)
    w2br = kb.sbuf("E_w2br", [1, 1024], BF16)
    gcols = kb.sbuf("E_gcols", [128, 8], F32)
    bcols = kb.sbuf("E_bcols", [128, 8], F32)
    hrt = [kb.sbuf(f"E_hrt{i}", [128, 4096], BF16) for i in range(2)]
    hit = [kb.sbuf(f"E_hit{i}", [128, 4096], BF16) for i in range(2)]
    hrtl = [kb.sbuf(f"E_hrtl{i}", [128, 4096], BF16) for i in range(2)]
    hitl = [kb.sbuf(f"E_hitl{i}", [128, 4096], BF16) for i in range(2)]
    xntc = [kb.sbuf(f"E_xntc{i}", [128, 4096], BF16) for i in range(2)]
    outb = kb.sbuf("E_outb", [128, 4096], BF16)
    y1s = kb.sbuf("E_y1s", [128, 4096], BF16)
    ysb = kb.sbuf("E_ysb", [128, 4096], BF16)
    onesb2 = kb.sbuf("E_ob2", [128, 1], BF16)
    kb.v.memset(onesb2[:, :], 1.0)
    mrow = kb.sbuf("E_mrow", [1, 1024], F32)
    mub = kb.sbuf("E_mub", [128, 512], F32)
    rsb = kb.sbuf("E_rsb", [128, 512], F32)
    s1t = kb.sbuf("E_s1t", [128, 512], BF16)
    po2 = [kb.psum(f"E_po{i}", [128, 512], F32) for i in range(2)]
    pst2 = kb.psum("E_pst2", [1, 1024], F32)
    pb2 = kb.psum("E_pb2", [128, 1024], F32)
    brows = kb.sbuf("E_brows", [1, 1024], F32)
    lw2 = kb.sem("E_lw2")
    lh = [kb.sem(f"E_lh{i}") for i in range(2)]
    rvv = kb.sem("E_rvv")
    scc = kb.sem("E_scc")
    psw = kb.sem("E_psw")

    for name, srcw in (("E_wm0", wmid[0]), ("E_wm1", wmid[1]),
                       ("E_wi2", win[2]), ("E_w1s", w1), ("E_w2s", w2)):
        dst = {"E_wm0": wm0, "E_wm1": wm1, "E_wi2": wi2, "E_w1s": w1s,
               "E_w2s": w2s}[name]
        kb.g.dma_start(out=dst[:, :].rearrange("p (g h) -> p g h", h=1024),
                       in_=srcw.rearrange("(g p) h -> p g h", p=128),
                       inc=(lw2, 16))
    lbr = kb.sem("E_lbr")
    kb.y.dma_start(out=brows[:, :], in_=bmid[0:1, :], inc=(lbr, 16))
    kb.y.dma_start(out=mrow[:, :], in_=bmid[1:2, :], inc=(lbr, 16))
    kb.g.dma_start(out=w1br[:, :], in_=w1b, inc=(lw2, 16))
    kb.g.dma_start(out=w2br[:, :], in_=w2b, inc=(lw2, 16))
    kb.g.dma_start(out=gcols[:, :],
                   in_=bass.AP(lng.tensor, 0, [[1, 128], [128, 8]]),
                   allow_slow_non_contiguous=True, inc=(lw2, 16))
    kb.g.dma_start(out=bcols[:, :],
                   in_=bass.AP(lnb.tensor, 0, [[1, 128], [128, 8]]),
                   allow_slow_non_contiguous=True, inc=(lw2, 16))
    kb.v.wait(lw2, lw2.n)
    kb.v.tensor_scalar(wm1[:, :], wm1[:, :], -1.0, None, ALU.mult, inc=(ev, 1))
    kb.v.wait(lbr, lbr.n)
    kb.v.tensor_tensor(brows[:, :], brows[:, :], mrow[:, :], ALU.subtract,
                       inc=(ev, 1))
    kb.y.wait(ev, ev.n)
    kb.y.dma_start(out=mrow[:, :], in_=bin_[2:3, :], inc=(lbr, 16))
    kb.v.wait(lbr, lbr.n)
    kb.v.tensor_tensor(brows[:, :], brows[:, :], mrow[:, :], ALU.add)
    kb.v.drain()
    kb.v.tensor_copy(bcomb[:, :], brows[:, :], inc=(ev, 1))
    kb.free("E_brows")

    mh = [0] * 9
    lh_hist = [0, 0]

    def issue_eload(tc):
        bqe = tc % 2
        ts2 = slice(tc * 512, (tc + 1) * 512)
        kb.y.wait(mm, mh[tc - 2] if tc >= 2 else 0)
        for arr, park_i in ((hrt, 0), (hit, 1), (hrtl, 2), (hitl, 3)):
            kb.y.dma_start(out=arr[bqe][:, :].rearrange("p (g t) -> p g t",
                                                        t=512),
                           in_=hpark[park_i, :, :, ts2].rearrange(
                               "g p t -> p g t"), inc=(lh[bqe], 16))
        kb.y.dma_start(out=xntc[bqe][:, :].rearrange("p (g t) -> p g t",
                                                     t=512),
                       in_=xn1[:, :, ts2].rearrange("g p t -> p g t"),
                       inc=(lh[bqe], 16))
        lh_hist[bqe] = lh[bqe].n

    issue_eload(0)
    for tc in range(8):
        bq = tc % 2
        ts_ = slice(tc * 512, (tc + 1) * 512)
        if tc + 1 < 8:
            issue_eload(tc + 1)
        kb.t.wait(lh[bq], lh_hist[bq])
        kb.t.wait(ev, ev.n)
        for hm in range(NG):
            pb = po2[hm % 2]
            kb.t.wait(rvv, max(0, rvv.n - 1))
            for g in range(NG):
                kb.t.matmul(pb[:, :], wm0[:, g * 1024 + hm * 128:g * 1024 + (hm + 1) * 128],
                            hrt[bq][:, g * 512:(g + 1) * 512],
                            start=(g == 0), stop=False)
            for g in range(NG):
                kb.t.matmul(pb[:, :], wm0[:, g * 1024 + hm * 128:g * 1024 + (hm + 1) * 128],
                            hrtl[bq][:, g * 512:(g + 1) * 512],
                            start=False, stop=False)
            for g in range(NG):
                kb.t.matmul(pb[:, :], wm1[:, g * 1024 + hm * 128:g * 1024 + (hm + 1) * 128],
                            hit[bq][:, g * 512:(g + 1) * 512],
                            start=False, stop=False)
            for g in range(NG):
                kb.t.matmul(pb[:, :], wm1[:, g * 1024 + hm * 128:g * 1024 + (hm + 1) * 128],
                            hitl[bq][:, g * 512:(g + 1) * 512],
                            start=False, stop=False)
            for g in range(NG):
                kb.t.matmul(pb[:, :], wi2[:, g * 1024 + hm * 128:g * 1024 + (hm + 1) * 128],
                            xntc[bq][:, g * 512:(g + 1) * 512],
                            start=False, stop=False)
            kb.t.matmul(pb[:, :], bcomb[:, hm * 128:(hm + 1) * 128],
                        onesrow[:, 0:512], start=False, stop=True, inc=(mm, 1))
            kb.v.wait(mm, mm.n)
            kb.v.tensor_copy(outb[:, hm * 512:(hm + 1) * 512], pb[:, :],
                             inc=(rvv, 1))
        mh[tc] = mm.n
        for jsl in range(NG):
            pb = po2[jsl % 2]
            kb.t.wait(rvv, rvv.n)
            kb.t.wait(scc, max(0, scc.n - 1))
            for g in range(NG):
                kb.t.matmul(pb[:, :], w1s[:, g * 1024 + jsl * 128:g * 1024 + (jsl + 1) * 128],
                            outb[:, g * 512:(g + 1) * 512],
                            start=(g == 0), stop=False)
            kb.t.matmul(pb[:, :], w1br[:, jsl * 128:(jsl + 1) * 128],
                        onesrow[:, 0:512], start=False, stop=True, inc=(mm, 1))
            kb.s.wait(mm, mm.n)
            kb.s.activation(y1s[:, jsl * 512:(jsl + 1) * 512], pb[:, :],
                            AF.Gelu, inc=(scc, 1))
        for ds in range(NG):
            pb = po2[ds % 2]
            kb.t.wait(scc, scc.n)
            kb.t.wait(rvv, max(0, rvv.n - 1))
            for g in range(NG):
                kb.t.matmul(pb[:, :], w2s[:, g * 1024 + ds * 128:g * 1024 + (ds + 1) * 128],
                            y1s[:, g * 512:(g + 1) * 512],
                            start=(g == 0), stop=False)
            kb.t.matmul(pb[:, :], w2br[:, ds * 128:(ds + 1) * 128],
                        onesrow[:, 0:512], start=False, stop=True, inc=(mm, 1))
            kb.v.wait(mm, mm.n)
            kb.v.tensor_tensor(ysb[:, ds * 512:(ds + 1) * 512], pb[:, :],
                               outb[:, ds * 512:(ds + 1) * 512], ALU.add,
                               inc=(rvv, 1))
            kb.s.wait(rvv, rvv.n)
            kb.s.activation(s1t[:, :], ysb[:, ds * 512:(ds + 1) * 512],
                            AF.Square, inc=(scc, 1))
            kb.t.wait(rvv, rvv.n)
            kb.t.matmul(pst2[0:1, 0:512], onesb2[:, :],
                        ysb[:, ds * 512:(ds + 1) * 512],
                        start=(ds == 0), stop=(ds == NG - 1))
            kb.t.wait(scc, scc.n)
            kb.t.matmul(pst2[0:1, 512:1024], onesb2[:, :], s1t[:, :],
                        start=(ds == 0), stop=(ds == NG - 1), inc=(mm, 1))
        kb.v.wait(mm, mm.n)
        kb.v.tensor_scalar(mrow[:, :], pst2[0:1, :], 1.0 / D, None, ALU.mult,
                           inc=(rvv, 1))
        kb.v.drain()
        kb.v.tensor_tensor(rsb[0:1, 0:512], mrow[:, 0:512], mrow[:, 0:512],
                           ALU.mult)
        kb.v.drain()
        kb.v.tensor_tensor(mrow[:, 512:1024], mrow[:, 512:1024],
                           rsb[0:1, 0:512], ALU.subtract, inc=(rvv, 1))
        kb.s.wait(rvv, rvv.n)
        kb.s.activation(mub[0:1, 0:512], mrow[:, 512:1024], AF.Sqrt,
                        bias=epscol[0:1, :], inc=(scc, 1))
        kb.v.wait(scc, scc.n)
        kb.v.reciprocal(mrow[:, 512:1024], mub[0:1, 0:512], inc=(rvv, 1))
        kb.t.wait(rvv, rvv.n)
        kb.t.matmul(pb2[:, 0:512], ones1f[:, :], mrow[:, 0:512], start=True,
                    stop=True)
        kb.t.matmul(pb2[:, 512:1024], ones1f[:, :], mrow[:, 512:1024],
                    start=True, stop=True, inc=(mm, 1))
        kb.v.wait(mm, mm.n)
        kb.v.tensor_copy(mub[:, :], pb2[:, 0:512])
        kb.v.tensor_copy(rsb[:, :], pb2[:, 512:1024])
        kb.v.drain()
        for ds in range(NG):
            kb.v.wait(psw, psw.n)
            kb.v.tensor_tensor(ysb[:, ds * 512:(ds + 1) * 512],
                               ysb[:, ds * 512:(ds + 1) * 512],
                               mub[:, :], ALU.subtract)
            kb.v.drain()
            kb.v.tensor_tensor(ysb[:, ds * 512:(ds + 1) * 512],
                               ysb[:, ds * 512:(ds + 1) * 512],
                               rsb[:, :], ALU.mult)
            kb.v.drain()
            kb.v.tensor_scalar(s1t[:, :], ysb[:, ds * 512:(ds + 1) * 512],
                               gcols[:, ds:ds + 1],
                               bcols[:, ds:ds + 1], ALU.mult, ALU.add,
                               inc=(rvv, 1))
            kb.y.wait(rvv, rvv.n)
            kb.y.dma_start(out=s1[ds, :, ts_], in_=s1t[:, :], inc=(psw, 16))
        kb.t.wait(rvv, rvv.n)

    kb.stage_end([psw])
    kb.free_all_bufs()



GELU_C = float(np.sqrt(2.0 / np.pi))
GELU_S = float(np.sqrt(0.044715))


def stage_F(kb, Tn):
    xn1, x1_fm, s1 = Tn["xn1"], Tn["x1_fm"], Tn["s1"]
    wfcs, bfcs = Tn["m_wfcs"], Tn["m_bfcs"]
    wprojs, bprojs = Tn["m_wprojs"], Tn["m_bprojs"]
    wr = Tn["m_wr"]
    wfc, bfc = Tn["m_wfc"], Tn["m_bfc"]
    wproj, bproj = Tn["m_wproj"], Tn["m_bproj"]
    out = Tn["out"]
    nc = kb.nc
    gate_d = nc.dram_tensor("gate_d", [8, T], F32, kind="Internal").ap()

    identf, sidf = build_identity(kb, "F_idf", F32)
    ones1f = kb.sbuf("F_o1f", [1, 128], F32)
    ones1b = kb.sbuf("F_o1b", [1, 128], BF16)
    onesrow = kb.sbuf("F_orow", [1, 512], BF16)
    msem = kb.sem("F_ms")
    kb.v.memset(ones1f[:, :], 1.0)
    kb.v.memset(ones1b[:, :], 1.0)
    kb.v.memset(onesrow[:, :], 1.0, inc=(msem, 1))

    wr_sb = kb.sbuf("F_wr", [128, 64], BF16)
    gate_fm = kb.sbuf("F_gfm", [8, T], F32)
    xg = [kb.sbuf(f"F_xg{i}", [128, 1024], BF16) for i in range(2)]
    gt = kb.sbuf("F_gt", [128, 24], F32)
    gtm = kb.sbuf("F_gtm", [128, 8], F32)
    pl = kb.psum("F_pl", [128, 512], F32)
    pgt = kb.psum("F_pgt", [8, 512], F32)
    lg = [kb.sem(f"F_lg{i}") for i in range(2)]
    mm = kb.sem("F_mm")
    ev = kb.sem("F_ev")
    sc = kb.sem("F_sc")

    kb.g.dma_start(out=wr_sb[:, :].rearrange("p (g e) -> p g e", e=8),
                   in_=wr.rearrange("(g p) e -> p g e", p=128), inc=(lg[0], 16))
    for tt in range(32):
        b = tt % 2
        kb.y.wait(mm, mm.n)
        kb.y.dma_start(out=xg[b][:, :].rearrange("p (g t) -> p g t", t=128),
                       in_=xn1[:, :, tt * 128:(tt + 1) * 128].rearrange(
                           "g p t -> p g t"), inc=(lg[b], 16))
        kb.t.wait(lg[b], lg[b].n)
        if tt == 0:
            kb.t.wait(lg[0], lg[0].n)
        kb.t.wait(ev, ev.n)
        for g in range(NG):
            kb.t.matmul(pl[0:128, 0:8], xg[b][:, g * 128:(g + 1) * 128],
                        wr_sb[:, g * 8:(g + 1) * 8],
                        start=(g == 0), stop=(g == NG - 1), inc=(mm, 1))
        kb.v.wait(mm, mm.n)
        kb.v.tensor_copy(gt[:, 0:8], pl[0:128, 0:8])
        kb.v.drain()
        kb.v.reduce_max(gt[:, 8:9], gt[:, 0:8], axis=AX.X)
        kb.v.drain()
        kb.v.tensor_scalar(gt[:, 9:10], gt[:, 8:9], -1.0, None, ALU.mult)
        kb.v.tensor_scalar(gt[:, 16:24], gt[:, 0:8], gt[:, 8:9], None,
                           ALU.is_ge)
        kb.v.drain()
        kb.v.scalar_tensor_tensor(gt[:, 16:24], gt[:, 16:24], -1e30,
                                  gt[:, 0:8], ALU.mult, ALU.add)
        kb.v.drain()
        kb.v.reduce_max(gt[:, 10:11], gt[:, 16:24], axis=AX.X, inc=(ev, 1))
        kb.s.wait(ev, ev.n)
        kb.s.activation(gt[:, 16:24], gt[:, 0:8], AF.Exp, bias=gt[:, 9:10],
                        inc=(sc, 1))
        kb.v.wait(sc, sc.n)
        kb.v.tensor_scalar(gt[:, 0:8], gt[:, 0:8], gt[:, 10:11], None,
                           ALU.is_ge)
        kb.v.drain()
        kb.v.tensor_tensor(gtm[:, :], gt[:, 16:24], gt[:, 0:8], ALU.mult)
        kb.v.drain()
        kb.v.reduce_sum(gt[:, 11:12], gtm[:, :], axis=AX.X)
        kb.v.drain()
        kb.v.reciprocal(gt[:, 12:13], gt[:, 11:12])
        kb.v.drain()
        kb.v.tensor_scalar(gtm[:, :], gtm[:, :], gt[:, 12:13], None, ALU.mult,
                           inc=(ev, 1))
        kb.t.wait(ev, ev.n)
        kb.t.transpose(pgt[0:8, 0:128], gtm[:, :], identf[:, :], inc=(mm, 1))
        kb.v.wait(mm, mm.n)
        kb.v.tensor_copy(gate_fm[:, tt * 128:(tt + 1) * 128], pgt[0:8, 0:128],
                         inc=(ev, 1))
        kb.t.wait(ev, ev.n)
    kb.y.wait(ev, ev.n)
    gw = kb.sem("F_gw")
    kb.y.dma_start(out=gate_d, in_=gate_fm[:, :], inc=(gw, 16))
    kb.stage_end([gw])
    kb.free("F_pgt", "F_pl", "F_gtm", "F_gt", "F_xg1", "F_xg0", "F_gfm")

    TH = 2048
    routed = kb.sbuf("F_rout", [128, NG * TH], F32)
    xnth = kb.sbuf("F_xnth", [128, NG * TH], BF16)
    wfcq = [kb.sbuf(f"F_wfcq{i}", [128, 8192], BF16) for i in range(2)]
    wprojq = [kb.sbuf(f"F_wprojq{i}", [128, 8192], BF16) for i in range(2)]
    hq = [kb.sbuf(f"F_hq{i}", [128, 8 * 512], BF16) for i in range(2)]
    gb = kb.sbuf("F_gb", [128, TH], BF16)
    grow = kb.sbuf("F_grow", [1, TH], BF16)
    bfcr = [kb.sbuf(f"F_bfcr{i}", [1, 1024], BF16) for i in range(2)]
    bprojr0 = kb.sbuf("F_bprojr", [1, 1024], BF16)
    bprojr = [bprojr0, bprojr0]
    asb = [kb.sbuf(f"F_asb{i}", [128, 512], F32) for i in range(2)]
    bsb = [kb.sbuf(f"F_bsb{i}", [128, 512], F32) for i in range(2)]
    usb = [kb.sbuf(f"F_usb{i}", [128, 512], F32) for i in range(2)]
    g1t = kb.sbuf("F_g1t", [128, 512], BF16)
    s1t = kb.sbuf("F_s1t", [128, 512], BF16)
    x1t = asb[0]
    x2t = bsb[0]
    ost = usb[0]
    ph = [kb.psum(f"F_ph{i}", [128, 512], F32) for i in range(4)]
    pp = [kb.psum(f"F_pp{i}", [128, 512], F32) for i in range(2)]
    pgb = kb.psum("F_pgb", [128, 512], F32)
    ptr = pgb
    lw = [kb.sem(f"F_lw{i}") for i in range(2)]
    vch = kb.sem("F_vch")
    lx = kb.sem("F_lx")
    lxs = kb.sem("F_lxs")
    rvv = kb.sem("F_rvv")
    scc = kb.sem("F_scc")
    pow_ = kb.sem("F_pow")

    iters = [(th, e, jq) for th in range(2) for e in range(9)
             for jq in range(4)]
    itr = kb.sem("F_itr")

    def issue_load(k, part):
        th, e, jq = iters[k]
        b = k % 2
        if part == 0:
            kb.g.wait(itr, max(0, k - 1))
            wsrc = wfcs if e == 0 else wfc[e - 1]
            kb.g.dma_start(
                out=wfcq[b][:, :].rearrange("p (g j) -> p g j", j=1024),
                in_=wsrc[:, jq * 1024:(jq + 1) * 1024].rearrange(
                    "(g p) j -> p g j", p=128), inc=(lw[b], 16))
        elif part == 1:
            psrc = wprojs if e == 0 else wproj[e - 1]
            kb.g.dma_start(
                out=wprojq[b][:, :].rearrange("p (c d) -> p c d", d=1024),
                in_=psrc[jq * 1024:(jq + 1) * 1024, :].rearrange(
                    "(c p) d -> p c d", p=128), inc=(lw[b], 16))
        else:
            bsrc = bfcs if e == 0 else bfc[e - 1:e]
            kb.g.dma_start(out=bfcr[b][:, :],
                           in_=bsrc[0:1, jq * 1024:(jq + 1) * 1024],
                           inc=(lw[b], 16))
            if jq == 0:
                bp = bprojs if e == 0 else bproj[e - 1:e]
                kb.g.dma_start(out=bprojr0[:, :], in_=bp[0:1, :],
                               inc=(lw[b], 16))

    for part in range(3):
        issue_load(0, part)
    lw_hist = [lw[0].n]
    for k, (th, e, jq) in enumerate(iters):
        b = k % 2
        t0 = th * TH
        if jq == 0:
            if e == 0:
                kb.y.wait(mm, mm.n)
                kb.y.dma_start(
                    out=xnth[:, :].rearrange("p (g t) -> p g t", t=TH),
                    in_=xn1[:, :, t0:t0 + TH].rearrange("g p t -> p g t"),
                    inc=(lx, 16))
                kb.v.wait(scc, scc.n)
                kb.v.memset(gb[:, :], 0.5, inc=(ev, 1))
            else:
                kb.g.wait(mm, mm.n)
                kb.g.dma_start(out=grow[:, :],
                               in_=gate_d[e - 1:e, t0:t0 + TH], inc=(ev, 16))
                for q in range(4):
                    kb.t.wait(ev, ev.n)
                    kb.t.wait(rvv, rvv.n)
                    kb.t.matmul(pgb[:, :], ones1b[:, :],
                                grow[:, q * 512:(q + 1) * 512], start=True,
                                stop=True, inc=(mm, 1))
                    kb.v.wait(mm, mm.n)
                    kb.v.wait(scc, scc.n)
                    kb.v.tensor_scalar(gb[:, q * 512:(q + 1) * 512], pgb[:, :],
                                       0.5, None, ALU.mult, inc=(rvv, 1))
        fc_scc = {}

        def fc_block(tc4):
            c0 = tc4 * 512
            hb_ = hq[tc4 % 2]
            kb.t.wait(lw[b], lw_hist[k])
            if e == 0 and jq == 0 and tc4 == 0:
                kb.t.wait(lx, lx.n)
                kb.t.wait(msem)
                kb.v.wait(pow_, pow_.n)
            for jsl in range(NG):
                i2 = (tc4 * NG + jsl) % 2
                pb = ph[(tc4 * NG + jsl) % 4]
                kb.t.wait(scc, max(0, scc.n - 3))
                for g in range(NG):
                    kb.t.matmul(
                        pb[:, :],
                        wfcq[b][:, g * 1024 + jsl * 128:g * 1024 + (jsl + 1) * 128],
                        xnth[:, g * TH + c0:g * TH + c0 + 512],
                        start=(g == 0), stop=False)
                kb.t.matmul(pb[:, :], bfcr[b][:, jsl * 128:(jsl + 1) * 128],
                            onesrow[:, 0:512], start=False, stop=True,
                            inc=(mm, 1))
                kb.s.wait(mm, mm.n)
                kb.s.wait(vch, max(0, vch.n - 2))
                kb.s.activation(asb[i2][:, :], pb[:, :], AF.Square,
                                scale=GELU_S, inc=(sc, 1))
                kb.v.wait(sc, sc.n)
                kb.v.scalar_tensor_tensor(bsb[i2][:, :], asb[i2][:, :], 1.0,
                                          pb[:, :], ALU.add, ALU.mult,
                                          inc=(vch, 1))
                kb.s.wait(vch, vch.n)
                kb.s.activation(asb[i2][:, :], bsb[i2][:, :], AF.Tanh,
                                scale=GELU_C, inc=(sc, 1))
                kb.v.wait(sc, sc.n)
                kb.v.wait(scc, max(0, scc.n - 1))
                kb.v.scalar_tensor_tensor(usb[i2][:, :], asb[i2][:, :], 1.0,
                                          pb[:, :], ALU.add, ALU.mult,
                                          inc=(vch, 1))
                kb.g.wait(vch, vch.n)
                kb.g.wait(mm, mm.n)
                kb.g.tensor_tensor(hb_[:, jsl * 512:(jsl + 1) * 512],
                                   usb[i2][:, :], gb[:, c0:c0 + 512], ALU.mult,
                                   inc=(scc, 1))
            fc_scc[tc4] = scc.n

        def proj_block(tc4):
            c0 = tc4 * 512
            hb_ = hq[tc4 % 2]
            for ds in range(NG):
                pb = pp[ds % 2]
                kb.t.wait(scc, fc_scc[tc4])
                kb.t.wait(rvv, max(0, rvv.n - 1))
                for jc in range(NG):
                    last = (jq != 0 and jc == NG - 1)
                    kb.t.matmul(
                        pb[:, :],
                        wprojq[b][:, jc * 1024 + ds * 128:jc * 1024 + (ds + 1) * 128],
                        hb_[:, jc * 512:(jc + 1) * 512],
                        start=(jc == 0), stop=last,
                        inc=(mm, 1) if last else None)
                if jq == 0:
                    rrhs = onesrow[:, 0:512] if e == 0 else \
                        grow[:, c0:c0 + 512]
                    kb.t.matmul(pb[:, :],
                                bprojr[e % 2][:, ds * 128:(ds + 1) * 128],
                                rrhs, start=False, stop=True, inc=(mm, 1))
                if tc4 == 3 and ds == NG - 1:
                    kb.t.nop(inc=(itr, 1))
                rsl = routed[:, ds * TH + c0:ds * TH + c0 + 512]
                kb.v.wait(mm, mm.n)
                if e == 0 and jq == 0:
                    kb.v.tensor_copy(rsl, pb[:, :], inc=(rvv, 1))
                else:
                    kb.v.tensor_tensor(rsl, rsl, pb[:, :], ALU.add,
                                       inc=(rvv, 1))

        fc_block(0)
        if k + 1 < len(iters):
            issue_load(k + 1, 0)
        fc_block(1)
        proj_block(0)
        if k + 1 < len(iters):
            issue_load(k + 1, 1)
        fc_block(2)
        proj_block(1)
        if k + 1 < len(iters):
            issue_load(k + 1, 2)
            lw_hist.append(lw[(k + 1) % 2].n)
        fc_block(3)
        proj_block(2)
        proj_block(3)
        if jq == 3 and e == 8:
            kb.t.wait(sidf)
            for ds in range(NG):
                for tc4 in range(4):
                    c0 = tc4 * 512
                    kb.y.wait(rvv, rvv.n)
                    kb.y.wait(scc, scc.n)
                    kb.y.wait(vch, vch.n)
                    kb.y.dma_start(out=x1t[:, :],
                                   in_=x1_fm[ds, :, t0 + c0:t0 + c0 + 512],
                                   inc=(lxs, 16))
                    kb.y.dma_start(out=s1t[:, :],
                                   in_=s1[ds, :, t0 + c0:t0 + c0 + 512],
                                   inc=(lxs, 16))
                    kb.s.wait(rvv, rvv.n)
                    kb.s.wait(scc, scc.n)
                    kb.s.activation(g1t[:, :],
                                    routed[:, ds * TH + c0:ds * TH + c0 + 512],
                                    AF.Silu, inc=(sc, 1))
                    kb.v.wait(sc, sc.n)
                    kb.v.wait(lxs, lxs.n)
                    kb.v.tensor_tensor(x2t[:, :], s1t[:, :], g1t[:, :],
                                       ALU.mult)
                    kb.v.drain()
                    kb.v.tensor_tensor(x2t[:, :], x2t[:, :], x1t[:, :],
                                       ALU.add, inc=(rvv, 1))
                    for q in range(4):
                        kb.t.wait(rvv, rvv.n)
                        kb.t.wait(ev, max(0, ev.n))
                        kb.t.transpose(ptr[:, 0:128],
                                       x2t[:, q * 128:(q + 1) * 128],
                                       identf[:, :], inc=(mm, 1))
                        kb.v.wait(mm, mm.n)
                        kb.v.tensor_copy(ost[:, q * 128:(q + 1) * 128],
                                         ptr[:, 0:128], inc=(ev, 1))
                    kb.y.wait(ev, ev.n)
                    src_o = ost[:, :].rearrange("p (q j) -> p q j", j=128)
                    dst_o = bass.AP(out.tensor, (t0 + c0) * D + ds * 128,
                                    [[D, 128], [128 * D, 4], [1, 128]])
                    kb.y.dma_start(out=dst_o, in_=src_o, inc=(pow_, 16))

    kb.stage_end([pow_])
    kb.free_all_bufs()



def core_input_arrays(inputs, core):
    A = {k: np.asarray(v) for k, v in inputs.items()}
    d = {}
    d["x"] = np.ascontiguousarray(A["L0_attn_x"][core])
    d["wq_t"] = np.ascontiguousarray(A["L0_attn_Wq"].T)
    d["wk_t"] = np.ascontiguousarray(A["L0_attn_Wk"].T)
    d["wv_t"] = np.ascontiguousarray(A["L0_attn_Wv"].T)
    d["rbt"] = np.ascontiguousarray(A["L0_attn_rel_bias_table"])
    d["l0_lng"] = A["L0_attn_ln_gamma"].reshape(1, D)
    d["l0_lnb"] = A["L0_attn_ln_beta"].reshape(1, D)
    d["fb_t"] = np.ascontiguousarray(A["L0_ffn_fast_b"].T)
    d["wfc"] = np.ascontiguousarray(A["L0_ffn_w_fc"])
    d["wproj"] = np.ascontiguousarray(A["L0_ffn_w_proj"])
    d["u_t"] = np.ascontiguousarray(A["L0_ffn_u"].T)
    d["v_t"] = np.ascontiguousarray(A["L0_ffn_v"].T)
    d["l0_gate"] = A["L0_ffn_gate"].reshape(1, 1)
    d["l1_win"] = np.ascontiguousarray(A["L1_attn_W_in"])
    d["l1_bin"] = np.ascontiguousarray(A["L1_attn_b_in"])
    d["l1_plog"] = np.ascontiguousarray(A["L1_attn_params_log"])
    d["l1_wmid"] = np.ascontiguousarray(A["L1_attn_W_mid"])
    d["l1_bmid"] = np.ascontiguousarray(A["L1_attn_b_mid"])
    d["l1_w1"] = np.ascontiguousarray(A["L1_attn_w1_w"])
    d["l1_w1b"] = A["L1_attn_w1_b"].reshape(1, D)
    d["l1_w2"] = np.ascontiguousarray(A["L1_attn_w2_w"])
    d["l1_w2b"] = A["L1_attn_w2_b"].reshape(1, D)
    d["l1_lng"] = A["L1_attn_ln_g"].reshape(1, D)
    d["l1_lnb"] = A["L1_attn_ln_b"].reshape(1, D)
    d["m_wfcs"] = np.ascontiguousarray(A["L1_ffn_Wfc_s"])
    d["m_bfcs"] = A["L1_ffn_bfc_s"].reshape(1, 4096)
    d["m_wprojs"] = np.ascontiguousarray(A["L1_ffn_Wproj_s"])
    d["m_bprojs"] = A["L1_ffn_bproj_s"].reshape(1, D)
    d["m_wr"] = np.ascontiguousarray(A["L1_ffn_Wr"])
    d["m_wfc"] = np.ascontiguousarray(A["L1_ffn_Wfc"])
    d["m_bfc"] = np.ascontiguousarray(A["L1_ffn_bfc"])
    d["m_wproj"] = np.ascontiguousarray(A["L1_ffn_Wproj"])
    d["m_bproj"] = np.ascontiguousarray(A["L1_ffn_bproj"])
    return d


PARK_SPECS = {
    "x_fm": ([NG, 128, T], F32),
    "xn0": ([NG, 128, T], BF16),
    "s0": ([NG, 128, T], BF16),
    "s0stats": ([2, T], F32),
    "x1_fm": ([NG, 128, T], F32),
    "x1stats": ([2, T], F32),
    "xn1": ([NG, 128, T], BF16),
    "s1": ([NG, 128, T], BF16),
}

INPUT_SPECS = {
    "x": ([T, D], F32),
    "wq_t": ([D, D], F32), "wk_t": ([D, D], F32), "wv_t": ([D, D], F32),
    "rbt": ([4095, 16], F32),
    "l0_lng": ([1, D], F32), "l0_lnb": ([1, D], F32),
    "fb_t": ([16, 16], F32), "wfc": ([4096, D], F32), "wproj": ([D, 4096], F32),
    "u_t": ([16, 4096], F32), "v_t": ([D, 16], F32), "l0_gate": ([1, 1], F32),
    "l1_win": ([3, D, D], F32), "l1_bin": ([3, D], F32), "l1_plog": ([3, D], F32),
    "l1_wmid": ([2, D, D], F32), "l1_bmid": ([2, D], F32),
    "l1_w1": ([D, D], F32), "l1_w1b": ([1, D], F32),
    "l1_w2": ([D, D], F32), "l1_w2b": ([1, D], F32),
    "l1_lng": ([1, D], F32), "l1_lnb": ([1, D], F32),
    "m_wfcs": ([D, 4096], F32), "m_bfcs": ([1, 4096], F32),
    "m_wprojs": ([4096, D], F32), "m_bprojs": ([1, D], F32),
    "m_wr": ([D, 8], F32),
    "m_wfc": ([8, D, 4096], F32), "m_bfc": ([8, 4096], F32),
    "m_wproj": ([8, 4096, D], F32), "m_bproj": ([8, D], F32),
}


def declare(nc, names, kinds):
    out = {}
    specs = dict(INPUT_SPECS)
    specs.update(PARK_SPECS)
    specs["out"] = ([T, D], F32)
    for nm in names:
        shape, dt = specs[nm]
        kind = {"in": "ExternalInput", "out": "ExternalOutput",
                "internal": "Internal"}[kinds[nm]]
        out[nm] = nc.dram_tensor(nm, shape, dt, kind=kind).ap()
    return out


STAGE_IO = [
    (stage_A, ["x"], ["x_fm", "xn0"]),
    (stage_B, ["xn0", "wq_t", "wk_t", "wv_t", "rbt"], ["s0"]),
    (stage_C, ["xn0", "s0", "x_fm", "wfc", "wproj", "u_t", "v_t", "fb_t",
               "l0_gate", "l0_lng", "l0_lnb"], ["x1_fm", "x1stats", "xn1"]),
    (stage_E, ["xn1", "l1_win", "l1_bin", "l1_plog", "l1_wmid", "l1_bmid",
               "l1_w1", "l1_w1b", "l1_w2", "l1_w2b", "l1_lng", "l1_lnb"],
     ["s1"]),
    (stage_F, ["xn1", "x1_fm", "s1", "m_wfcs", "m_bfcs", "m_wprojs",
               "m_bprojs", "m_wr", "m_wfc", "m_bfc", "m_wproj", "m_bproj"],
     ["out"]),
]


FUSED = True


def kernel(**inputs):
    per_core = [core_input_arrays(inputs, c) for c in range(NCORES)]
    if not FUSED:
        acc = [dict() for _ in range(NCORES)]
        for fn, ins, outs in STAGE_IO:
            nc = bass.Bass("TRN2", target_bir_lowering=False)
            kb = KB(nc)
            kinds = {n: "in" for n in ins}
            for n in outs:
                kinds[n] = "out"
            Tn = declare(nc, ins + outs, kinds)
            fn(kb, Tn)
            kb.emit()
            kb.close()
            maps = [{n: (acc[c][n] if n in acc[c] else per_core[c][n])
                     for n in ins} for c in range(NCORES)]
            res = run_bass_kernel_spmd(nc, maps, core_ids=list(range(NCORES)))
            for c in range(NCORES):
                for n in outs:
                    acc[c][n] = np.asarray(res.results[c][n])
        return np.stack([acc[c]["out"] for c in range(NCORES)])

    in_names = list(INPUT_SPECS)
    park_names = list(PARK_SPECS)
    nc = bass.Bass("TRN2", target_bir_lowering=False)
    kb = KB(nc)
    kinds = {n: "in" for n in in_names}
    kinds.update({n: "internal" for n in park_names})
    kinds["out"] = "out"
    Tn = declare(nc, in_names + park_names + ["out"], kinds)
    for fn, ins, outs in STAGE_IO:
        fn(kb, Tn)
    kb.emit()
    kb.close()
    maps = [{n: per_core[c][n] for n in in_names} for c in range(NCORES)]
    res = run_bass_kernel_spmd(nc, maps, core_ids=list(range(NCORES)))
    return np.stack([np.asarray(res.results[c]["out"])
                     for c in range(NCORES)])
